# version 11
# baseline (speedup 1.0000x reference)
import math
import numpy as np
import ml_dtypes
from contextlib import ExitStack

import concourse.bass as bass
import concourse.mybir as mybir
from concourse.bass_utils import run_bass_kernel_spmd

F32 = mybir.dt.float32
BF16 = mybir.dt.bfloat16
FP8 = mybir.dt.float8e4
AF = mybir.ActivationFunctionType
ALU = mybir.AluOpType

NCORES = 8
NB = 2
S = 2048
D = 2048
NH = 16
NP = 8
TT = 16
FT = 16

C32 = float(np.float32(1.5 * 2.0 ** 23))
C52 = float(np.float32(1.5 * 2.0 ** 52))
LN2INV2 = float(np.float32(1.0 / (2.0 * math.log(2.0))))
SM_SCALE = float(np.float32(128.0 ** -0.5))


def _yarn_tables():
    ROPE_BASE, FACTOR = 10000.0, 40.0
    BETA_FAST, BETA_SLOW, ORIG_MAX, MSCALE = 32.0, 1.0, 4096, 0.707
    dim = 64
    exponents = np.arange(0, dim, 2, dtype=np.float64) / dim
    freq_extra = 1.0 / ROPE_BASE ** exponents
    freq_inter = 1.0 / (FACTOR * ROPE_BASE ** exponents)

    def corr_dim(num_rot):
        return dim * math.log(ORIG_MAX / (num_rot * 2 * math.pi)) / (2 * math.log(ROPE_BASE))

    low = max(math.floor(corr_dim(BETA_FAST)), 0)
    high = min(math.ceil(corr_dim(BETA_SLOW)), dim - 1)
    if low == high:
        high = high + 0.001
    ramp = np.clip((np.arange(dim // 2, dtype=np.float64) - low) / (high - low), 0.0, 1.0)
    inv_freq = freq_inter * ramp + freq_extra * (1.0 - ramp)
    t = np.arange(S, dtype=np.float64)
    freqs = np.outer(t, inv_freq)

    def yarn_get_mscale(scale, mscale=1.0):
        if scale <= 1:
            return 1.0
        return 0.1 * mscale * math.log(scale) + 1.0

    msc = yarn_get_mscale(FACTOR, MSCALE) / yarn_get_mscale(FACTOR, 0.0)
    cosF = (np.cos(freqs) * msc).astype(np.float32).T
    sinF = (np.sin(freqs) * msc).astype(np.float32).T
    return (np.ascontiguousarray(np.tile(cosF, (4, 1))),
            np.ascontiguousarray(np.tile(sinF, (4, 1))))


class Ch:
    def __init__(self, stack, nc, name):
        self.sem = stack.enter_context(nc.semaphore(name))
        self.n = 0


PHASES = 99
E_STEPS = 99
E_SUB = 99
E_GEN = 99
DBG = False


def build_nc():
    nc = bass.Bass("TRN2", target_bir_lowering=False, detect_race_conditions=False)

    d_x = nc.dram_tensor("x", [NB, S, D], F32, kind="ExternalInput").ap()
    d_wqa = nc.dram_tensor("wqa", [D, 64], F32, kind="ExternalInput").ap()
    d_wqbN = nc.dram_tensor("wqbN", [64, NH * 64], F32, kind="ExternalInput").ap()
    d_wqbX = nc.dram_tensor("wqbX", [64, NH * 64], F32, kind="ExternalInput").ap()
    d_wqbR = nc.dram_tensor("wqbR", [64, NH * 64], F32, kind="ExternalInput").ap()
    d_wkva = nc.dram_tensor("wkva", [D, 128], F32, kind="ExternalInput").ap()
    d_wkbN = nc.dram_tensor("wkbN", [64, NH * 64], F32, kind="ExternalInput").ap()
    d_wkbV = nc.dram_tensor("wkbV", [64, NH * 64], F32, kind="ExternalInput").ap()
    d_qnw = nc.dram_tensor("qnw", [64, 1], F32, kind="ExternalInput").ap()
    d_knw = nc.dram_tensor("knw", [64, 1], F32, kind="ExternalInput").ap()
    d_wo = nc.dram_tensor("wo", [NH * 64, D], F32, kind="ExternalInput").ap()
    d_w8T = nc.dram_tensor("w8T", [D, D], FP8, kind="ExternalInput").ap()
    d_sw = nc.dram_tensor("sw", [128, 1], F32, kind="ExternalInput").ap()
    d_sin = nc.dram_tensor("sin_", [128, 1], BF16, kind="ExternalInput").ap()
    d_cos4 = nc.dram_tensor("cos4", [128, S], F32, kind="ExternalInput").ap()
    d_sin4 = nc.dram_tensor("sin4", [128, S], F32, kind="ExternalInput").ap()
    d_out = nc.dram_tensor("out", [NB, S, D], F32, kind="ExternalOutput").ap()
    scrk = "ExternalOutput" if DBG else "Internal"
    d_xqdbg = nc.dram_tensor("xqdbg", [8, 128, D], BF16, kind="ExternalOutput").ap() if DBG else None
    d_cppdbg = nc.dram_tensor("cppdbg", [128, 16], F32, kind="ExternalOutput").ap() if DBG else None
    d_xq8dbg = nc.dram_tensor("xq8dbg", [128, FT * 1024], FP8, kind="ExternalOutput").ap() if DBG else None
    d_zscr = nc.dram_tensor("zscr", [NB, S, D], F32, kind=scrk).ap()
    d_qscr = nc.dram_tensor("qscr", [NH, 128, S], BF16, kind=scrk).ap()
    d_kscr = nc.dram_tensor("kscr", [NH, 128, S], BF16, kind=scrk).ap()

    with ExitStack() as ctx:
        def sbt(name, shape, dt=F32):
            return ctx.enter_context(nc.sbuf_tensor(name, shape, dt))

        w8 = sbt("w8", [128, FT * D], FP8)
        cos4 = sbt("cos4_b", [128, S], BF16)
        sin4 = sbt("sin4_b", [128, S], BF16)
        wqa_b = sbt("wqa_b", [128, FT * 64], BF16)
        wkva_b = sbt("wkva_b", [128, FT * 128], BF16)
        wqbN_b = sbt("wqbN_b", [64, NH * 64], BF16)
        wqbX_b = sbt("wqbX_b", [64, NH * 64], BF16)
        wqbR_b = sbt("wqbR_b", [64, NH * 64], BF16)
        wkbN_b = sbt("wkbN_b", [64, NH * 64], BF16)
        wkbV_b = sbt("wkbV_b", [64, NH * 64], BF16)
        ones_b = sbt("ones_b", [64, 128], BF16)
        cpp = sbt("cpp", [128, 16], F32)
        qnw_s = sbt("qnw_s", [64, 1], F32)
        knw_s = sbt("knw_s", [64, 1], F32)
        C_TAU, C_LN2, C_NLN2, C_N2LN2, C_RS, C_DEQ, C_E6, C_E5, C_C40, C_TWO = range(10)

        bigA = sbt("bigA", [128, 16384], BF16)
        stats = sbt("stats", [128, 8], F32)

        PS = ctx.enter_context(nc.psum_tensor("PS", [128, 4096], F32))

        chnames = ["cLD", "cSET", "cXa", "cXb", "cTRa", "cTRb", "cTM", "cSC", "cVE", "cQKD",
                   "cQKLa", "cQKLb", "cQKF", "cL", "cE2", "cAV", "cEP", "cWM", "cX2a", "cX2b",
                   "cZD", "cZL", "cQT", "cQ8", "cFM", "cO", "cWV", "cQA", "cQB", "cWL", "cPSW", "cPSF"]
        C = {n: Ch(ctx, nc, n) for n in chnames}
        (cLD, cSET, cXa, cXb, cTRa, cTRb, cTM, cSC, cVE, cQKD, cQKLa, cQKLb, cQKF,
         cL, cE2, cAV, cEP, cWM, cX2a, cX2b, cZD, cZL, cQT, cQ8, cFM, cO, cWV,
         cQA, cQB, cWL, cPSW, cPSF) = (C[n] for n in chnames)

        block = ctx.enter_context(nc.Block())

        @block.scalar
        def _(s):
            from concourse.hw_specs import get_activation_tables
            for idx, funcs in enumerate(get_activation_tables(nc.m.arch).values()):
                if AF.Ln in funcs and AF.Exp in funcs:
                    s.add_instruction(mybir.InstLoadActFuncSet(
                        name=nc.get_next_instruction_name(),
                        act_func_set_id=idx, ins=[], outs=[]))
                    break

        @block.sync
        def _(sync):
            sync.dma_start(out=w8[:].rearrange("p (t c) -> p t c", c=D),
                           in_=d_w8T.rearrange("(t p) c -> p t c", p=128)).then_inc(cLD.sem, 16)
        cLD.n += 16

        with nc.sbuf_tensor("stage32", [128, S], F32) as stage32, \
             nc.sbuf_tensor("sin_b", [128, 1], BF16) as sin_b:

            stage_jobs = [
                (d_cos4, cos4, 128, S),
                (d_sin4, sin4, 128, S),
                (d_wqa.rearrange("(t p) c -> p t c", p=128), wqa_b, 128, FT * 64),
                (d_wkva.rearrange("(t p) c -> p t c", p=128), wkva_b, 128, FT * 128),
                (d_wqbN, wqbN_b, 64, NH * 64),
                (d_wqbX, wqbX_b, 64, NH * 64),
                (d_wkbN, wkbN_b, 64, NH * 64),
                (d_wkbV, wkbV_b, 64, NH * 64),
            ]
            for src, dst, p, width in stage_jobs:
                @block.sync
                def _(sync, src=src, p=p, width=width):
                    sync.wait_ge(cSET.sem, cSET.n)
                    o = stage32[0:p, 0:width]
                    if len(src.shape) == 3:
                        o = o.rearrange("p (t c) -> p t c", t=src.shape[1])
                    sync.dma_start(out=o, in_=src).then_inc(cLD.sem, 16)
                cLD.n += 16

                @block.vector
                def _(v, dst=dst, p=p, width=width):
                    v.wait_ge(cLD.sem, cLD.n)
                    v.tensor_copy(dst[0:p, 0:width], stage32[0:p, 0:width]).then_inc(cSET.sem, 1)
                cSET.n += 1

            @block.sync
            def _(sync):
                sync.wait_ge(cSET.sem, cSET.n)
                sync.dma_start(out=stage32[0:64, 0:NH * 64], in_=d_wqbR).then_inc(cLD.sem, 16)
            cLD.n += 16

            @block.vector
            def _(v):
                v.wait_ge(cLD.sem, cLD.n)
                si = stage32[0:64, 0:NH * 64].rearrange("p (h c) -> p h c", c=64)
                so = wqbR_b[0:64, :].rearrange("p (h c) -> p h c", c=64)
                v.tensor_scalar(so[:, :, 0:32], si[:, :, 0:32], -1.0, None, ALU.mult)
                v.tensor_copy(so[:, :, 32:64], si[:, :, 32:64]).then_inc(cSET.sem, 1)
            cSET.n += 1

            @block.sync
            def _(sync):
                sync.wait_ge(cSET.sem, cSET.n)
                sync.dma_start(out=cpp[:, C_RS:C_RS + 1], in_=d_sw).then_inc(cLD.sem, 16)
                sync.dma_start(out=sin_b[:], in_=d_sin).then_inc(cLD.sem, 16)
                sync.dma_start(out=qnw_s[:], in_=d_qnw).then_inc(cLD.sem, 16)
                sync.dma_start(out=knw_s[:], in_=d_knw).then_inc(cLD.sem, 16)
            cLD.n += 64

            @block.vector
            def _(v):
                v.memset(cpp[:, 10:16], 0.0)
                v.wait_ge(cLD.sem, cLD.n)
                v.memset(cpp[:, C_TAU:C_TAU + 1], 2.0 ** 52).then_inc(cSET.sem, 1)
                v.memset(cpp[:, C_TWO:C_TWO + 1], 2.0)
                v.memset(cpp[:, C_E6:C_E6 + 1], 1e-6)
                v.memset(cpp[:, C_E5:C_E5 + 1], 1e-5)
                v.memset(cpp[:, C_C40:C_C40 + 1], 2.0 ** -40)
                v.tensor_copy(cpp[:, C_DEQ:C_DEQ + 1], sin_b[:]).then_inc(cSET.sem, 1)
                v.wait_ge(cSET.sem, cSET.n + 2)
                v.tensor_scalar(cpp[:, C_TAU:C_TAU + 1], cpp[:, C_TAU:C_TAU + 1], 1.0, -(2.0 ** 52), ALU.add, ALU.add)
                v.tensor_tensor(cpp[:, C_DEQ:C_DEQ + 1], cpp[:, C_DEQ:C_DEQ + 1], cpp[:, C_RS:C_RS + 1], ALU.mult).then_inc(cSET.sem, 1)
                v.wait_ge(cSET.sem, cSET.n + 3)
                v.tensor_scalar(cpp[:, C_DEQ:C_DEQ + 1], cpp[:, C_DEQ:C_DEQ + 1], 4.0, None, ALU.mult)
                v.tensor_copy(cpp[:, C_RS:C_RS + 1], sin_b[:]).then_inc(cSET.sem, 1)
                v.wait_ge(cSET.sem, cSET.n + 4)
                v.reciprocal(cpp[:, C_RS:C_RS + 1], cpp[:, C_RS:C_RS + 1])
                v.tensor_scalar(qnw_s[:], qnw_s[:], SM_SCALE, None, ALU.mult)
                v.memset(ones_b[:], 1.0).then_inc(cSET.sem, 1)
            cSET.n += 5

            @block.scalar
            def _(s):
                s.wait_ge(cSET.sem, cSET.n)
                s.activation(cpp[:, C_LN2:C_LN2 + 1], cpp[:, C_TWO:C_TWO + 1], AF.Ln)
                s.mul(cpp[:, C_NLN2:C_NLN2 + 1], cpp[:, C_LN2:C_LN2 + 1], -1.0)
                s.mul(cpp[:, C_N2LN2:C_N2LN2 + 1], cpp[:, C_LN2:C_LN2 + 1], -2.0).then_inc(cSET.sem, 1)
            cSET.n += 1

        SETUP_DONE = cSET.n

        for b in range(NB if PHASES >= 6 else (1 if PHASES >= 2 else 0)):
            with ExitStack() as ea:
                def eat(name, shape, dt):
                    return ea.enter_context(nc.sbuf_tensor(name + str(b), shape, dt))
                vbuf = eat("vbuf", [128, TT * 1536], BF16)
                qp = [eat("qp0", [128, S], BF16), eat("qp1", [128, S], BF16)]
                kp = [eat("kp0", [128, S], BF16), eat("kp1", [128, S], BF16)]
                a2 = [eat(f"a2_{i}", [128, 1024], BF16) for i in range(3)]
                rcp_full = eat("rcp", [64, S], F32)
                rcp = rcp_full[0:64, 0:1024]
                qlat_n = eat("qlat_n", [64, S], BF16)
                kv_n = eat("kv_n", [64, S], BF16)
                kpe_r = eat("kpe_r", [64, S], BF16)

                with ExitStack() as ee:
                    def eet(name, shape, dt):
                        return ee.enter_context(nc.sbuf_tensor(name + str(b), shape, dt))
                    xbf = eet("xbf", [128, 2 * D], BF16)
                    kv_bf = eet("kv_bf", [64, S], BF16)
                    ql_bf = eet("ql_bf", [64, S], BF16)
                    kpe_bf = eet("kpe_bf", [64, S], BF16)
                    krot_bf = eet("krot_bf", [64, S], BF16)
                    tln = rcp_full
                    t1 = eet("t1", [128, 512], F32)
                    t2 = eet("t2", [128, 512], F32)
                    sq_bf = krot_bf

                    for hf in range(2):
                        for j in range(8):
                            tt = hf * 8 + j

                            cXp = cXa if tt % 2 == 0 else cXb
                            cTRp = cTRa if tt % 2 == 0 else cTRb

                            @block.gpsimd
                            def _(g, b=b, tt=tt, j=j, hf=hf, cXp=cXp, cTRp=cTRp):
                                if b > 0 and hf == 0 and j == 0:
                                    g.wait_ge(cO.sem, cO.n)
                                    g.wait_ge(cWV.sem, cWV.n)
                                    g.wait_ge(cSC.sem, cSC.n)
                                    g.wait_ge(cVE.sem, cVE.n)
                                if tt >= 2:
                                    g.wait_ge(cTRp.sem, cTRp.n)
                                g.dma_start(out=xbf[:, (j % 2) * D:(j % 2 + 1) * D],
                                            in_=d_x[b, tt * 128:(tt + 1) * 128, :]).then_inc(cXp.sem, 16)
                            cXp.n += 16

                            def mk_tr(fts):
                                def sec(s, j=j, b=b, hf=hf, cXp=cXp, cTRp=cTRp, fts=fts):
                                    s.wait_ge(cXp.sem, cXp.n)
                                    if j == 0:
                                        s.wait_ge(cTM.sem, cTM.n)
                                        s.wait_ge(cWM.sem, cWM.n)
                                    for ft in fts:
                                        s.dma_start_transpose(
                                            bigA[:, ft * 1024 + j * 128: ft * 1024 + (j + 1) * 128],
                                            xbf[:, (j % 2) * D + ft * 128:(j % 2) * D + (ft + 1) * 128]
                                        ).then_inc(cTRp.sem, 16)
                                return sec
                            block.sync(mk_tr(range(0, FT)))
                            cTRp.n += 256

                        @block.tensor
                        def _(t, hf=hf):
                            if E_STEPS < 2:
                                return
                            t.wait_ge(cTRa.sem, cTRa.n)
                            t.wait_ge(cTRb.sem, cTRb.n)
                            if hf == 0:
                                t.wait_ge(cSET.sem, SETUP_DONE)
                                t.wait_ge(cSC.sem, cSC.n)
                                t.wait_ge(cWV.sem, cWV.n)
                            for n in range(2):
                                for ft in range(FT):
                                    t.matmul(PS[:, hf * 1024 + n * 512: hf * 1024 + (n + 1) * 512],
                                             wkva_b[:, ft * 128:(ft + 1) * 128],
                                             bigA[:, ft * 1024 + n * 512: ft * 1024 + (n + 1) * 512],
                                             start=(ft == 0), stop=(ft == FT - 1))
                            for n in range(2):
                                for ft in range(FT):
                                    ins = t.matmul(PS[0:64, 2048 + hf * 1024 + n * 512: 2048 + hf * 1024 + (n + 1) * 512],
                                                   wqa_b[:, ft * 64:(ft + 1) * 64],
                                                   bigA[:, ft * 1024 + n * 512: ft * 1024 + (n + 1) * 512],
                                                   start=(ft == 0), stop=(ft == FT - 1))
                            ins.then_inc(cTM.sem, 1)
                        if E_STEPS >= 2:
                            cTM.n += 1
                    if E_STEPS < 3:
                        continue

                    @block.scalar
                    def _(s):
                        s.wait_ge(cTM.sem, cTM.n)
                        s.copy(kv_bf[:], PS[0:64, 0:2048])
                        s.copy(kpe_bf[:], PS[64:128, 0:2048])
                        s.copy(ql_bf[:], PS[0:64, 2048:4096])
                        s.mul(krot_bf[0:32, :], kpe_bf[32:64, :], -1.0)
                        s.copy(krot_bf[32:64, :], kpe_bf[0:32, :]).then_inc(cSC.sem, 1)
                    cSC.n += 1
                    if E_STEPS < 4:
                        continue

                    @block.vector
                    def _(v):
                        v.wait_ge(cSC.sem, cSC.n)
                        if cSET.n == SETUP_DONE:
                            v.wait_ge(cSET.sem, SETUP_DONE)
                        for n in range(4):
                            sl = slice(n * 512, (n + 1) * 512)
                            v.tensor_tensor(t1[0:64, :], kpe_bf[:, sl], cos4[0:64, sl], ALU.mult)
                            v.tensor_tensor(t2[0:64, :], krot_bf[:, sl], sin4[0:64, sl], ALU.mult)
                            ins = v.tensor_tensor(kpe_r[:, sl], t1[0:64, :], t2[0:64, :], ALU.add)
                        ins.then_inc(cVE.sem, 1)
                    cVE.n += 1

                    if E_SUB < 2:
                        continue

                    @block.vector
                    def _(v):
                        v.tensor_tensor(sq_bf[:], ql_bf[:], ql_bf[:], ALU.mult).then_inc(cVE.sem, 1)
                    cVE.n += 1
                    if E_SUB < 3:
                        continue

                    @block.tensor
                    def _(t):
                        t.wait_ge(cVE.sem, cVE.n)
                        for n in range(4):
                            ins = t.matmul(PS[:, n * 512:(n + 1) * 512], ones_b[:],
                                           sq_bf[:, n * 512:(n + 1) * 512], start=True, stop=True)
                        ins.then_inc(cTM.sem, 1)
                    cTM.n += 1

                    if E_SUB >= 4:
                        @block.scalar
                        def _(s):
                            s.wait_ge(cTM.sem, cTM.n)
                            s.activation(tln[:], PS[0:64, 0:2048], AF.Ln, bias=cpp[0:64, C_E6:C_E6 + 1], scale=1.0 / 64.0)
                            s.activation(tln[:], tln[:], AF.Exp, scale=-0.5).then_inc(cSC.sem, 1)
                        cSC.n += 1
                    if E_SUB < 5:
                        continue

                    @block.vector
                    def _(v):
                        v.wait_ge(cSC.sem, cSC.n)
                        v.tensor_tensor(tln[:], ql_bf[:], tln[:], ALU.mult)
                        v.tensor_scalar(qlat_n[:], tln[:], qnw_s[:], None, ALU.mult)
                        v.tensor_tensor(sq_bf[:], kv_bf[:], kv_bf[:], ALU.mult).then_inc(cVE.sem, 1)
                    cVE.n += 1
                    if E_SUB < 6:
                        continue

                    @block.tensor
                    def _(t):
                        t.wait_ge(cVE.sem, cVE.n)
                        for n in range(4):
                            ins = t.matmul(PS[:, 2048 + n * 512: 2048 + (n + 1) * 512], ones_b[:],
                                           sq_bf[:, n * 512:(n + 1) * 512], start=True, stop=True)
                        ins.then_inc(cTM.sem, 1)
                    cTM.n += 1

                    @block.scalar
                    def _(s):
                        s.wait_ge(cTM.sem, cTM.n)
                        s.activation(tln[:], PS[0:64, 2048:4096], AF.Ln, bias=cpp[0:64, C_E6:C_E6 + 1], scale=1.0 / 64.0)
                        s.activation(tln[:], tln[:], AF.Exp, scale=-0.5).then_inc(cSC.sem, 1)
                    cSC.n += 1

                    @block.vector
                    def _(v):
                        v.wait_ge(cSC.sem, cSC.n)
                        v.tensor_tensor(tln[:], kv_bf[:], tln[:], ALU.mult)
                        v.tensor_scalar(kv_n[:], tln[:], knw_s[:], None, ALU.mult).then_inc(cVE.sem, 1)
                    cVE.n += 1
                    if E_STEPS < 5:
                        continue

                    for tt in range(TT):
                        @block.tensor
                        def _(t, tt=tt):
                            t.wait_ge(cVE.sem, cVE.n)
                            t.wait_ge(cSC.sem, cSC.n)
                            for n in range(2):
                                ins = t.matmul(PS[:, (tt % 2) * 1024 + n * 512:(tt % 2) * 1024 + (n + 1) * 512],
                                               kv_n[:, tt * 128:(tt + 1) * 128],
                                               wkbV_b[:, n * 512:(n + 1) * 512], start=True, stop=True)
                            ins.then_inc(cTM.sem, 1)
                        cTM.n += 1

                        @block.scalar
                        def _(s, tt=tt, b=b):
                            s.wait_ge(cTM.sem, cTM.n)
                            if tt == 0 and b > 0:
                                s.wait_ge(cAV.sem, cAV.n)
                            src = PS[:, (tt % 2) * 1024:(tt % 2) * 1024 + 1024].rearrange(
                                "p (pr j c) -> p pr j c", pr=8, j=2, c=64)
                            dst = vbuf[:, tt * 1536:(tt + 1) * 1536].rearrange(
                                "p (pr g c) -> p pr g c", pr=8, g=3, c=64)[:, :, 0:3:2, :]
                            s.copy(dst, src).then_inc(cSC.sem, 1)
                        cSC.n += 1

                    if E_STEPS < 6:
                        continue

                    @block.vector
                    def _(v, b=b):
                        if b > 0:
                            v.wait_ge(cAV.sem, cAV.n)
                        ones_ap = vbuf[:].rearrange("p (t pr g c) -> p t pr g c", pr=8, g=3, c=64)[:, :, :, 1:2, :]
                        v.memset(ones_ap, 1.0).then_inc(cVE.sem, 1)
                    cVE.n += 1

                    for p in range(NP if E_STEPS >= 7 else 0):
                        @block.tensor
                        def _(t, p=p):
                            t.wait_ge(cSC.sem, cSC.n)
                            t.wait_ge(cVE.sem, cVE.n)
                            for n in range(4):
                                t.matmul(PS[:, n * 512:(n + 1) * 512], wqbX_b[:, p * 128:(p + 1) * 128],
                                         qlat_n[:, n * 512:(n + 1) * 512], start=True, stop=True)
                            for n in range(4):
                                ins = t.matmul(PS[:, 2048 + n * 512: 2048 + (n + 1) * 512], wqbR_b[:, p * 128:(p + 1) * 128],
                                               qlat_n[:, n * 512:(n + 1) * 512], start=True, stop=True)
                            ins.then_inc(cTM.sem, 1)
                        cTM.n += 1

                        if E_GEN < 2:
                            continue

                        @block.vector
                        def _(v, p=p):
                            v.wait_ge(cTM.sem, cTM.n)
                            if p > 0:
                                v.wait_ge(cQKD.sem, cQKD.n)
                            for n in range(4):
                                sl = slice(n * 512, (n + 1) * 512)
                                v.tensor_tensor(t1[:], PS[:, sl], cos4[:, sl], ALU.mult)
                                ins = v.tensor_tensor(t2[:], PS[:, 2048 + n * 512: 2048 + (n + 1) * 512],
                                                      sin4[:, sl], ALU.mult)
                                if n == 3:
                                    ins.then_inc(cVE.sem, 1)
                                v.tensor_tensor(qp[0][0:64, sl], t1[0:64, :], t2[0:64, :], ALU.add)
                                v.tensor_tensor(qp[1][64:128, sl], t1[64:128, :], t2[64:128, :], ALU.add)
                            v.tensor_copy(kp[0][0:64, :], kpe_r[:])
                            v.tensor_copy(kp[1][64:128, :], kpe_r[:]).then_inc(cVE.sem, 1)
                        cVE.n += 2

                        if E_GEN < 3:
                            continue

                        @block.tensor
                        def _(t, p=p):
                            t.wait_ge(cVE.sem, cVE.n)
                            for n in range(4):
                                t.matmul(PS[:, n * 512:(n + 1) * 512], wqbN_b[:, p * 128:(p + 1) * 128],
                                         qlat_n[:, n * 512:(n + 1) * 512], start=True, stop=True)
                            for n in range(4):
                                ins = t.matmul(PS[:, 2048 + n * 512: 2048 + (n + 1) * 512], wkbN_b[:, p * 128:(p + 1) * 128],
                                               kv_n[:, n * 512:(n + 1) * 512], start=True, stop=True)
                            ins.then_inc(cTM.sem, 1)
                        cTM.n += 1

                        if E_GEN < 4:
                            continue

                        @block.scalar
                        def _(s, p=p):
                            s.wait_ge(cTM.sem, cTM.n)
                            if p > 0:
                                s.wait_ge(cQKD.sem, cQKD.n)
                            s.copy(qp[0][64:128, :], PS[0:64, 0:2048])
                            s.copy(qp[1][0:64, :], PS[64:128, 0:2048])
                            s.copy(kp[0][64:128, :], PS[0:64, 2048:4096])
                            s.copy(kp[1][0:64, :], PS[64:128, 2048:4096]).then_inc(cSC.sem, 1)
                        cSC.n += 1

                        if E_GEN < 5:
                            continue

                        @block.sync
                        def _(sync, p=p):
                            sync.wait_ge(cSC.sem, cSC.n)
                            sync.wait_ge(cVE.sem, cVE.n)
                            sync.dma_start(out=d_qscr[2 * p], in_=qp[0][:]).then_inc(cQKD.sem, 16)
                            sync.dma_start(out=d_qscr[2 * p + 1], in_=qp[1][:]).then_inc(cQKD.sem, 16)
                            sync.dma_start(out=d_kscr[2 * p], in_=kp[0][:]).then_inc(cQKD.sem, 16)
                            sync.dma_start(out=d_kscr[2 * p + 1], in_=kp[1][:]).then_inc(cQKD.sem, 16)
                        cQKD.n += 64

                EXP_BASE, AV_BASE, EP_BASE, L_BASE = cE2.n, cAV.n, cEP.n, cL.n

                def emit_av(h, kt, g, hq):
                    @block.tensor
                    def _(t, h=h, kt=kt, g=g, hq=hq):
                        t.wait_ge(cE2.sem, EXP_BASE + g + 1)
                        if kt == 0 and hq >= 2:
                            t.wait_ge(cEP.sem, EP_BASE + hq - 1)
                        voff = kt * 1536 + (h // 2) * 192 + (h % 2) * 64
                        for n in range(2):
                            ins = t.matmul(PS[:, 2048 + (hq % 2) * 1024 + n * 512:
                                               2048 + (hq % 2) * 1024 + (n + 1) * 512],
                                           vbuf[:, voff:voff + 128],
                                           a2[g % 3][:, n * 512:(n + 1) * 512],
                                           start=(kt == 0), stop=(kt == 15))
                        ins.then_inc(cAV.sem, 1)
                    cAV.n += 1
                    if kt == 15:
                        @block.vector
                        def _(v, h=h, qc=hq % 2, hq=hq):
                            v.wait_ge(cAV.sem, AV_BASE + (hq + 1) * 16)
                            ob_ = 2048 + (hq % 2) * 1024
                            dst_c = (h // 2) * 2048 + qc * 1024
                            if h % 2 == 0:
                                v.reciprocal(rcp, PS[64:128, ob_:ob_ + 1024])
                                v.tensor_tensor(bigA[0:64, dst_c:dst_c + 1024],
                                                PS[0:64, ob_:ob_ + 1024], rcp, ALU.mult).then_inc(cEP.sem, 1)
                            else:
                                v.reciprocal(rcp, PS[0:64, ob_:ob_ + 1024])
                                v.tensor_tensor(bigA[64:128, dst_c:dst_c + 1024],
                                                PS[64:128, ob_:ob_ + 1024], rcp, ALU.mult).then_inc(cEP.sem, 1)
                        cEP.n += 1

                pend_av = None
                for h in range(NH if PHASES >= 3 else 0):
                    cQKLp = cQKLa if h % 2 == 0 else cQKLb

                    @block.sync
                    def _(sync, h=h, cQKLp=cQKLp):
                        if h < 2:
                            sync.wait_ge(cQKD.sem, cQKD.n)
                        else:
                            sync.wait_ge(cL.sem, L_BASE + (h - 1) * 32)
                        sync.dma_start(out=qp[h % 2][:], in_=d_qscr[h]).then_inc(cQKLp.sem, 16)
                        sync.dma_start(out=kp[h % 2][:], in_=d_kscr[h]).then_inc(cQKLp.sem, 16)
                    cQKLp.n += 32
                    QKL_TGT = cQKLp.n

                    for qc in range(2):
                        hq = h * 2 + qc
                        for kt in range(16):
                            g = hq * 16 + kt

                            @block.tensor
                            def _(t, h=h, qc=qc, kt=kt, g=g, cQKLp=cQKLp, QKL_TGT=QKL_TGT):
                                if kt == 0 and qc == 0:
                                    t.wait_ge(cQKLp.sem, QKL_TGT)
                                if g >= 2:
                                    t.wait_ge(cE2.sem, EXP_BASE + g - 1)
                                for n in range(2):
                                    ins = t.matmul(PS[:, (g % 2) * 1024 + n * 512:(g % 2) * 1024 + (n + 1) * 512],
                                                   kp[h % 2][:, kt * 128:(kt + 1) * 128],
                                                   qp[h % 2][:, qc * 1024 + n * 512: qc * 1024 + (n + 1) * 512],
                                                   start=True, stop=True)
                                ins.then_inc(cL.sem, 1)
                            cL.n += 1

                            @block.scalar
                            def _(s, g=g):
                                s.wait_ge(cL.sem, L_BASE + g + 1)
                                if g >= 3:
                                    s.wait_ge(cAV.sem, AV_BASE + g - 2)
                                s.activation(a2[g % 3][:], PS[:, (g % 2) * 1024:(g % 2) * 1024 + 1024],
                                             AF.Exp).then_inc(cE2.sem, 1)
                            cE2.n += 1

                            if pend_av is not None:
                                emit_av(*pend_av)
                            pend_av = (h, kt, g, hq)

                if pend_av is not None:
                    emit_av(*pend_av)
                    pend_av = None

                ATT_DONE_EP = cEP.n
                if PHASES < 4:
                    continue

            with ExitStack() as wf:
                def wft(name, shape, dt):
                    return wf.enter_context(nc.sbuf_tensor(name + str(b), shape, dt))
                wo_bf = wft("wo_bf", [128, 8 * D], BF16)
                x2 = wft("x2", [128, 2 * D], F32)
                vv = wft("vv", [128, D], F32)
                zz = wft("zz", [128, D], F32)
                ob = wft("ob", [128, D], F32)
                xqb = wft("xqb", [128, D], BF16)
                xqs = wft("xqs", [128, D], BF16)
                xqT8 = wft("xqT8", [128, FT * 1024], FP8)
                qd = wft("qd", [128, 7 * 512], F32)
                QU, QW, QE, QA32, QA52, QQ, QQI = [slice(i * 512, (i + 1) * 512) for i in range(7)]

                for p8 in range(8):
                    @block.sync
                    def _(sync, p8=p8):
                        if p8 == 0:
                            sync.wait_ge(cVE.sem, cVE.n)
                            sync.wait_ge(cSC.sem, cSC.n)
                            sync.wait_ge(cQKD.sem, cQKD.n)
                            sync.wait_ge(cQKLa.sem, cQKLa.n)
                            sync.wait_ge(cQKLb.sem, cQKLb.n)
                            sync.wait_ge(cO.sem, cO.n)
                            sync.wait_ge(cEP.sem, cEP.n)
                        sync.wait_ge(cWV.sem, cWV.n)
                        sync.dma_start(out=x2[:, 0:D], in_=d_wo[p8 * 128:(p8 + 1) * 128, :]).then_inc(cWL.sem, 16)
                    cWL.n += 16

                    @block.vector
                    def _(v, p8=p8):
                        v.wait_ge(cWL.sem, cWL.n)
                        v.tensor_copy(wo_bf[:, p8 * D:(p8 + 1) * D], x2[:, 0:D]).then_inc(cWV.sem, 1)
                    cWV.n += 1
                WO_READY = cWV.n

                for hf in range(2):
                    for j in range(8):
                        tt = hf * 8 + j

                        cX2p = cX2a if tt % 2 == 0 else cX2b

                        @block.sync
                        def _(sync, b=b, tt=tt, cX2p=cX2p):
                            sync.wait_ge(cWV.sem, cWV.n)
                            sync.dma_start(out=x2[:, (tt % 2) * D:(tt % 2 + 1) * D],
                                           in_=d_x[b, tt * 128:(tt + 1) * 128, :]).then_inc(cX2p.sem, 16)
                        cX2p.n += 16

                        @block.tensor
                        def _(t, tt=tt, j=j, hf=hf, PSW_T=cPSW.n, PSF_T=cPSF.n):
                            if j == 0 and hf == 0:
                                t.wait_ge(cEP.sem, ATT_DONE_EP)
                                t.wait_ge(cAV.sem, cAV.n)
                                t.wait_ge(cWV.sem, WO_READY)
                            if hf > 0 and j < 2:
                                t.wait_ge(cPSF.sem, PSF_T)
                            if j >= 2:
                                t.wait_ge(cPSW.sem, PSW_T - 1)
                            for p8 in range(8):
                                for n in range(4):
                                    ins = t.matmul(PS[:, (tt % 2) * 2048 + n * 512:(tt % 2) * 2048 + (n + 1) * 512],
                                                   bigA[:, p8 * 2048 + tt * 128: p8 * 2048 + (tt + 1) * 128],
                                                   wo_bf[:, p8 * D + n * 512: p8 * D + (n + 1) * 512],
                                                   start=(p8 == 0), stop=(p8 == 7))
                            ins.then_inc(cWM.sem, 1)
                        cWM.n += 1

                        @block.vector
                        def _(v, tt=tt, cX2p=cX2p):
                            v.wait_ge(cWM.sem, cWM.n)
                            v.wait_ge(cX2p.sem, cX2p.n)
                            v.wait_ge(cO.sem, cO.n)
                            pb = (tt % 2) * 2048
                            v.scalar_tensor_tensor(vv[:], PS[:, pb:pb + 2048], 1.0,
                                                   x2[:, (tt % 2) * D:(tt % 2 + 1) * D],
                                                   ALU.mult, ALU.add, accum_out=stats[:, 0:1]).then_inc(cPSW.sem, 1)
                            v.scalar_tensor_tensor(ob[:], vv[:], 1.0, vv[:], ALU.mult, ALU.mult,
                                                   accum_out=stats[:, 1:2])
                            v.tensor_scalar(stats[:, 2:3], stats[:, 0:1], 1.0 / 2048.0, None, ALU.mult)
                            v.tensor_scalar(stats[:, 3:4], stats[:, 1:2], 1.0 / 2048.0, None, ALU.mult).then_inc(cWV.sem, 1)
                            v.wait_ge(cWV.sem, cWV.n + 1)
                            v.tensor_tensor(stats[:, 4:5], stats[:, 2:3], stats[:, 2:3], ALU.mult).then_inc(cWV.sem, 1)
                            v.wait_ge(cWV.sem, cWV.n + 2)
                            v.tensor_tensor(stats[:, 3:4], stats[:, 3:4], stats[:, 4:5], ALU.subtract).then_inc(cWV.sem, 1)
                        cWV.n += 3
                        cPSW.n += 1

                        @block.scalar
                        def _(s):
                            s.wait_ge(cWV.sem, cWV.n)
                            s.activation(stats[:, 5:6], stats[:, 3:4], AF.Ln, bias=cpp[:, C_E5:C_E5 + 1]).then_inc(cSC.sem, 1)
                            s.wait_ge(cSC.sem, cSC.n + 1)
                            s.activation(stats[:, 5:6], stats[:, 5:6], AF.Exp, scale=-0.5).then_inc(cSC.sem, 1)
                        cSC.n += 2

                        @block.vector
                        def _(v):
                            v.wait_ge(cSC.sem, cSC.n)
                            v.wait_ge(cZD.sem, cZD.n)
                            v.tensor_scalar(zz[:], vv[:], stats[:, 2:3], stats[:, 5:6],
                                            ALU.subtract, ALU.mult).then_inc(cWV.sem, 1)
                        cWV.n += 1

                        @block.sync
                        def _(sync, b=b, tt=tt):
                            sync.wait_ge(cWV.sem, cWV.n)
                            sync.dma_start(out=d_zscr[b, tt * 128:(tt + 1) * 128, :], in_=zz[:]).then_inc(cZD.sem, 16)
                        cZD.n += 16

                        for qq in range(4):
                            c0, c1 = qq * 512, (qq + 1) * 512

                            @block.vector
                            def _(v, c0=c0, c1=c1):
                                v.tensor_scalar(qd[:, QU], zz[:, c0:c1], cpp[:, C_RS:C_RS + 1], 448.0,
                                                ALU.mult, ALU.min)
                                v.tensor_single_scalar(qd[:, QU], qd[:, QU], -448.0, ALU.max)
                                v.tensor_tensor(qd[:, QW], qd[:, QU], qd[:, QU], ALU.mult).then_inc(cQA.sem, 1)
                            cQA.n += 1

                            @block.scalar
                            def _(s):
                                s.wait_ge(cQA.sem, cQA.n)
                                s.activation(qd[:, QW], qd[:, QW], AF.Ln, bias=cpp[:, C_C40:C_C40 + 1]).then_inc(cQB.sem, 1)
                            cQB.n += 1

                            @block.vector
                            def _(v):
                                v.wait_ge(cQB.sem, cQB.n)
                                v.tensor_scalar(qd[:, QE], qd[:, QW], LN2INV2, -3.5, ALU.mult, ALU.add)
                                v.tensor_scalar(qd[:, QA32], qd[:, QE], C32, -C32, ALU.add, ALU.add)
                                v.tensor_scalar(qd[:, QA52], qd[:, QE], C52, -C52, ALU.add, ALU.add)
                                v.tensor_tensor(qd[:, QA52], qd[:, QA52], qd[:, QA32], ALU.subtract)
                                v.scalar_tensor_tensor(qd[:, QE], qd[:, QA52], cpp[:, C_TAU:C_TAU + 1],
                                                       qd[:, QA32], ALU.mult, ALU.add)
                                v.tensor_single_scalar(qd[:, QE], qd[:, QE], -9.0, ALU.max).then_inc(cQA.sem, 1)
                            cQA.n += 1

                            @block.scalar
                            def _(s):
                                s.wait_ge(cQA.sem, cQA.n)
                                s.activation(qd[:, QQ], qd[:, QE], AF.Exp,
                                             bias=cpp[:, C_N2LN2:C_N2LN2 + 1], scale=cpp[:, C_LN2:C_LN2 + 1])
                                s.activation(qd[:, QQI], qd[:, QE], AF.Exp,
                                             scale=cpp[:, C_NLN2:C_NLN2 + 1]).then_inc(cQB.sem, 1)
                            cQB.n += 1

                            @block.vector
                            def _(v, c0=c0, c1=c1):
                                v.wait_ge(cQB.sem, cQB.n)
                                if c0 == 0:
                                    v.wait_ge(cQT.sem, cQT.n)
                                v.tensor_tensor(qd[:, QQI], qd[:, QU], qd[:, QQI], ALU.mult)
                                v.tensor_scalar(qd[:, QA32], qd[:, QQI], C32, -C32, ALU.add, ALU.add)
                                v.tensor_scalar(qd[:, QA52], qd[:, QQI], C52, -C52, ALU.add, ALU.add)
                                v.tensor_tensor(qd[:, QA52], qd[:, QA52], qd[:, QA32], ALU.subtract)
                                v.scalar_tensor_tensor(qd[:, QE], qd[:, QA52], cpp[:, C_TAU:C_TAU + 1],
                                                       qd[:, QA32], ALU.mult, ALU.add)
                                v.tensor_tensor(xqb[:, c0:c1], qd[:, QE], qd[:, QQ], ALU.mult).then_inc(cWV.sem, 1)
                            cWV.n += 1

                        def mk_qtr(fts, j=j, hf=hf, b=b, cWVt=None, cQ8t=None):
                            cWVt, cQ8t = cWV.n, cQ8.n
                            def sec(s, j=j, hf=hf, b=b, fts=fts, cWVt=cWVt, cQ8t=cQ8t):
                                s.wait_ge(cWV.sem, cWVt)
                                s.wait_ge(cQ8.sem, cQ8t)
                                if j == 0 and (hf == 1 or b > 0):
                                    s.wait_ge(cFM.sem, cFM.n)
                                for ft in fts:
                                    s.dma_start_transpose(xqs[:, ft * 128:(ft + 1) * 128],
                                                          xqb[:, ft * 128:(ft + 1) * 128]).then_inc(cQT.sem, 16)
                            return sec
                        block.sync(mk_qtr(range(0, FT)))
                        cQT.n += 256

                        @block.vector
                        def _(v, j=j):
                            v.wait_ge(cQT.sem, cQT.n)
                            src = xqs[:].rearrange("p (f c) -> p f c", c=128)
                            dst = xqT8[:].rearrange("p (f c) -> p f c", c=1024)[:, :, j * 128:(j + 1) * 128]
                            v.tensor_copy(dst, src).then_inc(cQ8.sem, 1)
                        cQ8.n += 1

                        if DBG and b == 0 and hf == 0:
                            @block.sync
                            def _(sync, j=j):
                                sync.wait_ge(cWV.sem, cWV.n)
                                sync.dma_start(out=d_xqdbg[j], in_=xqb[:]).then_inc(cO.sem, 16)
                            cO.n += 16

                    if DBG and b == 0 and hf == 0:
                        @block.sync
                        def _(sync):
                            sync.wait_ge(cQ8.sem, cQ8.n)
                            sync.dma_start(out=d_xq8dbg, in_=xqT8[:]).then_inc(cO.sem, 16)
                            sync.dma_start(out=d_cppdbg, in_=cpp[:]).then_inc(cO.sem, 16)
                        cO.n += 32

                    Q8_HALF = cQ8.n
                    for j in range(8 if PHASES >= 5 else 0):
                        tt = hf * 8 + j

                        @block.sync
                        def _(sync, b=b, tt=tt):
                            sync.wait_ge(cWV.sem, cWV.n)
                            sync.dma_start(out=x2[:, 0:D],
                                           in_=d_zscr[b, tt * 128:(tt + 1) * 128, :]).then_inc(cZL.sem, 16)
                        cZL.n += 16

                        @block.tensor
                        def _(t, tt=tt, j=j, PSW_T=cPSW.n, PSF_T=cPSF.n):
                            t.wait_ge(cQ8.sem, Q8_HALF)
                            if j == 0:
                                t.wait_ge(cPSW.sem, PSW_T)
                            if j >= 2:
                                t.wait_ge(cPSF.sem, PSF_T - 1)
                            for ft in range(FT):
                                for n in range(4):
                                    ins = t.matmul(PS[:, (tt % 2) * 2048 + n * 512:(tt % 2) * 2048 + (n + 1) * 512],
                                                   xqT8[:, ft * 1024 + j * 128: ft * 1024 + j * 128 + 128],
                                                   w8[:, ft * D + n * 512: ft * D + (n + 1) * 512],
                                                   start=(ft == 0), stop=(ft == FT - 1))
                            ins.then_inc(cFM.sem, 1)
                        cFM.n += 1

                        @block.vector
                        def _(v, tt=tt):
                            v.wait_ge(cFM.sem, cFM.n)
                            v.wait_ge(cZL.sem, cZL.n)
                            v.wait_ge(cO.sem, cO.n)
                            v.wait_ge(cZD.sem, cZD.n)
                            pb = (tt % 2) * 2048
                            v.scalar_tensor_tensor(vv[:], PS[:, pb:pb + 2048], cpp[:, C_DEQ:C_DEQ + 1],
                                                   x2[:, 0:D], ALU.mult, ALU.add, accum_out=stats[:, 0:1]).then_inc(cPSF.sem, 1)
                            v.scalar_tensor_tensor(zz[:], vv[:], 1.0, vv[:], ALU.mult, ALU.mult,
                                                   accum_out=stats[:, 1:2])
                            v.tensor_scalar(stats[:, 2:3], stats[:, 0:1], 1.0 / 2048.0, None, ALU.mult)
                            v.tensor_scalar(stats[:, 3:4], stats[:, 1:2], 1.0 / 2048.0, None, ALU.mult).then_inc(cWV.sem, 1)
                            v.wait_ge(cWV.sem, cWV.n + 1)
                            v.tensor_tensor(stats[:, 4:5], stats[:, 2:3], stats[:, 2:3], ALU.mult).then_inc(cWV.sem, 1)
                            v.wait_ge(cWV.sem, cWV.n + 2)
                            v.tensor_tensor(stats[:, 3:4], stats[:, 3:4], stats[:, 4:5], ALU.subtract).then_inc(cWV.sem, 1)
                        cWV.n += 3
                        cPSF.n += 1

                        @block.scalar
                        def _(s):
                            s.wait_ge(cWV.sem, cWV.n)
                            s.activation(stats[:, 5:6], stats[:, 3:4], AF.Ln, bias=cpp[:, C_E5:C_E5 + 1]).then_inc(cSC.sem, 1)
                            s.wait_ge(cSC.sem, cSC.n + 1)
                            s.activation(stats[:, 5:6], stats[:, 5:6], AF.Exp, scale=-0.5).then_inc(cSC.sem, 1)
                        cSC.n += 2

                        @block.vector
                        def _(v):
                            v.wait_ge(cSC.sem, cSC.n)
                            v.tensor_scalar(ob[:], vv[:], stats[:, 2:3], stats[:, 5:6],
                                            ALU.subtract, ALU.mult).then_inc(cWV.sem, 1)
                        cWV.n += 1

                        @block.sync
                        def _(sync, b=b, tt=tt):
                            sync.wait_ge(cWV.sem, cWV.n)
                            sync.dma_start(out=d_out[b, tt * 128:(tt + 1) * 128, :], in_=ob[:]).then_inc(cO.sem, 16)
                        cO.n += 16

        @block.sync
        def _(sync):
            for ch in C.values():
                if ch.n > 0:
                    sync.wait_ge(ch.sem, ch.n)

    return nc


def _host_layouts(inputs):
    wq_b = np.ascontiguousarray(np.asarray(inputs["L0_attn_wq_b"], dtype=np.float32))
    wkv_a = np.ascontiguousarray(np.asarray(inputs["L0_attn_wkv_a"], dtype=np.float32))
    wkv_b = np.ascontiguousarray(np.asarray(inputs["L0_attn_wkv_b"], dtype=np.float32))
    perm = np.concatenate([np.arange(0, 64, 2), np.arange(1, 64, 2)])
    perm_r = np.concatenate([np.arange(1, 64, 2), np.arange(0, 64, 2)])
    wqbN = np.empty((64, NH * 64), np.float32)
    wqbX = np.empty((64, NH * 64), np.float32)
    wqbR = np.empty((64, NH * 64), np.float32)
    wkbN = np.empty((64, NH * 64), np.float32)
    wkbV = np.empty((64, NH * 64), np.float32)
    for h in range(NH):
        base = h * 128
        wqbN[:, h * 64:(h + 1) * 64] = wq_b[:, base:base + 64]
        wqbX[:, h * 64:(h + 1) * 64] = wq_b[:, base + 64 + perm]
        wqbR[:, h * 64:(h + 1) * 64] = wq_b[:, base + 64 + perm_r]
        wkbN[:, h * 64:(h + 1) * 64] = wkv_b[:, base:base + 64]
        wkbV[:, h * 64:(h + 1) * 64] = wkv_b[:, base + 64:base + 128]
    wkva = np.empty((D, 128), np.float32)
    wkva[:, 0:64] = wkv_a[:, 0:64]
    wkva[:, 64:128] = wkv_a[:, 64 + perm]
    w8 = np.asarray(inputs["L0_ffn_weight"])
    w8T = np.ascontiguousarray(w8.view(np.uint8).T).view(ml_dtypes.float8_e4m3)
    return wqbN, wqbX, wqbR, wkbN, wkbV, wkva, w8T


_NC_CACHE = {}


def kernel(**inputs):
    x = np.ascontiguousarray(np.asarray(inputs["L0_attn_hidden_states"], dtype=np.float32))
    wqbN, wqbX, wqbR, wkbN, wkbV, wkva, w8T = _host_layouts(inputs)
    cos4, sin4 = _yarn_tables()
    wqa = np.ascontiguousarray(np.asarray(inputs["L0_attn_wq_a"], dtype=np.float32))
    qnw = np.ascontiguousarray(np.asarray(inputs["L0_attn_q_a_norm_w"], dtype=np.float32).reshape(64, 1))
    knw = np.ascontiguousarray(np.asarray(inputs["L0_attn_kv_a_norm_w"], dtype=np.float32).reshape(64, 1))
    wo = np.ascontiguousarray(np.asarray(inputs["L0_attn_wo"], dtype=np.float32))
    sw = np.ascontiguousarray(np.repeat(np.asarray(inputs["L0_ffn_weight_scale"], dtype=np.float32).reshape(1, 1), 128, axis=0))
    s_in = np.asarray(inputs["L0_ffn_input_scale"]).reshape(1, 1)
    if s_in.dtype == np.uint16:
        s_in = s_in.view(ml_dtypes.bfloat16)
    assert s_in.dtype == ml_dtypes.bfloat16, s_in.dtype
    s_in = np.ascontiguousarray(np.repeat(s_in, 128, axis=0))

    if "nc" not in _NC_CACHE:
        _NC_CACHE["nc"] = build_nc()
    nc = _NC_CACHE["nc"]

    common = {
        "wqa": wqa, "wqbN": wqbN, "wqbX": wqbX, "wqbR": wqbR,
        "wkva": wkva, "wkbN": wkbN, "wkbV": wkbV,
        "qnw": qnw, "knw": knw, "wo": wo, "w8T": w8T,
        "sw": sw, "sin_": s_in, "cos4": cos4, "sin4": sin4,
    }
    in_maps = []
    for c in range(NCORES):
        m = dict(common)
        m["x"] = np.ascontiguousarray(x[c * NB:(c + 1) * NB])
        in_maps.append(m)

    res = run_bass_kernel_spmd(nc, in_maps, core_ids=list(range(NCORES)))
    out = np.concatenate([np.asarray(r["out"]) for r in res.results], axis=0)
    return out.astype(np.float32, copy=False)


# version 12
# speedup vs baseline: 1.0991x; 1.0991x over previous
import math
import numpy as np
import ml_dtypes
from contextlib import ExitStack

import concourse.bass as bass
import concourse.mybir as mybir
from concourse.bass_utils import run_bass_kernel_spmd

F32 = mybir.dt.float32
BF16 = mybir.dt.bfloat16
FP8 = mybir.dt.float8e4
AF = mybir.ActivationFunctionType
ALU = mybir.AluOpType

NCORES = 8
NB = 2
S = 2048
D = 2048
NH = 16
NP = 8
TT = 16
FT = 16

C32 = float(np.float32(1.5 * 2.0 ** 23))
C52 = float(np.float32(1.5 * 2.0 ** 52))
LN2INV2 = float(np.float32(1.0 / (2.0 * math.log(2.0))))
SM_SCALE = float(np.float32(128.0 ** -0.5))


def _yarn_tables():
    ROPE_BASE, FACTOR = 10000.0, 40.0
    BETA_FAST, BETA_SLOW, ORIG_MAX, MSCALE = 32.0, 1.0, 4096, 0.707
    dim = 64
    exponents = np.arange(0, dim, 2, dtype=np.float64) / dim
    freq_extra = 1.0 / ROPE_BASE ** exponents
    freq_inter = 1.0 / (FACTOR * ROPE_BASE ** exponents)

    def corr_dim(num_rot):
        return dim * math.log(ORIG_MAX / (num_rot * 2 * math.pi)) / (2 * math.log(ROPE_BASE))

    low = max(math.floor(corr_dim(BETA_FAST)), 0)
    high = min(math.ceil(corr_dim(BETA_SLOW)), dim - 1)
    if low == high:
        high = high + 0.001
    ramp = np.clip((np.arange(dim // 2, dtype=np.float64) - low) / (high - low), 0.0, 1.0)
    inv_freq = freq_inter * ramp + freq_extra * (1.0 - ramp)
    t = np.arange(S, dtype=np.float64)
    freqs = np.outer(t, inv_freq)

    def yarn_get_mscale(scale, mscale=1.0):
        if scale <= 1:
            return 1.0
        return 0.1 * mscale * math.log(scale) + 1.0

    msc = yarn_get_mscale(FACTOR, MSCALE) / yarn_get_mscale(FACTOR, 0.0)
    cosF = (np.cos(freqs) * msc).astype(np.float32).T
    sinF = (np.sin(freqs) * msc).astype(np.float32).T
    return (np.ascontiguousarray(np.tile(cosF, (4, 1))),
            np.ascontiguousarray(np.tile(sinF, (4, 1))))


class Ch:
    def __init__(self, stack, nc, name):
        self.sem = stack.enter_context(nc.semaphore(name))
        self.n = 0


PHASES = 99
E_STEPS = 99
E_SUB = 99
E_GEN = 99
DBG = False


def build_nc():
    nc = bass.Bass("TRN2", target_bir_lowering=False, detect_race_conditions=False)

    d_x = nc.dram_tensor("x", [NB, S, D], F32, kind="ExternalInput").ap()
    d_wqa = nc.dram_tensor("wqa", [D, 64], F32, kind="ExternalInput").ap()
    d_wqbN = nc.dram_tensor("wqbN", [64, NH * 64], F32, kind="ExternalInput").ap()
    d_wqbX = nc.dram_tensor("wqbX", [64, NH * 64], F32, kind="ExternalInput").ap()
    d_wqbR = nc.dram_tensor("wqbR", [64, NH * 64], F32, kind="ExternalInput").ap()
    d_wkva = nc.dram_tensor("wkva", [D, 128], F32, kind="ExternalInput").ap()
    d_wkbN = nc.dram_tensor("wkbN", [64, NH * 64], F32, kind="ExternalInput").ap()
    d_wkbV = nc.dram_tensor("wkbV", [64, NH * 64], F32, kind="ExternalInput").ap()
    d_qnw = nc.dram_tensor("qnw", [64, 1], F32, kind="ExternalInput").ap()
    d_knw = nc.dram_tensor("knw", [64, 1], F32, kind="ExternalInput").ap()
    d_wo = nc.dram_tensor("wo", [NH * 64, D], F32, kind="ExternalInput").ap()
    d_w8T = nc.dram_tensor("w8T", [D, D], FP8, kind="ExternalInput").ap()
    d_sw = nc.dram_tensor("sw", [128, 1], F32, kind="ExternalInput").ap()
    d_sin = nc.dram_tensor("sin_", [128, 1], BF16, kind="ExternalInput").ap()
    d_cos4 = nc.dram_tensor("cos4", [128, S], F32, kind="ExternalInput").ap()
    d_sin4 = nc.dram_tensor("sin4", [128, S], F32, kind="ExternalInput").ap()
    d_out = nc.dram_tensor("out", [NB, S, D], F32, kind="ExternalOutput").ap()
    scrk = "ExternalOutput" if DBG else "Internal"
    d_xqdbg = nc.dram_tensor("xqdbg", [8, 128, D], BF16, kind="ExternalOutput").ap() if DBG else None
    d_cppdbg = nc.dram_tensor("cppdbg", [128, 16], F32, kind="ExternalOutput").ap() if DBG else None
    d_xq8dbg = nc.dram_tensor("xq8dbg", [128, FT * 1024], FP8, kind="ExternalOutput").ap() if DBG else None
    d_zscr = nc.dram_tensor("zscr", [NB, S, D], F32, kind=scrk).ap()
    d_qscr = nc.dram_tensor("qscr", [NH, 128, S], BF16, kind=scrk).ap()
    d_kscr = nc.dram_tensor("kscr", [NH, 128, S], BF16, kind=scrk).ap()

    with ExitStack() as ctx:
        def sbt(name, shape, dt=F32):
            return ctx.enter_context(nc.sbuf_tensor(name, shape, dt))

        w8 = sbt("w8", [128, FT * D], FP8)
        cos4 = sbt("cos4_b", [128, S], BF16)
        sin4 = sbt("sin4_b", [128, S], BF16)
        wqa_b = sbt("wqa_b", [128, FT * 64], BF16)
        wkva_b = sbt("wkva_b", [128, FT * 128], BF16)
        wqbN_b = sbt("wqbN_b", [64, NH * 64], BF16)
        wqbX_b = sbt("wqbX_b", [64, NH * 64], BF16)
        wqbR_b = sbt("wqbR_b", [64, NH * 64], BF16)
        wkbN_b = sbt("wkbN_b", [64, NH * 64], BF16)
        wkbV_b = sbt("wkbV_b", [64, NH * 64], BF16)
        ones_b = sbt("ones_b", [64, 128], BF16)
        cpp = sbt("cpp", [128, 16], F32)
        qnw_s = sbt("qnw_s", [64, 1], F32)
        knw_s = sbt("knw_s", [64, 1], F32)
        C_TAU, C_LN2, C_NLN2, C_N2LN2, C_RS, C_DEQ, C_E6, C_E5, C_C40, C_TWO = range(10)

        bigA = sbt("bigA", [128, 16384], BF16)
        stats = sbt("stats", [128, 8], F32)

        PS = ctx.enter_context(nc.psum_tensor("PS", [128, 4096], F32))

        chnames = ["cLD", "cSET", "cXa", "cXb", "cTRa", "cTRb", "cTM", "cSC", "cVE", "cQKD",
                   "cQKLa", "cQKLb", "cQKF", "cL", "cE2", "cAV", "cEP", "cWM", "cX2a", "cX2b",
                   "cZD", "cZL", "cQT", "cQ8", "cFM", "cO", "cWV", "cQA", "cQB", "cWL", "cPSW", "cPSF"]
        C = {n: Ch(ctx, nc, n) for n in chnames}
        (cLD, cSET, cXa, cXb, cTRa, cTRb, cTM, cSC, cVE, cQKD, cQKLa, cQKLb, cQKF,
         cL, cE2, cAV, cEP, cWM, cX2a, cX2b, cZD, cZL, cQT, cQ8, cFM, cO, cWV,
         cQA, cQB, cWL, cPSW, cPSF) = (C[n] for n in chnames)

        block = ctx.enter_context(nc.Block())

        @block.scalar
        def _(s):
            from concourse.hw_specs import get_activation_tables
            for idx, funcs in enumerate(get_activation_tables(nc.m.arch).values()):
                if AF.Ln in funcs and AF.Exp in funcs:
                    s.add_instruction(mybir.InstLoadActFuncSet(
                        name=nc.get_next_instruction_name(),
                        act_func_set_id=idx, ins=[], outs=[]))
                    break

        @block.sync
        def _(sync):
            sync.dma_start(out=w8[:].rearrange("p (t c) -> p t c", c=D),
                           in_=d_w8T.rearrange("(t p) c -> p t c", p=128)).then_inc(cLD.sem, 16)
        cLD.n += 16

        with nc.sbuf_tensor("stage32", [128, S], F32) as stage32, \
             nc.sbuf_tensor("sin_b", [128, 1], BF16) as sin_b:

            stage_jobs = [
                (d_cos4, cos4, 128, S),
                (d_sin4, sin4, 128, S),
                (d_wqa.rearrange("(t p) c -> p t c", p=128), wqa_b, 128, FT * 64),
                (d_wkva.rearrange("(t p) c -> p t c", p=128), wkva_b, 128, FT * 128),
                (d_wqbN, wqbN_b, 64, NH * 64),
                (d_wqbX, wqbX_b, 64, NH * 64),
                (d_wkbN, wkbN_b, 64, NH * 64),
                (d_wkbV, wkbV_b, 64, NH * 64),
            ]
            for src, dst, p, width in stage_jobs:
                @block.sync
                def _(sync, src=src, p=p, width=width):
                    sync.wait_ge(cSET.sem, cSET.n)
                    o = stage32[0:p, 0:width]
                    if len(src.shape) == 3:
                        o = o.rearrange("p (t c) -> p t c", t=src.shape[1])
                    sync.dma_start(out=o, in_=src).then_inc(cLD.sem, 16)
                cLD.n += 16

                @block.vector
                def _(v, dst=dst, p=p, width=width):
                    v.wait_ge(cLD.sem, cLD.n)
                    v.tensor_copy(dst[0:p, 0:width], stage32[0:p, 0:width]).then_inc(cSET.sem, 1)
                cSET.n += 1

            @block.sync
            def _(sync):
                sync.wait_ge(cSET.sem, cSET.n)
                sync.dma_start(out=stage32[0:64, 0:NH * 64], in_=d_wqbR).then_inc(cLD.sem, 16)
            cLD.n += 16

            @block.vector
            def _(v):
                v.wait_ge(cLD.sem, cLD.n)
                si = stage32[0:64, 0:NH * 64].rearrange("p (h c) -> p h c", c=64)
                so = wqbR_b[0:64, :].rearrange("p (h c) -> p h c", c=64)
                v.tensor_scalar(so[:, :, 0:32], si[:, :, 0:32], -1.0, None, ALU.mult)
                v.tensor_copy(so[:, :, 32:64], si[:, :, 32:64]).then_inc(cSET.sem, 1)
            cSET.n += 1

            @block.sync
            def _(sync):
                sync.wait_ge(cSET.sem, cSET.n)
                sync.dma_start(out=cpp[:, C_RS:C_RS + 1], in_=d_sw).then_inc(cLD.sem, 16)
                sync.dma_start(out=sin_b[:], in_=d_sin).then_inc(cLD.sem, 16)
                sync.dma_start(out=qnw_s[:], in_=d_qnw).then_inc(cLD.sem, 16)
                sync.dma_start(out=knw_s[:], in_=d_knw).then_inc(cLD.sem, 16)
            cLD.n += 64

            @block.vector
            def _(v):
                v.memset(cpp[:, 10:16], 0.0)
                v.wait_ge(cLD.sem, cLD.n)
                v.memset(cpp[:, C_TAU:C_TAU + 1], 2.0 ** 52).then_inc(cSET.sem, 1)
                v.memset(cpp[:, C_TWO:C_TWO + 1], 2.0)
                v.memset(cpp[:, C_E6:C_E6 + 1], 1e-6)
                v.memset(cpp[:, C_E5:C_E5 + 1], 1e-5)
                v.memset(cpp[:, C_C40:C_C40 + 1], 2.0 ** -40)
                v.tensor_copy(cpp[:, C_DEQ:C_DEQ + 1], sin_b[:]).then_inc(cSET.sem, 1)
                v.wait_ge(cSET.sem, cSET.n + 2)
                v.tensor_scalar(cpp[:, C_TAU:C_TAU + 1], cpp[:, C_TAU:C_TAU + 1], 1.0, -(2.0 ** 52), ALU.add, ALU.add)
                v.tensor_tensor(cpp[:, C_DEQ:C_DEQ + 1], cpp[:, C_DEQ:C_DEQ + 1], cpp[:, C_RS:C_RS + 1], ALU.mult).then_inc(cSET.sem, 1)
                v.wait_ge(cSET.sem, cSET.n + 3)
                v.tensor_scalar(cpp[:, C_DEQ:C_DEQ + 1], cpp[:, C_DEQ:C_DEQ + 1], 4.0, None, ALU.mult)
                v.tensor_copy(cpp[:, C_RS:C_RS + 1], sin_b[:]).then_inc(cSET.sem, 1)
                v.wait_ge(cSET.sem, cSET.n + 4)
                v.reciprocal(cpp[:, C_RS:C_RS + 1], cpp[:, C_RS:C_RS + 1])
                v.tensor_scalar(qnw_s[:], qnw_s[:], SM_SCALE, None, ALU.mult)
                v.memset(ones_b[:], 1.0).then_inc(cSET.sem, 1)
            cSET.n += 5

            @block.scalar
            def _(s):
                s.wait_ge(cSET.sem, cSET.n)
                s.activation(cpp[:, C_LN2:C_LN2 + 1], cpp[:, C_TWO:C_TWO + 1], AF.Ln)
                s.mul(cpp[:, C_NLN2:C_NLN2 + 1], cpp[:, C_LN2:C_LN2 + 1], -1.0)
                s.mul(cpp[:, C_N2LN2:C_N2LN2 + 1], cpp[:, C_LN2:C_LN2 + 1], -2.0).then_inc(cSET.sem, 1)
            cSET.n += 1

        SETUP_DONE = cSET.n

        for b in range(NB if PHASES >= 6 else (1 if PHASES >= 2 else 0)):
            with ExitStack() as ea:
                def eat(name, shape, dt):
                    return ea.enter_context(nc.sbuf_tensor(name + str(b), shape, dt))
                vbuf = eat("vbuf", [128, TT * 1536], BF16)
                qp = [eat("qp0", [128, S], BF16), eat("qp1", [128, S], BF16)]
                kp = [eat("kp0", [128, S], BF16), eat("kp1", [128, S], BF16)]
                a2 = [eat(f"a2_{i}", [128, 1024], BF16) for i in range(3)]
                rcp_full = eat("rcp", [64, S], F32)
                rcp = rcp_full[0:64, 0:1024]
                qlat_n = eat("qlat_n", [64, S], BF16)
                kv_n = eat("kv_n", [64, S], BF16)
                kpe_r = eat("kpe_r", [64, S], BF16)

                with ExitStack() as ee:
                    def eet(name, shape, dt):
                        return ee.enter_context(nc.sbuf_tensor(name + str(b), shape, dt))
                    xbf = eet("xbf", [128, 2 * D], BF16)
                    kv_bf = eet("kv_bf", [64, S], BF16)
                    ql_bf = eet("ql_bf", [64, S], BF16)
                    kpe_bf = eet("kpe_bf", [64, S], BF16)
                    krot_bf = eet("krot_bf", [64, S], BF16)
                    tln = rcp_full
                    t1 = eet("t1", [128, 512], F32)
                    t2 = eet("t2", [128, 512], F32)
                    sq_bf = krot_bf

                    for hf in range(2):
                        for j in range(8):
                            tt = hf * 8 + j

                            cXp = cXa if tt % 2 == 0 else cXb
                            cTRp = cTRa if tt % 2 == 0 else cTRb

                            @block.gpsimd
                            def _(g, b=b, tt=tt, j=j, hf=hf, cXp=cXp, cTRp=cTRp):
                                if b > 0 and hf == 0 and j == 0:
                                    g.wait_ge(cO.sem, cO.n)
                                    g.wait_ge(cWV.sem, cWV.n)
                                    g.wait_ge(cSC.sem, cSC.n)
                                    g.wait_ge(cVE.sem, cVE.n)
                                if tt >= 2:
                                    g.wait_ge(cTRp.sem, cTRp.n)
                                g.dma_start(out=xbf[:, (j % 2) * D:(j % 2 + 1) * D],
                                            in_=d_x[b, tt * 128:(tt + 1) * 128, :]).then_inc(cXp.sem, 16)
                            cXp.n += 16

                            def mk_tr(fts):
                                def sec(s, j=j, b=b, hf=hf, cXp=cXp, cTRp=cTRp, fts=fts):
                                    s.wait_ge(cXp.sem, cXp.n)
                                    if j == 0:
                                        s.wait_ge(cTM.sem, cTM.n)
                                        s.wait_ge(cWM.sem, cWM.n)
                                    for ft in fts:
                                        s.dma_start_transpose(
                                            bigA[:, ft * 1024 + j * 128: ft * 1024 + (j + 1) * 128],
                                            xbf[:, (j % 2) * D + ft * 128:(j % 2) * D + (ft + 1) * 128]
                                        ).then_inc(cTRp.sem, 16)
                                return sec
                            block.sync(mk_tr(range(0, FT)))
                            cTRp.n += 256

                        @block.tensor
                        def _(t, hf=hf):
                            if E_STEPS < 2:
                                return
                            t.wait_ge(cTRa.sem, cTRa.n)
                            t.wait_ge(cTRb.sem, cTRb.n)
                            if hf == 0:
                                t.wait_ge(cSET.sem, SETUP_DONE)
                                t.wait_ge(cSC.sem, cSC.n)
                                t.wait_ge(cWV.sem, cWV.n)
                            for n in range(2):
                                for ft in range(FT):
                                    t.matmul(PS[:, hf * 1024 + n * 512: hf * 1024 + (n + 1) * 512],
                                             wkva_b[:, ft * 128:(ft + 1) * 128],
                                             bigA[:, ft * 1024 + n * 512: ft * 1024 + (n + 1) * 512],
                                             start=(ft == 0), stop=(ft == FT - 1))
                            for n in range(2):
                                for ft in range(FT):
                                    ins = t.matmul(PS[0:64, 2048 + hf * 1024 + n * 512: 2048 + hf * 1024 + (n + 1) * 512],
                                                   wqa_b[:, ft * 64:(ft + 1) * 64],
                                                   bigA[:, ft * 1024 + n * 512: ft * 1024 + (n + 1) * 512],
                                                   start=(ft == 0), stop=(ft == FT - 1))
                            ins.then_inc(cTM.sem, 1)
                        if E_STEPS >= 2:
                            cTM.n += 1
                    if E_STEPS < 3:
                        continue

                    @block.scalar
                    def _(s):
                        s.wait_ge(cTM.sem, cTM.n)
                        s.copy(kv_bf[:], PS[0:64, 0:2048])
                        s.copy(kpe_bf[:], PS[64:128, 0:2048])
                        s.copy(ql_bf[:], PS[0:64, 2048:4096])
                        s.mul(krot_bf[0:32, :], kpe_bf[32:64, :], -1.0)
                        s.copy(krot_bf[32:64, :], kpe_bf[0:32, :]).then_inc(cSC.sem, 1)
                    cSC.n += 1
                    if E_STEPS < 4:
                        continue

                    @block.vector
                    def _(v):
                        v.wait_ge(cSC.sem, cSC.n)
                        if cSET.n == SETUP_DONE:
                            v.wait_ge(cSET.sem, SETUP_DONE)
                        for n in range(4):
                            sl = slice(n * 512, (n + 1) * 512)
                            v.tensor_tensor(t1[0:64, :], kpe_bf[:, sl], cos4[0:64, sl], ALU.mult)
                            v.tensor_tensor(t2[0:64, :], krot_bf[:, sl], sin4[0:64, sl], ALU.mult)
                            ins = v.tensor_tensor(kpe_r[:, sl], t1[0:64, :], t2[0:64, :], ALU.add)
                        ins.then_inc(cVE.sem, 1)
                    cVE.n += 1

                    if E_SUB < 2:
                        continue

                    @block.vector
                    def _(v):
                        v.tensor_tensor(sq_bf[:], ql_bf[:], ql_bf[:], ALU.mult).then_inc(cVE.sem, 1)
                    cVE.n += 1
                    if E_SUB < 3:
                        continue

                    @block.tensor
                    def _(t):
                        t.wait_ge(cVE.sem, cVE.n)
                        for n in range(4):
                            ins = t.matmul(PS[:, n * 512:(n + 1) * 512], ones_b[:],
                                           sq_bf[:, n * 512:(n + 1) * 512], start=True, stop=True)
                        ins.then_inc(cTM.sem, 1)
                    cTM.n += 1

                    if E_SUB >= 4:
                        @block.scalar
                        def _(s):
                            s.wait_ge(cTM.sem, cTM.n)
                            s.activation(tln[:], PS[0:64, 0:2048], AF.Ln, bias=cpp[0:64, C_E6:C_E6 + 1], scale=1.0 / 64.0)
                            s.activation(tln[:], tln[:], AF.Exp, scale=-0.5).then_inc(cSC.sem, 1)
                        cSC.n += 1
                    if E_SUB < 5:
                        continue

                    @block.vector
                    def _(v):
                        v.wait_ge(cSC.sem, cSC.n)
                        v.tensor_tensor(tln[:], ql_bf[:], tln[:], ALU.mult)
                        v.tensor_scalar(qlat_n[:], tln[:], qnw_s[:], None, ALU.mult)
                        v.tensor_tensor(sq_bf[:], kv_bf[:], kv_bf[:], ALU.mult).then_inc(cVE.sem, 1)
                    cVE.n += 1
                    if E_SUB < 6:
                        continue

                    @block.tensor
                    def _(t):
                        t.wait_ge(cVE.sem, cVE.n)
                        for n in range(4):
                            ins = t.matmul(PS[:, 2048 + n * 512: 2048 + (n + 1) * 512], ones_b[:],
                                           sq_bf[:, n * 512:(n + 1) * 512], start=True, stop=True)
                        ins.then_inc(cTM.sem, 1)
                    cTM.n += 1

                    @block.scalar
                    def _(s):
                        s.wait_ge(cTM.sem, cTM.n)
                        s.activation(tln[:], PS[0:64, 2048:4096], AF.Ln, bias=cpp[0:64, C_E6:C_E6 + 1], scale=1.0 / 64.0)
                        s.activation(tln[:], tln[:], AF.Exp, scale=-0.5).then_inc(cSC.sem, 1)
                    cSC.n += 1

                    @block.vector
                    def _(v):
                        v.wait_ge(cSC.sem, cSC.n)
                        v.tensor_tensor(tln[:], kv_bf[:], tln[:], ALU.mult)
                        v.tensor_scalar(kv_n[:], tln[:], knw_s[:], None, ALU.mult).then_inc(cVE.sem, 1)
                    cVE.n += 1
                    if E_STEPS < 5:
                        continue

                    for tt in range(TT):
                        @block.tensor
                        def _(t, tt=tt):
                            t.wait_ge(cVE.sem, cVE.n)
                            t.wait_ge(cSC.sem, cSC.n)
                            for n in range(2):
                                ins = t.matmul(PS[:, (tt % 2) * 1024 + n * 512:(tt % 2) * 1024 + (n + 1) * 512],
                                               kv_n[:, tt * 128:(tt + 1) * 128],
                                               wkbV_b[:, n * 512:(n + 1) * 512], start=True, stop=True)
                            ins.then_inc(cTM.sem, 1)
                        cTM.n += 1

                        @block.scalar
                        def _(s, tt=tt, b=b):
                            s.wait_ge(cTM.sem, cTM.n)
                            if tt == 0 and b > 0:
                                s.wait_ge(cAV.sem, cAV.n)
                            src = PS[:, (tt % 2) * 1024:(tt % 2) * 1024 + 1024].rearrange(
                                "p (pr j c) -> p pr j c", pr=8, j=2, c=64)
                            dst = vbuf[:, tt * 1536:(tt + 1) * 1536].rearrange(
                                "p (pr g c) -> p pr g c", pr=8, g=3, c=64)[:, :, 0:3:2, :]
                            s.copy(dst, src).then_inc(cSC.sem, 1)
                        cSC.n += 1

                    if E_STEPS < 6:
                        continue

                    @block.vector
                    def _(v, b=b):
                        if b > 0:
                            v.wait_ge(cAV.sem, cAV.n)
                        ones_ap = vbuf[:].rearrange("p (t pr g c) -> p t pr g c", pr=8, g=3, c=64)[:, :, :, 1:2, :]
                        v.memset(ones_ap, 1.0).then_inc(cVE.sem, 1)
                    cVE.n += 1

                    for p in range(NP if E_STEPS >= 7 else 0):
                        @block.tensor
                        def _(t, p=p):
                            t.wait_ge(cSC.sem, cSC.n)
                            t.wait_ge(cVE.sem, cVE.n)
                            for n in range(4):
                                t.matmul(PS[:, n * 512:(n + 1) * 512], wqbX_b[:, p * 128:(p + 1) * 128],
                                         qlat_n[:, n * 512:(n + 1) * 512], start=True, stop=True)
                            for n in range(4):
                                ins = t.matmul(PS[:, 2048 + n * 512: 2048 + (n + 1) * 512], wqbR_b[:, p * 128:(p + 1) * 128],
                                               qlat_n[:, n * 512:(n + 1) * 512], start=True, stop=True)
                            ins.then_inc(cTM.sem, 1)
                        cTM.n += 1

                        if E_GEN < 2:
                            continue

                        @block.vector
                        def _(v, p=p):
                            v.wait_ge(cTM.sem, cTM.n)
                            if p > 0:
                                v.wait_ge(cQKD.sem, cQKD.n)
                            for n in range(4):
                                sl = slice(n * 512, (n + 1) * 512)
                                v.tensor_tensor(t1[:], PS[:, sl], cos4[:, sl], ALU.mult)
                                ins = v.tensor_tensor(t2[:], PS[:, 2048 + n * 512: 2048 + (n + 1) * 512],
                                                      sin4[:, sl], ALU.mult)
                                if n == 3:
                                    ins.then_inc(cVE.sem, 1)
                                v.tensor_tensor(qp[0][0:64, sl], t1[0:64, :], t2[0:64, :], ALU.add)
                                v.tensor_tensor(qp[1][64:128, sl], t1[64:128, :], t2[64:128, :], ALU.add)
                            v.tensor_copy(kp[0][0:64, :], kpe_r[:])
                            v.tensor_copy(kp[1][64:128, :], kpe_r[:]).then_inc(cVE.sem, 1)
                        cVE.n += 2

                        if E_GEN < 3:
                            continue

                        @block.tensor
                        def _(t, p=p):
                            t.wait_ge(cVE.sem, cVE.n)
                            for n in range(4):
                                t.matmul(PS[:, n * 512:(n + 1) * 512], wqbN_b[:, p * 128:(p + 1) * 128],
                                         qlat_n[:, n * 512:(n + 1) * 512], start=True, stop=True)
                            for n in range(4):
                                ins = t.matmul(PS[:, 2048 + n * 512: 2048 + (n + 1) * 512], wkbN_b[:, p * 128:(p + 1) * 128],
                                               kv_n[:, n * 512:(n + 1) * 512], start=True, stop=True)
                            ins.then_inc(cTM.sem, 1)
                        cTM.n += 1

                        if E_GEN < 4:
                            continue

                        @block.scalar
                        def _(s, p=p):
                            s.wait_ge(cTM.sem, cTM.n)
                            if p > 0:
                                s.wait_ge(cQKD.sem, cQKD.n)
                            s.copy(qp[0][64:128, :], PS[0:64, 0:2048])
                            s.copy(qp[1][0:64, :], PS[64:128, 0:2048])
                            s.copy(kp[0][64:128, :], PS[0:64, 2048:4096])
                            s.copy(kp[1][0:64, :], PS[64:128, 2048:4096]).then_inc(cSC.sem, 1)
                        cSC.n += 1

                        if E_GEN < 5:
                            continue

                        @block.sync
                        def _(sync, p=p):
                            sync.wait_ge(cSC.sem, cSC.n)
                            sync.wait_ge(cVE.sem, cVE.n)
                            sync.dma_start(out=d_qscr[2 * p], in_=qp[0][:]).then_inc(cQKD.sem, 16)
                            sync.dma_start(out=d_qscr[2 * p + 1], in_=qp[1][:]).then_inc(cQKD.sem, 16)
                            sync.dma_start(out=d_kscr[2 * p], in_=kp[0][:]).then_inc(cQKD.sem, 16)
                            sync.dma_start(out=d_kscr[2 * p + 1], in_=kp[1][:]).then_inc(cQKD.sem, 16)
                        cQKD.n += 64

                EXP_BASE, AV_BASE, EP_BASE, L_BASE = cE2.n, cAV.n, cEP.n, cL.n

                def emit_av(h, kt, g, hq):
                    @block.tensor
                    def _(t, h=h, kt=kt, g=g, hq=hq):
                        t.wait_ge(cE2.sem, EXP_BASE + g + 1)
                        if kt == 0 and hq >= 2:
                            t.wait_ge(cEP.sem, EP_BASE + hq - 1)
                        voff = kt * 1536 + (h // 2) * 192 + (h % 2) * 64
                        for n in range(2):
                            ins = t.matmul(PS[:, 2048 + (hq % 2) * 1024 + n * 512:
                                               2048 + (hq % 2) * 1024 + (n + 1) * 512],
                                           vbuf[:, voff:voff + 128],
                                           a2[g % 3][:, n * 512:(n + 1) * 512],
                                           start=(kt == 0), stop=(kt == 15))
                        ins.then_inc(cAV.sem, 1)
                    cAV.n += 1
                    if kt == 15:
                        @block.vector
                        def _(v, h=h, qc=hq % 2, hq=hq):
                            v.wait_ge(cAV.sem, AV_BASE + (hq + 1) * 16)
                            ob_ = 2048 + (hq % 2) * 1024
                            dst_c = (h // 2) * 2048 + qc * 1024
                            if h % 2 == 0:
                                v.reciprocal(rcp, PS[64:128, ob_:ob_ + 1024])
                                v.tensor_tensor(bigA[0:64, dst_c:dst_c + 1024],
                                                PS[0:64, ob_:ob_ + 1024], rcp, ALU.mult).then_inc(cEP.sem, 1)
                            else:
                                v.reciprocal(rcp, PS[0:64, ob_:ob_ + 1024])
                                v.tensor_tensor(bigA[64:128, dst_c:dst_c + 1024],
                                                PS[64:128, ob_:ob_ + 1024], rcp, ALU.mult).then_inc(cEP.sem, 1)
                        cEP.n += 1

                pend_av = None
                for h in range(NH if PHASES >= 3 else 0):
                    cQKLp = cQKLa if h % 2 == 0 else cQKLb

                    @block.sync
                    def _(sync, h=h, cQKLp=cQKLp):
                        if h < 2:
                            sync.wait_ge(cQKD.sem, cQKD.n)
                        else:
                            sync.wait_ge(cL.sem, L_BASE + (h - 1) * 32)
                        sync.dma_start(out=qp[h % 2][:], in_=d_qscr[h]).then_inc(cQKLp.sem, 16)
                        sync.dma_start(out=kp[h % 2][:], in_=d_kscr[h]).then_inc(cQKLp.sem, 16)
                    cQKLp.n += 32
                    QKL_TGT = cQKLp.n

                    for qc in range(2):
                        hq = h * 2 + qc
                        for kt in range(16):
                            g = hq * 16 + kt

                            @block.tensor
                            def _(t, h=h, qc=qc, kt=kt, g=g, cQKLp=cQKLp, QKL_TGT=QKL_TGT):
                                if kt == 0 and qc == 0:
                                    t.wait_ge(cQKLp.sem, QKL_TGT)
                                if g >= 2:
                                    t.wait_ge(cE2.sem, EXP_BASE + g - 1)
                                for n in range(2):
                                    ins = t.matmul(PS[:, (g % 2) * 1024 + n * 512:(g % 2) * 1024 + (n + 1) * 512],
                                                   kp[h % 2][:, kt * 128:(kt + 1) * 128],
                                                   qp[h % 2][:, qc * 1024 + n * 512: qc * 1024 + (n + 1) * 512],
                                                   start=True, stop=True)
                                ins.then_inc(cL.sem, 1)
                            cL.n += 1

                            @block.scalar
                            def _(s, g=g):
                                s.wait_ge(cL.sem, L_BASE + g + 1)
                                if g >= 3:
                                    s.wait_ge(cAV.sem, AV_BASE + g - 2)
                                s.activation(a2[g % 3][:], PS[:, (g % 2) * 1024:(g % 2) * 1024 + 1024],
                                             AF.Exp).then_inc(cE2.sem, 1)
                            cE2.n += 1

                            if pend_av is not None:
                                emit_av(*pend_av)
                            pend_av = (h, kt, g, hq)

                if pend_av is not None:
                    emit_av(*pend_av)
                    pend_av = None

                ATT_DONE_EP = cEP.n
                if PHASES < 4:
                    continue

            with ExitStack() as wf:
                def wft(name, shape, dt):
                    return wf.enter_context(nc.sbuf_tensor(name + str(b), shape, dt))
                wo_bf = wft("wo_bf", [128, 8 * D], BF16)
                x2 = wft("x2", [128, 2 * D], F32)
                vv = wft("vv", [128, D], F32)
                zz = wft("zz", [128, D], F32)
                ob = wft("ob", [128, D], F32)
                xqb = wft("xqb", [128, D], BF16)
                xqs = wft("xqs", [128, D], BF16)
                xqT8 = wft("xqT8", [128, FT * 1024], FP8)
                qd = wft("qd", [128, 7 * 512], F32)
                QU, QW, QE, QA32, QA52, QQ, QQI = [slice(i * 512, (i + 1) * 512) for i in range(7)]

                for p8 in range(8):
                    @block.sync
                    def _(sync, p8=p8):
                        if p8 == 0:
                            sync.wait_ge(cVE.sem, cVE.n)
                            sync.wait_ge(cSC.sem, cSC.n)
                            sync.wait_ge(cQKD.sem, cQKD.n)
                            sync.wait_ge(cQKLa.sem, cQKLa.n)
                            sync.wait_ge(cQKLb.sem, cQKLb.n)
                            sync.wait_ge(cO.sem, cO.n)
                            sync.wait_ge(cEP.sem, cEP.n)
                        sync.wait_ge(cWV.sem, cWV.n)
                        sync.dma_start(out=x2[:, 0:D], in_=d_wo[p8 * 128:(p8 + 1) * 128, :]).then_inc(cWL.sem, 16)
                    cWL.n += 16

                    @block.vector
                    def _(v, p8=p8):
                        v.wait_ge(cWL.sem, cWL.n)
                        v.tensor_copy(wo_bf[:, p8 * D:(p8 + 1) * D], x2[:, 0:D]).then_inc(cWV.sem, 1)
                    cWV.n += 1
                WO_READY = cWV.n

                for hf in range(2):
                    for j in range(8):
                        tt = hf * 8 + j

                        cX2p = cX2a if tt % 2 == 0 else cX2b

                        @block.sync
                        def _(sync, b=b, tt=tt, cX2p=cX2p):
                            sync.wait_ge(cWV.sem, cWV.n)
                            sync.dma_start(out=x2[:, (tt % 2) * D:(tt % 2 + 1) * D],
                                           in_=d_x[b, tt * 128:(tt + 1) * 128, :]).then_inc(cX2p.sem, 16)
                        cX2p.n += 16

                        @block.tensor
                        def _(t, tt=tt, j=j, hf=hf, PSW_T=cPSW.n, PSF_T=cPSF.n):
                            if j == 0 and hf == 0:
                                t.wait_ge(cEP.sem, ATT_DONE_EP)
                                t.wait_ge(cAV.sem, cAV.n)
                                t.wait_ge(cWV.sem, WO_READY)
                            if hf > 0 and j < 2:
                                t.wait_ge(cPSF.sem, PSF_T)
                            if j >= 2:
                                t.wait_ge(cPSW.sem, PSW_T - 1)
                            for p8 in range(8):
                                for n in range(4):
                                    ins = t.matmul(PS[:, (tt % 2) * 2048 + n * 512:(tt % 2) * 2048 + (n + 1) * 512],
                                                   bigA[:, p8 * 2048 + tt * 128: p8 * 2048 + (tt + 1) * 128],
                                                   wo_bf[:, p8 * D + n * 512: p8 * D + (n + 1) * 512],
                                                   start=(p8 == 0), stop=(p8 == 7))
                            ins.then_inc(cWM.sem, 1)
                        cWM.n += 1

                        @block.vector
                        def _(v, tt=tt, cX2p=cX2p):
                            v.wait_ge(cWM.sem, cWM.n)
                            v.wait_ge(cX2p.sem, cX2p.n)
                            v.wait_ge(cO.sem, cO.n)
                            pb = (tt % 2) * 2048
                            v.scalar_tensor_tensor(vv[:], PS[:, pb:pb + 2048], 1.0,
                                                   x2[:, (tt % 2) * D:(tt % 2 + 1) * D],
                                                   ALU.mult, ALU.add, accum_out=stats[:, 0:1]).then_inc(cPSW.sem, 1)
                            v.scalar_tensor_tensor(ob[:], vv[:], 1.0, vv[:], ALU.mult, ALU.mult,
                                                   accum_out=stats[:, 1:2])
                            v.tensor_scalar(stats[:, 2:3], stats[:, 0:1], 1.0 / 2048.0, None, ALU.mult)
                            v.tensor_scalar(stats[:, 3:4], stats[:, 1:2], 1.0 / 2048.0, None, ALU.mult).then_inc(cWV.sem, 1)
                            v.wait_ge(cWV.sem, cWV.n + 1)
                            v.tensor_tensor(stats[:, 4:5], stats[:, 2:3], stats[:, 2:3], ALU.mult).then_inc(cWV.sem, 1)
                            v.wait_ge(cWV.sem, cWV.n + 2)
                            v.tensor_tensor(stats[:, 3:4], stats[:, 3:4], stats[:, 4:5], ALU.subtract).then_inc(cWV.sem, 1)
                        cWV.n += 3
                        cPSW.n += 1

                        @block.scalar
                        def _(s):
                            s.wait_ge(cWV.sem, cWV.n)
                            s.activation(stats[:, 5:6], stats[:, 3:4], AF.Ln, bias=cpp[:, C_E5:C_E5 + 1]).then_inc(cSC.sem, 1)
                            s.wait_ge(cSC.sem, cSC.n + 1)
                            s.activation(stats[:, 5:6], stats[:, 5:6], AF.Exp, scale=-0.5).then_inc(cSC.sem, 1)
                        cSC.n += 2

                        @block.vector
                        def _(v):
                            v.wait_ge(cSC.sem, cSC.n)
                            v.wait_ge(cZD.sem, cZD.n)
                            v.tensor_scalar(zz[:], vv[:], stats[:, 2:3], stats[:, 5:6],
                                            ALU.subtract, ALU.mult).then_inc(cWV.sem, 1)
                        cWV.n += 1

                        @block.sync
                        def _(sync, b=b, tt=tt):
                            sync.wait_ge(cWV.sem, cWV.n)
                            sync.dma_start(out=d_zscr[b, tt * 128:(tt + 1) * 128, :], in_=zz[:]).then_inc(cZD.sem, 16)
                        cZD.n += 16

                        for qq in range(4):
                            c0, c1 = qq * 512, (qq + 1) * 512

                            @block.vector
                            def _(v, c0=c0, c1=c1):
                                v.tensor_scalar(qd[:, QU], zz[:, c0:c1], cpp[:, C_RS:C_RS + 1], 448.0,
                                                ALU.mult, ALU.min)
                                v.tensor_single_scalar(qd[:, QU], qd[:, QU], -448.0, ALU.max)
                                v.tensor_tensor(qd[:, QW], qd[:, QU], qd[:, QU], ALU.mult).then_inc(cQA.sem, 1)
                            cQA.n += 1

                            @block.scalar
                            def _(s):
                                s.wait_ge(cQA.sem, cQA.n)
                                s.activation(qd[:, QW], qd[:, QW], AF.Ln, bias=cpp[:, C_C40:C_C40 + 1]).then_inc(cQB.sem, 1)
                            cQB.n += 1

                            @block.vector
                            def _(v):
                                v.wait_ge(cQB.sem, cQB.n)
                                v.tensor_scalar(qd[:, QE], qd[:, QW], LN2INV2, -3.5, ALU.mult, ALU.add)
                                v.tensor_scalar(qd[:, QA32], qd[:, QE], C32, -C32, ALU.add, ALU.add)
                                v.tensor_scalar(qd[:, QA52], qd[:, QE], C52, -C52, ALU.add, ALU.add)
                                v.tensor_tensor(qd[:, QA52], qd[:, QA52], qd[:, QA32], ALU.subtract)
                                v.scalar_tensor_tensor(qd[:, QE], qd[:, QA52], cpp[:, C_TAU:C_TAU + 1],
                                                       qd[:, QA32], ALU.mult, ALU.add)
                                v.tensor_single_scalar(qd[:, QE], qd[:, QE], -9.0, ALU.max).then_inc(cQA.sem, 1)
                            cQA.n += 1

                            @block.scalar
                            def _(s):
                                s.wait_ge(cQA.sem, cQA.n)
                                s.activation(qd[:, QQ], qd[:, QE], AF.Exp,
                                             bias=cpp[:, C_N2LN2:C_N2LN2 + 1], scale=cpp[:, C_LN2:C_LN2 + 1])
                                s.activation(qd[:, QQI], qd[:, QE], AF.Exp,
                                             scale=cpp[:, C_NLN2:C_NLN2 + 1]).then_inc(cQB.sem, 1)
                            cQB.n += 1

                            @block.vector
                            def _(v, c0=c0, c1=c1):
                                v.wait_ge(cQB.sem, cQB.n)
                                if c0 == 0:
                                    v.wait_ge(cQT.sem, cQT.n)
                                v.tensor_tensor(qd[:, QQI], qd[:, QU], qd[:, QQI], ALU.mult)
                                v.tensor_scalar(qd[:, QA32], qd[:, QQI], C32, -C32, ALU.add, ALU.add)
                                v.tensor_scalar(qd[:, QA52], qd[:, QQI], C52, -C52, ALU.add, ALU.add)
                                v.tensor_tensor(qd[:, QA52], qd[:, QA52], qd[:, QA32], ALU.subtract)
                                v.scalar_tensor_tensor(qd[:, QE], qd[:, QA52], cpp[:, C_TAU:C_TAU + 1],
                                                       qd[:, QA32], ALU.mult, ALU.add)
                                v.tensor_tensor(xqb[:, c0:c1], qd[:, QE], qd[:, QQ], ALU.mult).then_inc(cWV.sem, 1)
                            cWV.n += 1

                            @block.sync
                            def _(s, j=j, hf=hf, b=b, qq=qq, cWVt=cWV.n, cQ8t=cQ8.n):
                                s.wait_ge(cWV.sem, cWVt)
                                s.wait_ge(cQ8.sem, cQ8t)
                                if qq == 0 and j == 0 and (hf == 1 or b > 0):
                                    s.wait_ge(cFM.sem, cFM.n)
                                for ft in range(4 * qq, 4 * qq + 4):
                                    s.dma_start_transpose(xqs[:, ft * 128:(ft + 1) * 128],
                                                          xqb[:, ft * 128:(ft + 1) * 128]).then_inc(cQT.sem, 16)
                            cQT.n += 64

                        @block.vector
                        def _(v, j=j):
                            v.wait_ge(cQT.sem, cQT.n)
                            src = xqs[:].rearrange("p (f c) -> p f c", c=128)
                            dst = xqT8[:].rearrange("p (f c) -> p f c", c=1024)[:, :, j * 128:(j + 1) * 128]
                            v.tensor_copy(dst, src).then_inc(cQ8.sem, 1)
                        cQ8.n += 1

                        if DBG and b == 0 and hf == 0:
                            @block.sync
                            def _(sync, j=j):
                                sync.wait_ge(cWV.sem, cWV.n)
                                sync.dma_start(out=d_xqdbg[j], in_=xqb[:]).then_inc(cO.sem, 16)
                            cO.n += 16

                    if DBG and b == 0 and hf == 0:
                        @block.sync
                        def _(sync):
                            sync.wait_ge(cQ8.sem, cQ8.n)
                            sync.dma_start(out=d_xq8dbg, in_=xqT8[:]).then_inc(cO.sem, 16)
                            sync.dma_start(out=d_cppdbg, in_=cpp[:]).then_inc(cO.sem, 16)
                        cO.n += 32

                    Q8_HALF = cQ8.n
                    for j in range(8 if PHASES >= 5 else 0):
                        tt = hf * 8 + j

                        @block.sync
                        def _(sync, b=b, tt=tt):
                            sync.wait_ge(cWV.sem, cWV.n)
                            sync.dma_start(out=x2[:, 0:D],
                                           in_=d_zscr[b, tt * 128:(tt + 1) * 128, :]).then_inc(cZL.sem, 16)
                        cZL.n += 16

                        @block.tensor
                        def _(t, tt=tt, j=j, PSW_T=cPSW.n, PSF_T=cPSF.n):
                            t.wait_ge(cQ8.sem, Q8_HALF - 7 + j)
                            if j == 0:
                                t.wait_ge(cPSW.sem, PSW_T)
                            if j >= 2:
                                t.wait_ge(cPSF.sem, PSF_T - 1)
                            for ft in range(FT):
                                for n in range(4):
                                    ins = t.matmul(PS[:, (tt % 2) * 2048 + n * 512:(tt % 2) * 2048 + (n + 1) * 512],
                                                   xqT8[:, ft * 1024 + j * 128: ft * 1024 + j * 128 + 128],
                                                   w8[:, ft * D + n * 512: ft * D + (n + 1) * 512],
                                                   start=(ft == 0), stop=(ft == FT - 1))
                            ins.then_inc(cFM.sem, 1)
                        cFM.n += 1

                        @block.vector
                        def _(v, tt=tt):
                            v.wait_ge(cFM.sem, cFM.n)
                            v.wait_ge(cZL.sem, cZL.n)
                            v.wait_ge(cO.sem, cO.n)
                            v.wait_ge(cZD.sem, cZD.n)
                            pb = (tt % 2) * 2048
                            v.scalar_tensor_tensor(vv[:], PS[:, pb:pb + 2048], cpp[:, C_DEQ:C_DEQ + 1],
                                                   x2[:, 0:D], ALU.mult, ALU.add, accum_out=stats[:, 0:1]).then_inc(cPSF.sem, 1)
                            v.scalar_tensor_tensor(zz[:], vv[:], 1.0, vv[:], ALU.mult, ALU.mult,
                                                   accum_out=stats[:, 1:2])
                            v.tensor_scalar(stats[:, 2:3], stats[:, 0:1], 1.0 / 2048.0, None, ALU.mult)
                            v.tensor_scalar(stats[:, 3:4], stats[:, 1:2], 1.0 / 2048.0, None, ALU.mult).then_inc(cWV.sem, 1)
                            v.wait_ge(cWV.sem, cWV.n + 1)
                            v.tensor_tensor(stats[:, 4:5], stats[:, 2:3], stats[:, 2:3], ALU.mult).then_inc(cWV.sem, 1)
                            v.wait_ge(cWV.sem, cWV.n + 2)
                            v.tensor_tensor(stats[:, 3:4], stats[:, 3:4], stats[:, 4:5], ALU.subtract).then_inc(cWV.sem, 1)
                        cWV.n += 3
                        cPSF.n += 1

                        @block.scalar
                        def _(s):
                            s.wait_ge(cWV.sem, cWV.n)
                            s.activation(stats[:, 5:6], stats[:, 3:4], AF.Ln, bias=cpp[:, C_E5:C_E5 + 1]).then_inc(cSC.sem, 1)
                            s.wait_ge(cSC.sem, cSC.n + 1)
                            s.activation(stats[:, 5:6], stats[:, 5:6], AF.Exp, scale=-0.5).then_inc(cSC.sem, 1)
                        cSC.n += 2

                        @block.vector
                        def _(v):
                            v.wait_ge(cSC.sem, cSC.n)
                            v.tensor_scalar(ob[:], vv[:], stats[:, 2:3], stats[:, 5:6],
                                            ALU.subtract, ALU.mult).then_inc(cWV.sem, 1)
                        cWV.n += 1

                        @block.sync
                        def _(sync, b=b, tt=tt):
                            sync.wait_ge(cWV.sem, cWV.n)
                            sync.dma_start(out=d_out[b, tt * 128:(tt + 1) * 128, :], in_=ob[:]).then_inc(cO.sem, 16)
                        cO.n += 16

        @block.sync
        def _(sync):
            for ch in C.values():
                if ch.n > 0:
                    sync.wait_ge(ch.sem, ch.n)

    return nc


def _host_layouts(inputs):
    wq_b = np.ascontiguousarray(np.asarray(inputs["L0_attn_wq_b"], dtype=np.float32))
    wkv_a = np.ascontiguousarray(np.asarray(inputs["L0_attn_wkv_a"], dtype=np.float32))
    wkv_b = np.ascontiguousarray(np.asarray(inputs["L0_attn_wkv_b"], dtype=np.float32))
    perm = np.concatenate([np.arange(0, 64, 2), np.arange(1, 64, 2)])
    perm_r = np.concatenate([np.arange(1, 64, 2), np.arange(0, 64, 2)])
    wqbN = np.empty((64, NH * 64), np.float32)
    wqbX = np.empty((64, NH * 64), np.float32)
    wqbR = np.empty((64, NH * 64), np.float32)
    wkbN = np.empty((64, NH * 64), np.float32)
    wkbV = np.empty((64, NH * 64), np.float32)
    for h in range(NH):
        base = h * 128
        wqbN[:, h * 64:(h + 1) * 64] = wq_b[:, base:base + 64]
        wqbX[:, h * 64:(h + 1) * 64] = wq_b[:, base + 64 + perm]
        wqbR[:, h * 64:(h + 1) * 64] = wq_b[:, base + 64 + perm_r]
        wkbN[:, h * 64:(h + 1) * 64] = wkv_b[:, base:base + 64]
        wkbV[:, h * 64:(h + 1) * 64] = wkv_b[:, base + 64:base + 128]
    wkva = np.empty((D, 128), np.float32)
    wkva[:, 0:64] = wkv_a[:, 0:64]
    wkva[:, 64:128] = wkv_a[:, 64 + perm]
    w8 = np.asarray(inputs["L0_ffn_weight"])
    w8T = np.ascontiguousarray(w8.view(np.uint8).T).view(ml_dtypes.float8_e4m3)
    return wqbN, wqbX, wqbR, wkbN, wkbV, wkva, w8T


_NC_CACHE = {}


def kernel(**inputs):
    x = np.ascontiguousarray(np.asarray(inputs["L0_attn_hidden_states"], dtype=np.float32))
    wqbN, wqbX, wqbR, wkbN, wkbV, wkva, w8T = _host_layouts(inputs)
    cos4, sin4 = _yarn_tables()
    wqa = np.ascontiguousarray(np.asarray(inputs["L0_attn_wq_a"], dtype=np.float32))
    qnw = np.ascontiguousarray(np.asarray(inputs["L0_attn_q_a_norm_w"], dtype=np.float32).reshape(64, 1))
    knw = np.ascontiguousarray(np.asarray(inputs["L0_attn_kv_a_norm_w"], dtype=np.float32).reshape(64, 1))
    wo = np.ascontiguousarray(np.asarray(inputs["L0_attn_wo"], dtype=np.float32))
    sw = np.ascontiguousarray(np.repeat(np.asarray(inputs["L0_ffn_weight_scale"], dtype=np.float32).reshape(1, 1), 128, axis=0))
    s_in = np.asarray(inputs["L0_ffn_input_scale"]).reshape(1, 1)
    if s_in.dtype == np.uint16:
        s_in = s_in.view(ml_dtypes.bfloat16)
    assert s_in.dtype == ml_dtypes.bfloat16, s_in.dtype
    s_in = np.ascontiguousarray(np.repeat(s_in, 128, axis=0))

    if "nc" not in _NC_CACHE:
        _NC_CACHE["nc"] = build_nc()
    nc = _NC_CACHE["nc"]

    common = {
        "wqa": wqa, "wqbN": wqbN, "wqbX": wqbX, "wqbR": wqbR,
        "wkva": wkva, "wkbN": wkbN, "wkbV": wkbV,
        "qnw": qnw, "knw": knw, "wo": wo, "w8T": w8T,
        "sw": sw, "sin_": s_in, "cos4": cos4, "sin4": sin4,
    }
    in_maps = []
    for c in range(NCORES):
        m = dict(common)
        m["x"] = np.ascontiguousarray(x[c * NB:(c + 1) * NB])
        in_maps.append(m)

    res = run_bass_kernel_spmd(nc, in_maps, core_ids=list(range(NCORES)))
    out = np.concatenate([np.asarray(r["out"]) for r in res.results], axis=0)
    return out.astype(np.float32, copy=False)


# version 13
# speedup vs baseline: 1.1033x; 1.0038x over previous
import math
import numpy as np
import ml_dtypes
from contextlib import ExitStack

import concourse.bass as bass
import concourse.mybir as mybir
from concourse.bass_utils import run_bass_kernel_spmd

F32 = mybir.dt.float32
BF16 = mybir.dt.bfloat16
FP8 = mybir.dt.float8e4
AF = mybir.ActivationFunctionType
ALU = mybir.AluOpType

NCORES = 8
NB = 2
S = 2048
D = 2048
NH = 16
NP = 8
TT = 16
FT = 16

C32 = float(np.float32(1.5 * 2.0 ** 23))
C52 = float(np.float32(1.5 * 2.0 ** 52))
LN2INV2 = float(np.float32(1.0 / (2.0 * math.log(2.0))))
SM_SCALE = float(np.float32(128.0 ** -0.5))


def _yarn_tables():
    ROPE_BASE, FACTOR = 10000.0, 40.0
    BETA_FAST, BETA_SLOW, ORIG_MAX, MSCALE = 32.0, 1.0, 4096, 0.707
    dim = 64
    exponents = np.arange(0, dim, 2, dtype=np.float64) / dim
    freq_extra = 1.0 / ROPE_BASE ** exponents
    freq_inter = 1.0 / (FACTOR * ROPE_BASE ** exponents)

    def corr_dim(num_rot):
        return dim * math.log(ORIG_MAX / (num_rot * 2 * math.pi)) / (2 * math.log(ROPE_BASE))

    low = max(math.floor(corr_dim(BETA_FAST)), 0)
    high = min(math.ceil(corr_dim(BETA_SLOW)), dim - 1)
    if low == high:
        high = high + 0.001
    ramp = np.clip((np.arange(dim // 2, dtype=np.float64) - low) / (high - low), 0.0, 1.0)
    inv_freq = freq_inter * ramp + freq_extra * (1.0 - ramp)
    t = np.arange(S, dtype=np.float64)
    freqs = np.outer(t, inv_freq)

    def yarn_get_mscale(scale, mscale=1.0):
        if scale <= 1:
            return 1.0
        return 0.1 * mscale * math.log(scale) + 1.0

    msc = yarn_get_mscale(FACTOR, MSCALE) / yarn_get_mscale(FACTOR, 0.0)
    cosF = (np.cos(freqs) * msc).astype(np.float32).T
    sinF = (np.sin(freqs) * msc).astype(np.float32).T
    return (np.ascontiguousarray(np.tile(cosF, (4, 1))),
            np.ascontiguousarray(np.tile(sinF, (4, 1))))


class Ch:
    def __init__(self, stack, nc, name):
        self.sem = stack.enter_context(nc.semaphore(name))
        self.n = 0


PHASES = 99
E_STEPS = 99
E_SUB = 99
E_GEN = 99
DBG = False


def build_nc():
    nc = bass.Bass("TRN2", target_bir_lowering=False, detect_race_conditions=False)

    d_x = nc.dram_tensor("x", [NB, S, D], F32, kind="ExternalInput").ap()
    d_wqa = nc.dram_tensor("wqa", [D, 64], F32, kind="ExternalInput").ap()
    d_wqbN = nc.dram_tensor("wqbN", [64, NH * 64], F32, kind="ExternalInput").ap()
    d_wqbX = nc.dram_tensor("wqbX", [64, NH * 64], F32, kind="ExternalInput").ap()
    d_wqbR = nc.dram_tensor("wqbR", [64, NH * 64], F32, kind="ExternalInput").ap()
    d_wkva = nc.dram_tensor("wkva", [D, 128], F32, kind="ExternalInput").ap()
    d_wkbN = nc.dram_tensor("wkbN", [64, NH * 64], F32, kind="ExternalInput").ap()
    d_wkbV = nc.dram_tensor("wkbV", [64, NH * 64], F32, kind="ExternalInput").ap()
    d_qnw = nc.dram_tensor("qnw", [64, 1], F32, kind="ExternalInput").ap()
    d_knw = nc.dram_tensor("knw", [64, 1], F32, kind="ExternalInput").ap()
    d_wo = nc.dram_tensor("wo", [NH * 64, D], F32, kind="ExternalInput").ap()
    d_w8T = nc.dram_tensor("w8T", [D, D], FP8, kind="ExternalInput").ap()
    d_sw = nc.dram_tensor("sw", [128, 1], F32, kind="ExternalInput").ap()
    d_sin = nc.dram_tensor("sin_", [128, 1], BF16, kind="ExternalInput").ap()
    d_cos4 = nc.dram_tensor("cos4", [128, S], F32, kind="ExternalInput").ap()
    d_sin4 = nc.dram_tensor("sin4", [128, S], F32, kind="ExternalInput").ap()
    d_out = nc.dram_tensor("out", [NB, S, D], F32, kind="ExternalOutput").ap()
    scrk = "ExternalOutput" if DBG else "Internal"
    d_xqdbg = nc.dram_tensor("xqdbg", [8, 128, D], BF16, kind="ExternalOutput").ap() if DBG else None
    d_cppdbg = nc.dram_tensor("cppdbg", [128, 16], F32, kind="ExternalOutput").ap() if DBG else None
    d_xq8dbg = nc.dram_tensor("xq8dbg", [128, FT * 1024], FP8, kind="ExternalOutput").ap() if DBG else None
    d_zscr = nc.dram_tensor("zscr", [NB, S, D], F32, kind=scrk).ap()
    d_qscr = nc.dram_tensor("qscr", [NH, 128, S], BF16, kind=scrk).ap()
    d_kscr = nc.dram_tensor("kscr", [NH, 128, S], BF16, kind=scrk).ap()

    with ExitStack() as ctx:
        def sbt(name, shape, dt=F32):
            return ctx.enter_context(nc.sbuf_tensor(name, shape, dt))

        w8 = sbt("w8", [128, FT * D], FP8)
        cos4 = sbt("cos4_b", [128, S], BF16)
        sin4 = sbt("sin4_b", [128, S], BF16)
        wqa_b = sbt("wqa_b", [128, FT * 64], BF16)
        wkva_b = sbt("wkva_b", [128, FT * 128], BF16)
        wqbN_b = sbt("wqbN_b", [64, NH * 64], BF16)
        wqbX_b = sbt("wqbX_b", [64, NH * 64], BF16)
        wqbR_b = sbt("wqbR_b", [64, NH * 64], BF16)
        wkbN_b = sbt("wkbN_b", [64, NH * 64], BF16)
        wkbV_b = sbt("wkbV_b", [64, NH * 64], BF16)
        ones_b = sbt("ones_b", [64, 128], BF16)
        cpp = sbt("cpp", [128, 16], F32)
        qnw_s = sbt("qnw_s", [64, 1], F32)
        knw_s = sbt("knw_s", [64, 1], F32)
        C_TAU, C_LN2, C_NLN2, C_N2LN2, C_RS, C_DEQ, C_E6, C_E5, C_C40, C_TWO = range(10)

        bigA = sbt("bigA", [128, 16384], BF16)
        stats = sbt("stats", [128, 8], F32)

        PS = ctx.enter_context(nc.psum_tensor("PS", [128, 4096], F32))

        chnames = ["cLD", "cSET", "cXa", "cXb", "cTRa", "cTRb", "cTM", "cSC", "cVE", "cQKD",
                   "cQKLa", "cQKLb", "cQKF", "cL", "cE2", "cAV", "cEP", "cWM", "cX2a", "cX2b",
                   "cZD", "cZL", "cQT", "cQ8", "cFM", "cO", "cWV", "cQA", "cQB", "cWL", "cPSW", "cPSF"]
        C = {n: Ch(ctx, nc, n) for n in chnames}
        (cLD, cSET, cXa, cXb, cTRa, cTRb, cTM, cSC, cVE, cQKD, cQKLa, cQKLb, cQKF,
         cL, cE2, cAV, cEP, cWM, cX2a, cX2b, cZD, cZL, cQT, cQ8, cFM, cO, cWV,
         cQA, cQB, cWL, cPSW, cPSF) = (C[n] for n in chnames)

        block = ctx.enter_context(nc.Block())

        @block.scalar
        def _(s):
            from concourse.hw_specs import get_activation_tables
            for idx, funcs in enumerate(get_activation_tables(nc.m.arch).values()):
                if AF.Ln in funcs and AF.Exp in funcs:
                    s.add_instruction(mybir.InstLoadActFuncSet(
                        name=nc.get_next_instruction_name(),
                        act_func_set_id=idx, ins=[], outs=[]))
                    break

        @block.sync
        def _(sync):
            sync.dma_start(out=w8[:].rearrange("p (t c) -> p t c", c=D),
                           in_=d_w8T.rearrange("(t p) c -> p t c", p=128)).then_inc(cLD.sem, 16)
        cLD.n += 16

        with nc.sbuf_tensor("stage32", [128, S], F32) as stage32, \
             nc.sbuf_tensor("sin_b", [128, 1], BF16) as sin_b:

            stage_jobs = [
                (d_cos4, cos4, 128, S),
                (d_sin4, sin4, 128, S),
                (d_wqa.rearrange("(t p) c -> p t c", p=128), wqa_b, 128, FT * 64),
                (d_wkva.rearrange("(t p) c -> p t c", p=128), wkva_b, 128, FT * 128),
                (d_wqbN, wqbN_b, 64, NH * 64),
                (d_wqbX, wqbX_b, 64, NH * 64),
                (d_wkbN, wkbN_b, 64, NH * 64),
                (d_wkbV, wkbV_b, 64, NH * 64),
            ]
            for src, dst, p, width in stage_jobs:
                @block.sync
                def _(sync, src=src, p=p, width=width):
                    sync.wait_ge(cSET.sem, cSET.n)
                    o = stage32[0:p, 0:width]
                    if len(src.shape) == 3:
                        o = o.rearrange("p (t c) -> p t c", t=src.shape[1])
                    sync.dma_start(out=o, in_=src).then_inc(cLD.sem, 16)
                cLD.n += 16

                @block.vector
                def _(v, dst=dst, p=p, width=width):
                    v.wait_ge(cLD.sem, cLD.n)
                    v.tensor_copy(dst[0:p, 0:width], stage32[0:p, 0:width]).then_inc(cSET.sem, 1)
                cSET.n += 1

            @block.sync
            def _(sync):
                sync.wait_ge(cSET.sem, cSET.n)
                sync.dma_start(out=stage32[0:64, 0:NH * 64], in_=d_wqbR).then_inc(cLD.sem, 16)
            cLD.n += 16

            @block.vector
            def _(v):
                v.wait_ge(cLD.sem, cLD.n)
                si = stage32[0:64, 0:NH * 64].rearrange("p (h c) -> p h c", c=64)
                so = wqbR_b[0:64, :].rearrange("p (h c) -> p h c", c=64)
                v.tensor_scalar(so[:, :, 0:32], si[:, :, 0:32], -1.0, None, ALU.mult)
                v.tensor_copy(so[:, :, 32:64], si[:, :, 32:64]).then_inc(cSET.sem, 1)
            cSET.n += 1

            @block.sync
            def _(sync):
                sync.wait_ge(cSET.sem, cSET.n)
                sync.dma_start(out=cpp[:, C_RS:C_RS + 1], in_=d_sw).then_inc(cLD.sem, 16)
                sync.dma_start(out=sin_b[:], in_=d_sin).then_inc(cLD.sem, 16)
                sync.dma_start(out=qnw_s[:], in_=d_qnw).then_inc(cLD.sem, 16)
                sync.dma_start(out=knw_s[:], in_=d_knw).then_inc(cLD.sem, 16)
            cLD.n += 64

            @block.vector
            def _(v):
                v.memset(cpp[:, 10:16], 0.0)
                v.wait_ge(cLD.sem, cLD.n)
                v.memset(cpp[:, C_TAU:C_TAU + 1], 2.0 ** 52).then_inc(cSET.sem, 1)
                v.memset(cpp[:, C_TWO:C_TWO + 1], 2.0)
                v.memset(cpp[:, C_E6:C_E6 + 1], 1e-6)
                v.memset(cpp[:, C_E5:C_E5 + 1], 1e-5)
                v.memset(cpp[:, C_C40:C_C40 + 1], 2.0 ** -40)
                v.tensor_copy(cpp[:, C_DEQ:C_DEQ + 1], sin_b[:]).then_inc(cSET.sem, 1)
                v.wait_ge(cSET.sem, cSET.n + 2)
                v.tensor_scalar(cpp[:, C_TAU:C_TAU + 1], cpp[:, C_TAU:C_TAU + 1], 1.0, -(2.0 ** 52), ALU.add, ALU.add)
                v.tensor_tensor(cpp[:, C_DEQ:C_DEQ + 1], cpp[:, C_DEQ:C_DEQ + 1], cpp[:, C_RS:C_RS + 1], ALU.mult).then_inc(cSET.sem, 1)
                v.wait_ge(cSET.sem, cSET.n + 3)
                v.tensor_scalar(cpp[:, C_DEQ:C_DEQ + 1], cpp[:, C_DEQ:C_DEQ + 1], 4.0, None, ALU.mult)
                v.tensor_copy(cpp[:, C_RS:C_RS + 1], sin_b[:]).then_inc(cSET.sem, 1)
                v.wait_ge(cSET.sem, cSET.n + 4)
                v.reciprocal(cpp[:, C_RS:C_RS + 1], cpp[:, C_RS:C_RS + 1])
                v.tensor_scalar(qnw_s[:], qnw_s[:], SM_SCALE, None, ALU.mult)
                v.memset(ones_b[:], 1.0).then_inc(cSET.sem, 1)
            cSET.n += 5

            @block.scalar
            def _(s):
                s.wait_ge(cSET.sem, cSET.n)
                s.activation(cpp[:, C_LN2:C_LN2 + 1], cpp[:, C_TWO:C_TWO + 1], AF.Ln)
                s.mul(cpp[:, C_NLN2:C_NLN2 + 1], cpp[:, C_LN2:C_LN2 + 1], -1.0)
                s.mul(cpp[:, C_N2LN2:C_N2LN2 + 1], cpp[:, C_LN2:C_LN2 + 1], -2.0).then_inc(cSET.sem, 1)
            cSET.n += 1

        SETUP_DONE = cSET.n

        for b in range(NB if PHASES >= 6 else (1 if PHASES >= 2 else 0)):
            with ExitStack() as ea:
                def eat(name, shape, dt):
                    return ea.enter_context(nc.sbuf_tensor(name + str(b), shape, dt))
                vbuf = eat("vbuf", [128, TT * 1536], BF16)
                qp = [eat("qp0", [128, S], BF16), eat("qp1", [128, S], BF16)]
                kp = [eat("kp0", [128, S], BF16), eat("kp1", [128, S], BF16)]
                a2 = [eat(f"a2_{i}", [128, 1024], BF16) for i in range(3)]
                rcp_full = eat("rcp", [64, S], F32)
                rcp = rcp_full[0:64, 0:1024]
                qlat_n = eat("qlat_n", [64, S], BF16)
                kv_n = eat("kv_n", [64, S], BF16)
                kpe_r = eat("kpe_r", [64, S], BF16)

                with ExitStack() as ee:
                    def eet(name, shape, dt):
                        return ee.enter_context(nc.sbuf_tensor(name + str(b), shape, dt))
                    xbf = eet("xbf", [128, 2 * D], BF16)
                    kv_bf = eet("kv_bf", [64, S], BF16)
                    ql_bf = eet("ql_bf", [64, S], BF16)
                    kpe_bf = eet("kpe_bf", [64, S], BF16)
                    krot_bf = eet("krot_bf", [64, S], BF16)
                    tln = rcp_full
                    t1 = eet("t1", [128, 512], F32)
                    t2 = eet("t2", [128, 512], F32)
                    sq_bf = krot_bf

                    for hf in range(2):
                        for j in range(8):
                            tt = hf * 8 + j

                            cXp = cXa if tt % 2 == 0 else cXb
                            cTRp = cTRa if tt % 2 == 0 else cTRb

                            @block.gpsimd
                            def _(g, b=b, tt=tt, j=j, hf=hf, cXp=cXp, cTRp=cTRp):
                                if b > 0 and hf == 0 and j == 0:
                                    g.wait_ge(cO.sem, cO.n)
                                    g.wait_ge(cWV.sem, cWV.n)
                                    g.wait_ge(cSC.sem, cSC.n)
                                    g.wait_ge(cVE.sem, cVE.n)
                                if tt >= 2:
                                    g.wait_ge(cTRp.sem, cTRp.n)
                                g.dma_start(out=xbf[:, (j % 2) * D:(j % 2 + 1) * D],
                                            in_=d_x[b, tt * 128:(tt + 1) * 128, :]).then_inc(cXp.sem, 16)
                            cXp.n += 16

                            def mk_tr(fts):
                                def sec(s, j=j, b=b, hf=hf, cXp=cXp, cTRp=cTRp, fts=fts):
                                    s.wait_ge(cXp.sem, cXp.n)
                                    if j == 0:
                                        s.wait_ge(cTM.sem, cTM.n)
                                        s.wait_ge(cWM.sem, cWM.n)
                                    for ft in fts:
                                        s.dma_start_transpose(
                                            bigA[:, ft * 1024 + j * 128: ft * 1024 + (j + 1) * 128],
                                            xbf[:, (j % 2) * D + ft * 128:(j % 2) * D + (ft + 1) * 128]
                                        ).then_inc(cTRp.sem, 16)
                                return sec
                            block.sync(mk_tr(range(0, FT)))
                            cTRp.n += 256

                        @block.tensor
                        def _(t, hf=hf):
                            if E_STEPS < 2:
                                return
                            t.wait_ge(cTRa.sem, cTRa.n)
                            t.wait_ge(cTRb.sem, cTRb.n)
                            if hf == 0:
                                t.wait_ge(cSET.sem, SETUP_DONE)
                                t.wait_ge(cSC.sem, cSC.n)
                                t.wait_ge(cWV.sem, cWV.n)
                            for n in range(2):
                                for ft in range(FT):
                                    t.matmul(PS[:, hf * 1024 + n * 512: hf * 1024 + (n + 1) * 512],
                                             wkva_b[:, ft * 128:(ft + 1) * 128],
                                             bigA[:, ft * 1024 + n * 512: ft * 1024 + (n + 1) * 512],
                                             start=(ft == 0), stop=(ft == FT - 1))
                            for n in range(2):
                                for ft in range(FT):
                                    ins = t.matmul(PS[0:64, 2048 + hf * 1024 + n * 512: 2048 + hf * 1024 + (n + 1) * 512],
                                                   wqa_b[:, ft * 64:(ft + 1) * 64],
                                                   bigA[:, ft * 1024 + n * 512: ft * 1024 + (n + 1) * 512],
                                                   start=(ft == 0), stop=(ft == FT - 1))
                            ins.then_inc(cTM.sem, 1)
                        if E_STEPS >= 2:
                            cTM.n += 1
                    if E_STEPS < 3:
                        continue

                    @block.scalar
                    def _(s):
                        s.wait_ge(cTM.sem, cTM.n)
                        s.copy(kv_bf[:], PS[0:64, 0:2048])
                        s.copy(kpe_bf[:], PS[64:128, 0:2048])
                        s.copy(ql_bf[:], PS[0:64, 2048:4096])
                        s.mul(krot_bf[0:32, :], kpe_bf[32:64, :], -1.0)
                        s.copy(krot_bf[32:64, :], kpe_bf[0:32, :]).then_inc(cSC.sem, 1)
                    cSC.n += 1
                    if E_STEPS < 4:
                        continue

                    @block.vector
                    def _(v):
                        v.wait_ge(cSC.sem, cSC.n)
                        if cSET.n == SETUP_DONE:
                            v.wait_ge(cSET.sem, SETUP_DONE)
                        for n in range(4):
                            sl = slice(n * 512, (n + 1) * 512)
                            v.tensor_tensor(t1[0:64, :], kpe_bf[:, sl], cos4[0:64, sl], ALU.mult)
                            v.tensor_tensor(t2[0:64, :], krot_bf[:, sl], sin4[0:64, sl], ALU.mult)
                            ins = v.tensor_tensor(kpe_r[:, sl], t1[0:64, :], t2[0:64, :], ALU.add)
                        ins.then_inc(cVE.sem, 1)
                    cVE.n += 1

                    if E_SUB < 2:
                        continue

                    @block.vector
                    def _(v):
                        v.tensor_tensor(sq_bf[:], ql_bf[:], ql_bf[:], ALU.mult).then_inc(cVE.sem, 1)
                    cVE.n += 1
                    if E_SUB < 3:
                        continue

                    @block.tensor
                    def _(t):
                        t.wait_ge(cVE.sem, cVE.n)
                        for n in range(4):
                            ins = t.matmul(PS[:, n * 512:(n + 1) * 512], ones_b[:],
                                           sq_bf[:, n * 512:(n + 1) * 512], start=True, stop=True)
                        ins.then_inc(cTM.sem, 1)
                    cTM.n += 1

                    if E_SUB >= 4:
                        @block.scalar
                        def _(s):
                            s.wait_ge(cTM.sem, cTM.n)
                            s.activation(tln[:], PS[0:64, 0:2048], AF.Ln, bias=cpp[0:64, C_E6:C_E6 + 1], scale=1.0 / 64.0)
                            s.activation(tln[:], tln[:], AF.Exp, scale=-0.5).then_inc(cSC.sem, 1)
                        cSC.n += 1
                    if E_SUB < 5:
                        continue

                    @block.vector
                    def _(v):
                        v.wait_ge(cSC.sem, cSC.n)
                        v.tensor_tensor(tln[:], ql_bf[:], tln[:], ALU.mult)
                        v.tensor_scalar(qlat_n[:], tln[:], qnw_s[:], None, ALU.mult)
                        v.tensor_tensor(sq_bf[:], kv_bf[:], kv_bf[:], ALU.mult).then_inc(cVE.sem, 1)
                    cVE.n += 1
                    if E_SUB < 6:
                        continue

                    @block.tensor
                    def _(t):
                        t.wait_ge(cVE.sem, cVE.n)
                        for n in range(4):
                            ins = t.matmul(PS[:, 2048 + n * 512: 2048 + (n + 1) * 512], ones_b[:],
                                           sq_bf[:, n * 512:(n + 1) * 512], start=True, stop=True)
                        ins.then_inc(cTM.sem, 1)
                    cTM.n += 1

                    @block.scalar
                    def _(s):
                        s.wait_ge(cTM.sem, cTM.n)
                        s.activation(tln[:], PS[0:64, 2048:4096], AF.Ln, bias=cpp[0:64, C_E6:C_E6 + 1], scale=1.0 / 64.0)
                        s.activation(tln[:], tln[:], AF.Exp, scale=-0.5).then_inc(cSC.sem, 1)
                    cSC.n += 1

                    @block.vector
                    def _(v):
                        v.wait_ge(cSC.sem, cSC.n)
                        v.tensor_tensor(tln[:], kv_bf[:], tln[:], ALU.mult)
                        v.tensor_scalar(kv_n[:], tln[:], knw_s[:], None, ALU.mult).then_inc(cVE.sem, 1)
                    cVE.n += 1
                    if E_STEPS < 5:
                        continue

                    for tt in range(TT):
                        @block.tensor
                        def _(t, tt=tt):
                            t.wait_ge(cVE.sem, cVE.n)
                            t.wait_ge(cSC.sem, cSC.n)
                            for n in range(2):
                                ins = t.matmul(PS[:, (tt % 2) * 1024 + n * 512:(tt % 2) * 1024 + (n + 1) * 512],
                                               kv_n[:, tt * 128:(tt + 1) * 128],
                                               wkbV_b[:, n * 512:(n + 1) * 512], start=True, stop=True)
                            ins.then_inc(cTM.sem, 1)
                        cTM.n += 1

                        @block.scalar
                        def _(s, tt=tt, b=b):
                            s.wait_ge(cTM.sem, cTM.n)
                            if tt == 0 and b > 0:
                                s.wait_ge(cAV.sem, cAV.n)
                            src = PS[:, (tt % 2) * 1024:(tt % 2) * 1024 + 1024].rearrange(
                                "p (pr j c) -> p pr j c", pr=8, j=2, c=64)
                            dst = vbuf[:, tt * 1536:(tt + 1) * 1536].rearrange(
                                "p (pr g c) -> p pr g c", pr=8, g=3, c=64)[:, :, 0:3:2, :]
                            s.copy(dst, src).then_inc(cSC.sem, 1)
                        cSC.n += 1

                    if E_STEPS < 6:
                        continue

                    @block.vector
                    def _(v, b=b):
                        if b > 0:
                            v.wait_ge(cAV.sem, cAV.n)
                        ones_ap = vbuf[:].rearrange("p (t pr g c) -> p t pr g c", pr=8, g=3, c=64)[:, :, :, 1:2, :]
                        v.memset(ones_ap, 1.0).then_inc(cVE.sem, 1)
                    cVE.n += 1

                    for p in range(NP if E_STEPS >= 7 else 0):
                        @block.tensor
                        def _(t, p=p):
                            t.wait_ge(cSC.sem, cSC.n)
                            t.wait_ge(cVE.sem, cVE.n)
                            for n in range(4):
                                t.matmul(PS[:, n * 512:(n + 1) * 512], wqbX_b[:, p * 128:(p + 1) * 128],
                                         qlat_n[:, n * 512:(n + 1) * 512], start=True, stop=True)
                            for n in range(4):
                                ins = t.matmul(PS[:, 2048 + n * 512: 2048 + (n + 1) * 512], wqbR_b[:, p * 128:(p + 1) * 128],
                                               qlat_n[:, n * 512:(n + 1) * 512], start=True, stop=True)
                            ins.then_inc(cTM.sem, 1)
                        cTM.n += 1

                        if E_GEN < 2:
                            continue

                        @block.vector
                        def _(v, p=p):
                            v.wait_ge(cTM.sem, cTM.n)
                            if p > 0:
                                v.wait_ge(cQKD.sem, cQKD.n)
                            for n in range(4):
                                sl = slice(n * 512, (n + 1) * 512)
                                v.tensor_tensor(t1[:], PS[:, sl], cos4[:, sl], ALU.mult)
                                ins = v.tensor_tensor(t2[:], PS[:, 2048 + n * 512: 2048 + (n + 1) * 512],
                                                      sin4[:, sl], ALU.mult)
                                if n == 3:
                                    ins.then_inc(cVE.sem, 1)
                                v.tensor_tensor(qp[0][0:64, sl], t1[0:64, :], t2[0:64, :], ALU.add)
                                v.tensor_tensor(qp[1][64:128, sl], t1[64:128, :], t2[64:128, :], ALU.add)
                            v.tensor_copy(kp[0][0:64, :], kpe_r[:])
                            v.tensor_copy(kp[1][64:128, :], kpe_r[:]).then_inc(cVE.sem, 1)
                        cVE.n += 2

                        if E_GEN < 3:
                            continue

                        @block.tensor
                        def _(t, p=p):
                            t.wait_ge(cVE.sem, cVE.n)
                            for n in range(4):
                                t.matmul(PS[:, n * 512:(n + 1) * 512], wqbN_b[:, p * 128:(p + 1) * 128],
                                         qlat_n[:, n * 512:(n + 1) * 512], start=True, stop=True)
                            for n in range(4):
                                ins = t.matmul(PS[:, 2048 + n * 512: 2048 + (n + 1) * 512], wkbN_b[:, p * 128:(p + 1) * 128],
                                               kv_n[:, n * 512:(n + 1) * 512], start=True, stop=True)
                            ins.then_inc(cTM.sem, 1)
                        cTM.n += 1

                        if E_GEN < 4:
                            continue

                        @block.scalar
                        def _(s, p=p):
                            s.wait_ge(cTM.sem, cTM.n)
                            if p > 0:
                                s.wait_ge(cQKD.sem, cQKD.n)
                            s.copy(qp[0][64:128, :], PS[0:64, 0:2048])
                            s.copy(qp[1][0:64, :], PS[64:128, 0:2048])
                            s.copy(kp[0][64:128, :], PS[0:64, 2048:4096])
                            s.copy(kp[1][0:64, :], PS[64:128, 2048:4096]).then_inc(cSC.sem, 1)
                        cSC.n += 1

                        if E_GEN < 5:
                            continue

                        @block.sync
                        def _(sync, p=p):
                            sync.wait_ge(cSC.sem, cSC.n)
                            sync.wait_ge(cVE.sem, cVE.n)
                            sync.dma_start(out=d_qscr[2 * p], in_=qp[0][:]).then_inc(cQKD.sem, 16)
                            sync.dma_start(out=d_qscr[2 * p + 1], in_=qp[1][:]).then_inc(cQKD.sem, 16)
                            sync.dma_start(out=d_kscr[2 * p], in_=kp[0][:]).then_inc(cQKD.sem, 16)
                            sync.dma_start(out=d_kscr[2 * p + 1], in_=kp[1][:]).then_inc(cQKD.sem, 16)
                        cQKD.n += 64

                EXP_BASE, AV_BASE, EP_BASE, L_BASE = cE2.n, cAV.n, cEP.n, cL.n

                def emit_av(h, kt, g, hq):
                    @block.tensor
                    def _(t, h=h, kt=kt, g=g, hq=hq):
                        t.wait_ge(cE2.sem, EXP_BASE + g + 1)
                        if kt == 0 and hq >= 2:
                            t.wait_ge(cEP.sem, EP_BASE + hq - 1)
                        voff = kt * 1536 + (h // 2) * 192 + (h % 2) * 64
                        for n in range(2):
                            ins = t.matmul(PS[:, 2048 + (hq % 2) * 1024 + n * 512:
                                               2048 + (hq % 2) * 1024 + (n + 1) * 512],
                                           vbuf[:, voff:voff + 128],
                                           a2[g % 3][:, n * 512:(n + 1) * 512],
                                           start=(kt == 0), stop=(kt == 15))
                        ins.then_inc(cAV.sem, 1)
                    cAV.n += 1
                    if kt == 15:
                        @block.vector
                        def _(v, h=h, qc=hq % 2, hq=hq):
                            v.wait_ge(cAV.sem, AV_BASE + (hq + 1) * 16)
                            ob_ = 2048 + (hq % 2) * 1024
                            dst_c = (h // 2) * 2048 + qc * 1024
                            if h % 2 == 0:
                                v.reciprocal(rcp, PS[64:128, ob_:ob_ + 1024])
                                v.tensor_tensor(bigA[0:64, dst_c:dst_c + 1024],
                                                PS[0:64, ob_:ob_ + 1024], rcp, ALU.mult).then_inc(cEP.sem, 1)
                            else:
                                v.reciprocal(rcp, PS[0:64, ob_:ob_ + 1024])
                                v.tensor_tensor(bigA[64:128, dst_c:dst_c + 1024],
                                                PS[64:128, ob_:ob_ + 1024], rcp, ALU.mult).then_inc(cEP.sem, 1)
                        cEP.n += 1

                pend_av = None
                for h in range(NH if PHASES >= 3 else 0):
                    cQKLp = cQKLa if h % 2 == 0 else cQKLb

                    @block.sync
                    def _(sync, h=h, cQKLp=cQKLp):
                        if h < 2:
                            sync.wait_ge(cQKD.sem, cQKD.n)
                        else:
                            sync.wait_ge(cL.sem, L_BASE + (h - 1) * 32)
                        sync.dma_start(out=qp[h % 2][:], in_=d_qscr[h]).then_inc(cQKLp.sem, 16)
                        sync.dma_start(out=kp[h % 2][:], in_=d_kscr[h]).then_inc(cQKLp.sem, 16)
                    cQKLp.n += 32
                    QKL_TGT = cQKLp.n

                    for qc in range(2):
                        hq = h * 2 + qc
                        for kt in range(16):
                            g = hq * 16 + kt

                            @block.tensor
                            def _(t, h=h, qc=qc, kt=kt, g=g, cQKLp=cQKLp, QKL_TGT=QKL_TGT):
                                if kt == 0 and qc == 0:
                                    t.wait_ge(cQKLp.sem, QKL_TGT)
                                if g >= 2:
                                    t.wait_ge(cE2.sem, EXP_BASE + g - 1)
                                for n in range(2):
                                    ins = t.matmul(PS[:, (g % 2) * 1024 + n * 512:(g % 2) * 1024 + (n + 1) * 512],
                                                   kp[h % 2][:, kt * 128:(kt + 1) * 128],
                                                   qp[h % 2][:, qc * 1024 + n * 512: qc * 1024 + (n + 1) * 512],
                                                   start=True, stop=True)
                                ins.then_inc(cL.sem, 1)
                            cL.n += 1

                            @block.scalar
                            def _(s, g=g):
                                s.wait_ge(cL.sem, L_BASE + g + 1)
                                if g >= 3:
                                    s.wait_ge(cAV.sem, AV_BASE + g - 2)
                                s.activation(a2[g % 3][:], PS[:, (g % 2) * 1024:(g % 2) * 1024 + 1024],
                                             AF.Exp).then_inc(cE2.sem, 1)
                            cE2.n += 1

                            if pend_av is not None:
                                emit_av(*pend_av)
                            pend_av = (h, kt, g, hq)

                if pend_av is not None:
                    emit_av(*pend_av)
                    pend_av = None

                ATT_DONE_EP = cEP.n
                if PHASES < 4:
                    continue

            with ExitStack() as wf:
                def wft(name, shape, dt):
                    return wf.enter_context(nc.sbuf_tensor(name + str(b), shape, dt))
                wo_bf = wft("wo_bf", [128, 8 * D], BF16)
                x2 = wft("x2", [128, 2 * D], F32)
                vv = wft("vv", [128, D], F32)
                zz = wft("zz", [128, D], F32)
                ob = wft("ob", [128, D], F32)
                xqb = wft("xqb", [128, D], BF16)
                xqs = wft("xqs", [128, D], BF16)
                xqT8 = wft("xqT8", [128, FT * 1024], FP8)
                qd = wft("qd", [128, 7 * 512], F32)
                QU, QW, QE, QA32, QA52, QQ, QQI = [slice(i * 512, (i + 1) * 512) for i in range(7)]

                for p8 in range(8):
                    @block.sync
                    def _(sync, p8=p8):
                        if p8 == 0:
                            sync.wait_ge(cVE.sem, cVE.n)
                            sync.wait_ge(cSC.sem, cSC.n)
                            sync.wait_ge(cQKD.sem, cQKD.n)
                            sync.wait_ge(cQKLa.sem, cQKLa.n)
                            sync.wait_ge(cQKLb.sem, cQKLb.n)
                            sync.wait_ge(cO.sem, cO.n)
                            sync.wait_ge(cEP.sem, cEP.n)
                        sync.wait_ge(cWV.sem, cWV.n)
                        sync.dma_start(out=x2[:, 0:D], in_=d_wo[p8 * 128:(p8 + 1) * 128, :]).then_inc(cWL.sem, 16)
                    cWL.n += 16

                    @block.vector
                    def _(v, p8=p8):
                        v.wait_ge(cWL.sem, cWL.n)
                        v.tensor_copy(wo_bf[:, p8 * D:(p8 + 1) * D], x2[:, 0:D]).then_inc(cWV.sem, 1)
                    cWV.n += 1
                WO_READY = cWV.n

                for hf in range(2):
                    for j in range(8):
                        tt = hf * 8 + j

                        cX2p = cX2a if tt % 2 == 0 else cX2b

                        @block.sync
                        def _(sync, b=b, tt=tt, cX2p=cX2p):
                            sync.wait_ge(cWV.sem, cWV.n)
                            sync.dma_start(out=x2[:, (tt % 2) * D:(tt % 2 + 1) * D],
                                           in_=d_x[b, tt * 128:(tt + 1) * 128, :]).then_inc(cX2p.sem, 16)
                        cX2p.n += 16

                        @block.tensor
                        def _(t, tt=tt, j=j, hf=hf, PSW_T=cPSW.n, PSF_T=cPSF.n):
                            if j == 0 and hf == 0:
                                t.wait_ge(cEP.sem, ATT_DONE_EP)
                                t.wait_ge(cAV.sem, cAV.n)
                                t.wait_ge(cWV.sem, WO_READY)
                            if hf > 0 and j < 2:
                                t.wait_ge(cPSF.sem, PSF_T)
                            if j >= 2:
                                t.wait_ge(cPSW.sem, PSW_T - 1)
                            for p8 in range(8):
                                for n in range(4):
                                    ins = t.matmul(PS[:, (tt % 2) * 2048 + n * 512:(tt % 2) * 2048 + (n + 1) * 512],
                                                   bigA[:, p8 * 2048 + tt * 128: p8 * 2048 + (tt + 1) * 128],
                                                   wo_bf[:, p8 * D + n * 512: p8 * D + (n + 1) * 512],
                                                   start=(p8 == 0), stop=(p8 == 7))
                            ins.then_inc(cWM.sem, 1)
                        cWM.n += 1

                        @block.vector
                        def _(v, tt=tt, cX2p=cX2p):
                            v.wait_ge(cWM.sem, cWM.n)
                            v.wait_ge(cX2p.sem, cX2p.n)
                            v.wait_ge(cO.sem, cO.n)
                            pb = (tt % 2) * 2048
                            v.scalar_tensor_tensor(vv[:], PS[:, pb:pb + 2048], 1.0,
                                                   x2[:, (tt % 2) * D:(tt % 2 + 1) * D],
                                                   ALU.mult, ALU.add, accum_out=stats[:, 0:1]).then_inc(cPSW.sem, 1)
                            v.scalar_tensor_tensor(ob[:], vv[:], 1.0, vv[:], ALU.mult, ALU.mult,
                                                   accum_out=stats[:, 1:2])
                            v.tensor_scalar(stats[:, 2:3], stats[:, 0:1], 1.0 / 2048.0, None, ALU.mult)
                            v.tensor_scalar(stats[:, 3:4], stats[:, 1:2], 1.0 / 2048.0, None, ALU.mult).then_inc(cWV.sem, 1)
                            v.wait_ge(cWV.sem, cWV.n + 1)
                            v.tensor_tensor(stats[:, 4:5], stats[:, 2:3], stats[:, 2:3], ALU.mult).then_inc(cWV.sem, 1)
                            v.wait_ge(cWV.sem, cWV.n + 2)
                            v.tensor_tensor(stats[:, 3:4], stats[:, 3:4], stats[:, 4:5], ALU.subtract).then_inc(cWV.sem, 1)
                        cWV.n += 3
                        cPSW.n += 1

                        @block.scalar
                        def _(s):
                            s.wait_ge(cWV.sem, cWV.n)
                            s.activation(stats[:, 5:6], stats[:, 3:4], AF.Ln, bias=cpp[:, C_E5:C_E5 + 1]).then_inc(cSC.sem, 1)
                            s.wait_ge(cSC.sem, cSC.n + 1)
                            s.activation(stats[:, 5:6], stats[:, 5:6], AF.Exp, scale=-0.5).then_inc(cSC.sem, 1)
                        cSC.n += 2

                        @block.vector
                        def _(v):
                            v.wait_ge(cSC.sem, cSC.n)
                            v.wait_ge(cZD.sem, cZD.n)
                            v.tensor_scalar(zz[:], vv[:], stats[:, 2:3], stats[:, 5:6],
                                            ALU.subtract, ALU.mult).then_inc(cWV.sem, 1)
                        cWV.n += 1

                        @block.sync
                        def _(sync, b=b, tt=tt):
                            sync.wait_ge(cWV.sem, cWV.n)
                            sync.dma_start(out=d_zscr[b, tt * 128:(tt + 1) * 128, :], in_=zz[:]).then_inc(cZD.sem, 16)
                        cZD.n += 16

                        for qq in range(4):
                            c0, c1 = qq * 512, (qq + 1) * 512

                            @block.vector
                            def _(v, c0=c0, c1=c1):
                                v.tensor_scalar(qd[:, QU], zz[:, c0:c1], cpp[:, C_RS:C_RS + 1], 448.0,
                                                ALU.mult, ALU.min)
                                v.tensor_single_scalar(qd[:, QU], qd[:, QU], -448.0, ALU.max)
                                v.tensor_tensor(qd[:, QW], qd[:, QU], qd[:, QU], ALU.mult).then_inc(cQA.sem, 1)
                            cQA.n += 1

                            @block.scalar
                            def _(s):
                                s.wait_ge(cQA.sem, cQA.n)
                                s.activation(qd[:, QW], qd[:, QW], AF.Ln, bias=cpp[:, C_C40:C_C40 + 1]).then_inc(cQB.sem, 1)
                            cQB.n += 1

                            @block.vector
                            def _(v):
                                v.wait_ge(cQB.sem, cQB.n)
                                v.tensor_scalar(qd[:, QE], qd[:, QW], LN2INV2, -3.5, ALU.mult, ALU.add)
                                v.tensor_scalar(qd[:, QA32], qd[:, QE], C32, -C32, ALU.add, ALU.add)
                                v.tensor_scalar(qd[:, QA52], qd[:, QE], C52, -C52, ALU.add, ALU.add)
                                v.tensor_tensor(qd[:, QA52], qd[:, QA52], qd[:, QA32], ALU.subtract)
                                v.scalar_tensor_tensor(qd[:, QE], qd[:, QA52], cpp[:, C_TAU:C_TAU + 1],
                                                       qd[:, QA32], ALU.mult, ALU.add)
                                v.tensor_single_scalar(qd[:, QE], qd[:, QE], -9.0, ALU.max).then_inc(cQA.sem, 1)
                            cQA.n += 1

                            @block.scalar
                            def _(s):
                                s.wait_ge(cQA.sem, cQA.n)
                                s.activation(qd[:, QQ], qd[:, QE], AF.Exp,
                                             bias=cpp[:, C_N2LN2:C_N2LN2 + 1], scale=cpp[:, C_LN2:C_LN2 + 1])
                                s.activation(qd[:, QQI], qd[:, QE], AF.Exp,
                                             scale=cpp[:, C_NLN2:C_NLN2 + 1]).then_inc(cQB.sem, 1)
                            cQB.n += 1

                            @block.vector
                            def _(v, c0=c0, c1=c1):
                                v.wait_ge(cQB.sem, cQB.n)
                                if c0 == 0:
                                    v.wait_ge(cQT.sem, cQT.n)
                                v.tensor_tensor(qd[:, QQI], qd[:, QU], qd[:, QQI], ALU.mult)
                                v.tensor_scalar(qd[:, QA32], qd[:, QQI], C32, -C32, ALU.add, ALU.add)
                                v.tensor_scalar(qd[:, QA52], qd[:, QQI], C52, -C52, ALU.add, ALU.add)
                                v.tensor_tensor(qd[:, QA52], qd[:, QA52], qd[:, QA32], ALU.subtract)
                                v.scalar_tensor_tensor(qd[:, QE], qd[:, QA52], cpp[:, C_TAU:C_TAU + 1],
                                                       qd[:, QA32], ALU.mult, ALU.add)
                                v.tensor_tensor(xqb[:, c0:c1], qd[:, QE], qd[:, QQ], ALU.mult).then_inc(cWV.sem, 1)
                            cWV.n += 1

                            @block.sync
                            def _(s, j=j, hf=hf, b=b, qq=qq, cWVt=cWV.n, cQ8t=cQ8.n):
                                s.wait_ge(cWV.sem, cWVt)
                                s.wait_ge(cQ8.sem, cQ8t)
                                if qq == 0 and j == 0 and (hf == 1 or b > 0):
                                    s.wait_ge(cFM.sem, cFM.n)
                                for ft in range(4 * qq, 4 * qq + 4):
                                    s.dma_start_transpose(xqs[:, ft * 128:(ft + 1) * 128],
                                                          xqb[:, ft * 128:(ft + 1) * 128]).then_inc(cQT.sem, 16)
                            cQT.n += 64

                        @block.scalar
                        def _(s, j=j):
                            s.wait_ge(cQT.sem, cQT.n)
                            src = xqs[:].rearrange("p (f c) -> p f c", c=128)
                            dst = xqT8[:].rearrange("p (f c) -> p f c", c=1024)[:, :, j * 128:(j + 1) * 128]
                            s.copy(dst, src).then_inc(cQ8.sem, 1)
                        cQ8.n += 1

                        if DBG and b == 0 and hf == 0:
                            @block.sync
                            def _(sync, j=j):
                                sync.wait_ge(cWV.sem, cWV.n)
                                sync.dma_start(out=d_xqdbg[j], in_=xqb[:]).then_inc(cO.sem, 16)
                            cO.n += 16

                    if DBG and b == 0 and hf == 0:
                        @block.sync
                        def _(sync):
                            sync.wait_ge(cQ8.sem, cQ8.n)
                            sync.dma_start(out=d_xq8dbg, in_=xqT8[:]).then_inc(cO.sem, 16)
                            sync.dma_start(out=d_cppdbg, in_=cpp[:]).then_inc(cO.sem, 16)
                        cO.n += 32

                    Q8_HALF = cQ8.n
                    for j in range(8 if PHASES >= 5 else 0):
                        tt = hf * 8 + j

                        @block.sync
                        def _(sync, b=b, tt=tt, j=j, PSW_T=cPSW.n, PSF_T=cPSF.n):
                            if j < 2:
                                sync.wait_ge(cPSW.sem, PSW_T)
                            else:
                                sync.wait_ge(cPSF.sem, PSF_T - 1)
                            sync.dma_start(out=x2[:, (j % 2) * D:(j % 2 + 1) * D],
                                           in_=d_zscr[b, tt * 128:(tt + 1) * 128, :]).then_inc(cZL.sem, 16)
                        cZL.n += 16

                        @block.tensor
                        def _(t, tt=tt, j=j, PSW_T=cPSW.n, PSF_T=cPSF.n):
                            t.wait_ge(cQ8.sem, Q8_HALF - 7 + j)
                            if j == 0:
                                t.wait_ge(cPSW.sem, PSW_T)
                            if j >= 2:
                                t.wait_ge(cPSF.sem, PSF_T - 1)
                            for ft in range(FT):
                                for n in range(4):
                                    ins = t.matmul(PS[:, (tt % 2) * 2048 + n * 512:(tt % 2) * 2048 + (n + 1) * 512],
                                                   xqT8[:, ft * 1024 + j * 128: ft * 1024 + j * 128 + 128],
                                                   w8[:, ft * D + n * 512: ft * D + (n + 1) * 512],
                                                   start=(ft == 0), stop=(ft == FT - 1))
                            ins.then_inc(cFM.sem, 1)
                        cFM.n += 1

                        @block.vector
                        def _(v, tt=tt, j=j):
                            v.wait_ge(cFM.sem, cFM.n)
                            v.wait_ge(cZL.sem, cZL.n)
                            v.wait_ge(cO.sem, cO.n)
                            v.wait_ge(cZD.sem, cZD.n)
                            pb = (tt % 2) * 2048
                            v.scalar_tensor_tensor(vv[:], PS[:, pb:pb + 2048], cpp[:, C_DEQ:C_DEQ + 1],
                                                   x2[:, (j % 2) * D:(j % 2 + 1) * D],
                                                   ALU.mult, ALU.add, accum_out=stats[:, 0:1]).then_inc(cPSF.sem, 1)
                            v.scalar_tensor_tensor(zz[:], vv[:], 1.0, vv[:], ALU.mult, ALU.mult,
                                                   accum_out=stats[:, 1:2])
                            v.tensor_scalar(stats[:, 2:3], stats[:, 0:1], 1.0 / 2048.0, None, ALU.mult)
                            v.tensor_scalar(stats[:, 3:4], stats[:, 1:2], 1.0 / 2048.0, None, ALU.mult).then_inc(cWV.sem, 1)
                            v.wait_ge(cWV.sem, cWV.n + 1)
                            v.tensor_tensor(stats[:, 4:5], stats[:, 2:3], stats[:, 2:3], ALU.mult).then_inc(cWV.sem, 1)
                            v.wait_ge(cWV.sem, cWV.n + 2)
                            v.tensor_tensor(stats[:, 3:4], stats[:, 3:4], stats[:, 4:5], ALU.subtract).then_inc(cWV.sem, 1)
                        cWV.n += 3
                        cPSF.n += 1

                        @block.scalar
                        def _(s):
                            s.wait_ge(cWV.sem, cWV.n)
                            s.activation(stats[:, 5:6], stats[:, 3:4], AF.Ln, bias=cpp[:, C_E5:C_E5 + 1]).then_inc(cSC.sem, 1)
                            s.wait_ge(cSC.sem, cSC.n + 1)
                            s.activation(stats[:, 5:6], stats[:, 5:6], AF.Exp, scale=-0.5).then_inc(cSC.sem, 1)
                        cSC.n += 2

                        @block.vector
                        def _(v):
                            v.wait_ge(cSC.sem, cSC.n)
                            v.tensor_scalar(ob[:], vv[:], stats[:, 2:3], stats[:, 5:6],
                                            ALU.subtract, ALU.mult).then_inc(cWV.sem, 1)
                        cWV.n += 1

                        @block.sync
                        def _(sync, b=b, tt=tt):
                            sync.wait_ge(cWV.sem, cWV.n)
                            sync.dma_start(out=d_out[b, tt * 128:(tt + 1) * 128, :], in_=ob[:]).then_inc(cO.sem, 16)
                        cO.n += 16

        @block.sync
        def _(sync):
            for ch in C.values():
                if ch.n > 0:
                    sync.wait_ge(ch.sem, ch.n)

    return nc


def _host_layouts(inputs):
    wq_b = np.ascontiguousarray(np.asarray(inputs["L0_attn_wq_b"], dtype=np.float32))
    wkv_a = np.ascontiguousarray(np.asarray(inputs["L0_attn_wkv_a"], dtype=np.float32))
    wkv_b = np.ascontiguousarray(np.asarray(inputs["L0_attn_wkv_b"], dtype=np.float32))
    perm = np.concatenate([np.arange(0, 64, 2), np.arange(1, 64, 2)])
    perm_r = np.concatenate([np.arange(1, 64, 2), np.arange(0, 64, 2)])
    wqbN = np.empty((64, NH * 64), np.float32)
    wqbX = np.empty((64, NH * 64), np.float32)
    wqbR = np.empty((64, NH * 64), np.float32)
    wkbN = np.empty((64, NH * 64), np.float32)
    wkbV = np.empty((64, NH * 64), np.float32)
    for h in range(NH):
        base = h * 128
        wqbN[:, h * 64:(h + 1) * 64] = wq_b[:, base:base + 64]
        wqbX[:, h * 64:(h + 1) * 64] = wq_b[:, base + 64 + perm]
        wqbR[:, h * 64:(h + 1) * 64] = wq_b[:, base + 64 + perm_r]
        wkbN[:, h * 64:(h + 1) * 64] = wkv_b[:, base:base + 64]
        wkbV[:, h * 64:(h + 1) * 64] = wkv_b[:, base + 64:base + 128]
    wkva = np.empty((D, 128), np.float32)
    wkva[:, 0:64] = wkv_a[:, 0:64]
    wkva[:, 64:128] = wkv_a[:, 64 + perm]
    w8 = np.asarray(inputs["L0_ffn_weight"])
    w8T = np.ascontiguousarray(w8.view(np.uint8).T).view(ml_dtypes.float8_e4m3)
    return wqbN, wqbX, wqbR, wkbN, wkbV, wkva, w8T


_NC_CACHE = {}


def kernel(**inputs):
    x = np.ascontiguousarray(np.asarray(inputs["L0_attn_hidden_states"], dtype=np.float32))
    wqbN, wqbX, wqbR, wkbN, wkbV, wkva, w8T = _host_layouts(inputs)
    cos4, sin4 = _yarn_tables()
    wqa = np.ascontiguousarray(np.asarray(inputs["L0_attn_wq_a"], dtype=np.float32))
    qnw = np.ascontiguousarray(np.asarray(inputs["L0_attn_q_a_norm_w"], dtype=np.float32).reshape(64, 1))
    knw = np.ascontiguousarray(np.asarray(inputs["L0_attn_kv_a_norm_w"], dtype=np.float32).reshape(64, 1))
    wo = np.ascontiguousarray(np.asarray(inputs["L0_attn_wo"], dtype=np.float32))
    sw = np.ascontiguousarray(np.repeat(np.asarray(inputs["L0_ffn_weight_scale"], dtype=np.float32).reshape(1, 1), 128, axis=0))
    s_in = np.asarray(inputs["L0_ffn_input_scale"]).reshape(1, 1)
    if s_in.dtype == np.uint16:
        s_in = s_in.view(ml_dtypes.bfloat16)
    assert s_in.dtype == ml_dtypes.bfloat16, s_in.dtype
    s_in = np.ascontiguousarray(np.repeat(s_in, 128, axis=0))

    if "nc" not in _NC_CACHE:
        _NC_CACHE["nc"] = build_nc()
    nc = _NC_CACHE["nc"]

    common = {
        "wqa": wqa, "wqbN": wqbN, "wqbX": wqbX, "wqbR": wqbR,
        "wkva": wkva, "wkbN": wkbN, "wkbV": wkbV,
        "qnw": qnw, "knw": knw, "wo": wo, "w8T": w8T,
        "sw": sw, "sin_": s_in, "cos4": cos4, "sin4": sin4,
    }
    in_maps = []
    for c in range(NCORES):
        m = dict(common)
        m["x"] = np.ascontiguousarray(x[c * NB:(c + 1) * NB])
        in_maps.append(m)

    res = run_bass_kernel_spmd(nc, in_maps, core_ids=list(range(NCORES)))
    out = np.concatenate([np.asarray(r["out"]) for r in res.results], axis=0)
    return out.astype(np.float32, copy=False)


# version 14
# speedup vs baseline: 1.2502x; 1.1331x over previous
import math
import numpy as np
import ml_dtypes
from contextlib import ExitStack

import concourse.bass as bass
import concourse.mybir as mybir
from concourse.bass_utils import run_bass_kernel_spmd

F32 = mybir.dt.float32
BF16 = mybir.dt.bfloat16
FP8 = mybir.dt.float8e4
AF = mybir.ActivationFunctionType
ALU = mybir.AluOpType

NCORES = 8
NB = 2
S = 2048
D = 2048
NH = 16
NP = 8
TT = 16
FT = 16

C32 = float(np.float32(1.5 * 2.0 ** 23))
C52 = float(np.float32(1.5 * 2.0 ** 52))
LN2INV2 = float(np.float32(1.0 / (2.0 * math.log(2.0))))
SM_SCALE = float(np.float32(128.0 ** -0.5))


def _yarn_tables():
    ROPE_BASE, FACTOR = 10000.0, 40.0
    BETA_FAST, BETA_SLOW, ORIG_MAX, MSCALE = 32.0, 1.0, 4096, 0.707
    dim = 64
    exponents = np.arange(0, dim, 2, dtype=np.float64) / dim
    freq_extra = 1.0 / ROPE_BASE ** exponents
    freq_inter = 1.0 / (FACTOR * ROPE_BASE ** exponents)

    def corr_dim(num_rot):
        return dim * math.log(ORIG_MAX / (num_rot * 2 * math.pi)) / (2 * math.log(ROPE_BASE))

    low = max(math.floor(corr_dim(BETA_FAST)), 0)
    high = min(math.ceil(corr_dim(BETA_SLOW)), dim - 1)
    if low == high:
        high = high + 0.001
    ramp = np.clip((np.arange(dim // 2, dtype=np.float64) - low) / (high - low), 0.0, 1.0)
    inv_freq = freq_inter * ramp + freq_extra * (1.0 - ramp)
    t = np.arange(S, dtype=np.float64)
    freqs = np.outer(t, inv_freq)

    def yarn_get_mscale(scale, mscale=1.0):
        if scale <= 1:
            return 1.0
        return 0.1 * mscale * math.log(scale) + 1.0

    msc = yarn_get_mscale(FACTOR, MSCALE) / yarn_get_mscale(FACTOR, 0.0)
    cosF = (np.cos(freqs) * msc).astype(np.float32).T
    sinF = (np.sin(freqs) * msc).astype(np.float32).T
    return (np.ascontiguousarray(np.tile(cosF, (4, 1))),
            np.ascontiguousarray(np.tile(sinF, (4, 1))))


class Ch:
    def __init__(self, stack, nc, name):
        self.sem = stack.enter_context(nc.semaphore(name))
        self.n = 0


PHASES = 99
E_STEPS = 99
E_SUB = 99
E_GEN = 99
DBG = False


def build_nc():
    nc = bass.Bass("TRN2", target_bir_lowering=False, detect_race_conditions=False)

    d_x = nc.dram_tensor("x", [NB, S, D], F32, kind="ExternalInput").ap()
    d_wqa = nc.dram_tensor("wqa", [D, 64], F32, kind="ExternalInput").ap()
    d_wqbN = nc.dram_tensor("wqbN", [64, NH * 64], F32, kind="ExternalInput").ap()
    d_wqbX = nc.dram_tensor("wqbX", [64, NH * 64], F32, kind="ExternalInput").ap()
    d_wqbR = nc.dram_tensor("wqbR", [64, NH * 64], F32, kind="ExternalInput").ap()
    d_wkva = nc.dram_tensor("wkva", [D, 128], F32, kind="ExternalInput").ap()
    d_wkbN = nc.dram_tensor("wkbN", [64, NH * 64], F32, kind="ExternalInput").ap()
    d_wkbV = nc.dram_tensor("wkbV", [64, NH * 64], F32, kind="ExternalInput").ap()
    d_qnw = nc.dram_tensor("qnw", [64, 1], F32, kind="ExternalInput").ap()
    d_knw = nc.dram_tensor("knw", [64, 1], F32, kind="ExternalInput").ap()
    d_wo = nc.dram_tensor("wo", [NH * 64, D], F32, kind="ExternalInput").ap()
    d_w8T = nc.dram_tensor("w8T", [D, D], FP8, kind="ExternalInput").ap()
    d_sw = nc.dram_tensor("sw", [128, 1], F32, kind="ExternalInput").ap()
    d_sin = nc.dram_tensor("sin_", [128, 1], BF16, kind="ExternalInput").ap()
    d_cos4 = nc.dram_tensor("cos4", [128, S], F32, kind="ExternalInput").ap()
    d_sin4 = nc.dram_tensor("sin4", [128, S], F32, kind="ExternalInput").ap()
    d_out = nc.dram_tensor("out", [NB, S, D], F32, kind="ExternalOutput").ap()
    scrk = "ExternalOutput" if DBG else "Internal"
    d_xqdbg = nc.dram_tensor("xqdbg", [8, 128, D], BF16, kind="ExternalOutput").ap() if DBG else None
    d_cppdbg = nc.dram_tensor("cppdbg", [128, 16], F32, kind="ExternalOutput").ap() if DBG else None
    d_xq8dbg = nc.dram_tensor("xq8dbg", [128, FT * 1024], FP8, kind="ExternalOutput").ap() if DBG else None
    d_zscr = nc.dram_tensor("zscr", [NB, S, D], F32, kind=scrk).ap()
    d_qscr = nc.dram_tensor("qscr", [NH, 128, S], BF16, kind=scrk).ap()
    d_kscr = nc.dram_tensor("kscr", [NH, 128, S], BF16, kind=scrk).ap()

    with ExitStack() as ctx:
        def sbt(name, shape, dt=F32):
            return ctx.enter_context(nc.sbuf_tensor(name, shape, dt))

        w8 = sbt("w8", [128, FT * D], FP8)
        cos4 = sbt("cos4_b", [128, S], BF16)
        sin4 = sbt("sin4_b", [128, S], BF16)
        wqa_b = sbt("wqa_b", [128, FT * 64], BF16)
        wkva_b = sbt("wkva_b", [128, FT * 128], BF16)
        wqbN_b = sbt("wqbN_b", [64, NH * 64], BF16)
        wqbX_b = sbt("wqbX_b", [64, NH * 64], BF16)
        wqbR_b = sbt("wqbR_b", [64, NH * 64], BF16)
        wkbN_b = sbt("wkbN_b", [64, NH * 64], BF16)
        wkbV_b = sbt("wkbV_b", [64, NH * 64], BF16)
        ones_b = sbt("ones_b", [64, 128], BF16)
        cpp = sbt("cpp", [128, 16], F32)
        qnw_s = sbt("qnw_s", [64, 1], F32)
        knw_s = sbt("knw_s", [64, 1], F32)
        C_TAU, C_LN2, C_NLN2, C_N2LN2, C_RS, C_DEQ, C_E6, C_E5, C_C40, C_TWO = range(10)

        bigA = sbt("bigA", [128, 16384], BF16)
        stats = sbt("stats", [128, 8], F32)

        PS = ctx.enter_context(nc.psum_tensor("PS", [128, 4096], F32))

        chnames = ["cLD", "cSET", "cXa", "cXb", "cTRa", "cTRb", "cTM", "cSC", "cVE", "cQKD",
                   "cQKLa", "cQKLb", "cQKF", "cL", "cE2", "cAV", "cEP", "cWM", "cX2a", "cX2b",
                   "cZD", "cZL", "cQT", "cQ8", "cFM", "cO", "cWV", "cQA", "cQB", "cWL", "cPSW", "cPSF"]
        C = {n: Ch(ctx, nc, n) for n in chnames}
        (cLD, cSET, cXa, cXb, cTRa, cTRb, cTM, cSC, cVE, cQKD, cQKLa, cQKLb, cQKF,
         cL, cE2, cAV, cEP, cWM, cX2a, cX2b, cZD, cZL, cQT, cQ8, cFM, cO, cWV,
         cQA, cQB, cWL, cPSW, cPSF) = (C[n] for n in chnames)

        block = ctx.enter_context(nc.Block())

        @block.scalar
        def _(s):
            from concourse.hw_specs import get_activation_tables
            for idx, funcs in enumerate(get_activation_tables(nc.m.arch).values()):
                if AF.Ln in funcs and AF.Exp in funcs:
                    s.add_instruction(mybir.InstLoadActFuncSet(
                        name=nc.get_next_instruction_name(),
                        act_func_set_id=idx, ins=[], outs=[]))
                    break

        @block.sync
        def _(sync):
            sync.dma_start(out=w8[:].rearrange("p (t c) -> p t c", c=D),
                           in_=d_w8T.rearrange("(t p) c -> p t c", p=128)).then_inc(cLD.sem, 16)
        cLD.n += 16

        with nc.sbuf_tensor("stage32", [128, S], F32) as stage32, \
             nc.sbuf_tensor("sin_b", [128, 1], BF16) as sin_b:

            stage_jobs = [
                (d_cos4, cos4, 128, S),
                (d_sin4, sin4, 128, S),
                (d_wqa.rearrange("(t p) c -> p t c", p=128), wqa_b, 128, FT * 64),
                (d_wkva.rearrange("(t p) c -> p t c", p=128), wkva_b, 128, FT * 128),
                (d_wqbN, wqbN_b, 64, NH * 64),
                (d_wqbX, wqbX_b, 64, NH * 64),
                (d_wkbN, wkbN_b, 64, NH * 64),
                (d_wkbV, wkbV_b, 64, NH * 64),
            ]
            for src, dst, p, width in stage_jobs:
                @block.sync
                def _(sync, src=src, p=p, width=width):
                    sync.wait_ge(cSET.sem, cSET.n)
                    o = stage32[0:p, 0:width]
                    if len(src.shape) == 3:
                        o = o.rearrange("p (t c) -> p t c", t=src.shape[1])
                    sync.dma_start(out=o, in_=src).then_inc(cLD.sem, 16)
                cLD.n += 16

                @block.vector
                def _(v, dst=dst, p=p, width=width):
                    v.wait_ge(cLD.sem, cLD.n)
                    v.tensor_copy(dst[0:p, 0:width], stage32[0:p, 0:width]).then_inc(cSET.sem, 1)
                cSET.n += 1

            @block.sync
            def _(sync):
                sync.wait_ge(cSET.sem, cSET.n)
                sync.dma_start(out=stage32[0:64, 0:NH * 64], in_=d_wqbR).then_inc(cLD.sem, 16)
            cLD.n += 16

            @block.vector
            def _(v):
                v.wait_ge(cLD.sem, cLD.n)
                si = stage32[0:64, 0:NH * 64].rearrange("p (h c) -> p h c", c=64)
                so = wqbR_b[0:64, :].rearrange("p (h c) -> p h c", c=64)
                v.tensor_scalar(so[:, :, 0:32], si[:, :, 0:32], -1.0, None, ALU.mult)
                v.tensor_copy(so[:, :, 32:64], si[:, :, 32:64]).then_inc(cSET.sem, 1)
            cSET.n += 1

            @block.sync
            def _(sync):
                sync.wait_ge(cSET.sem, cSET.n)
                sync.dma_start(out=cpp[:, C_RS:C_RS + 1], in_=d_sw).then_inc(cLD.sem, 16)
                sync.dma_start(out=sin_b[:], in_=d_sin).then_inc(cLD.sem, 16)
                sync.dma_start(out=qnw_s[:], in_=d_qnw).then_inc(cLD.sem, 16)
                sync.dma_start(out=knw_s[:], in_=d_knw).then_inc(cLD.sem, 16)
            cLD.n += 64

            @block.vector
            def _(v):
                v.memset(cpp[:, 10:16], 0.0)
                v.wait_ge(cLD.sem, cLD.n)
                v.memset(cpp[:, C_TAU:C_TAU + 1], 2.0 ** 52).then_inc(cSET.sem, 1)
                v.memset(cpp[:, C_TWO:C_TWO + 1], 2.0)
                v.memset(cpp[:, C_E6:C_E6 + 1], 1e-6)
                v.memset(cpp[:, C_E5:C_E5 + 1], 1e-5)
                v.memset(cpp[:, C_C40:C_C40 + 1], 2.0 ** -40)
                v.tensor_copy(cpp[:, C_DEQ:C_DEQ + 1], sin_b[:]).then_inc(cSET.sem, 1)
                v.wait_ge(cSET.sem, cSET.n + 2)
                v.tensor_scalar(cpp[:, C_TAU:C_TAU + 1], cpp[:, C_TAU:C_TAU + 1], 1.0, -(2.0 ** 52), ALU.add, ALU.add)
                v.tensor_tensor(cpp[:, C_DEQ:C_DEQ + 1], cpp[:, C_DEQ:C_DEQ + 1], cpp[:, C_RS:C_RS + 1], ALU.mult).then_inc(cSET.sem, 1)
                v.wait_ge(cSET.sem, cSET.n + 3)
                v.tensor_scalar(cpp[:, C_DEQ:C_DEQ + 1], cpp[:, C_DEQ:C_DEQ + 1], 4.0, None, ALU.mult)
                v.tensor_copy(cpp[:, C_RS:C_RS + 1], sin_b[:]).then_inc(cSET.sem, 1)
                v.wait_ge(cSET.sem, cSET.n + 4)
                v.reciprocal(cpp[:, C_RS:C_RS + 1], cpp[:, C_RS:C_RS + 1])
                v.tensor_scalar(qnw_s[:], qnw_s[:], SM_SCALE, None, ALU.mult)
                v.memset(ones_b[:], 1.0).then_inc(cSET.sem, 1)
            cSET.n += 5

            @block.scalar
            def _(s):
                s.wait_ge(cSET.sem, cSET.n)
                s.activation(cpp[:, C_LN2:C_LN2 + 1], cpp[:, C_TWO:C_TWO + 1], AF.Ln)
                s.mul(cpp[:, C_NLN2:C_NLN2 + 1], cpp[:, C_LN2:C_LN2 + 1], -1.0)
                s.mul(cpp[:, C_N2LN2:C_N2LN2 + 1], cpp[:, C_LN2:C_LN2 + 1], -2.0).then_inc(cSET.sem, 1)
            cSET.n += 1

        SETUP_DONE = cSET.n

        for b in range(NB if PHASES >= 6 else (1 if PHASES >= 2 else 0)):
            with ExitStack() as ea:
                def eat(name, shape, dt):
                    return ea.enter_context(nc.sbuf_tensor(name + str(b), shape, dt))
                vbuf = eat("vbuf", [128, TT * 1536], BF16)
                qp = [eat("qp0", [128, S], BF16), eat("qp1", [128, S], BF16)]
                kp = [eat("kp0", [128, S], BF16), eat("kp1", [128, S], BF16)]
                a2 = [eat(f"a2_{i}", [128, 1024], BF16) for i in range(3)]
                rcp_full = eat("rcp", [64, S], F32)
                rcp = rcp_full[0:64, 0:1024]
                qlat_n = eat("qlat_n", [64, S], BF16)
                kv_n = eat("kv_n", [64, S], BF16)
                kpe_r = eat("kpe_r", [64, S], BF16)

                with ExitStack() as ee:
                    def eet(name, shape, dt):
                        return ee.enter_context(nc.sbuf_tensor(name + str(b), shape, dt))
                    xbf = eet("xbf", [128, 2 * D], BF16)
                    kv_bf = eet("kv_bf", [64, S], BF16)
                    ql_bf = eet("ql_bf", [64, S], BF16)
                    kpe_bf = eet("kpe_bf", [64, S], BF16)
                    krot_bf = eet("krot_bf", [64, S], BF16)
                    tln = rcp_full
                    t1 = eet("t1", [128, 512], F32)
                    t2 = eet("t2", [128, 512], F32)
                    sq_bf = krot_bf

                    for hf in range(2):
                        for j in range(8):
                            tt = hf * 8 + j

                            cXp = cXa if tt % 2 == 0 else cXb
                            cTRp = cTRa if tt % 2 == 0 else cTRb

                            @block.gpsimd
                            def _(g, b=b, tt=tt, j=j, hf=hf, cXp=cXp, cTRp=cTRp):
                                if b > 0 and hf == 0 and j == 0:
                                    g.wait_ge(cO.sem, cO.n)
                                    g.wait_ge(cWV.sem, cWV.n)
                                    g.wait_ge(cSC.sem, cSC.n)
                                    g.wait_ge(cVE.sem, cVE.n)
                                if tt >= 2:
                                    g.wait_ge(cTRp.sem, cTRp.n)
                                g.dma_start(out=xbf[:, (j % 2) * D:(j % 2 + 1) * D],
                                            in_=d_x[b, tt * 128:(tt + 1) * 128, :]).then_inc(cXp.sem, 16)
                            cXp.n += 16

                            @block.sync
                            def _(s, j=j, b=b, hf=hf, cXp=cXp, cTRp=cTRp):
                                s.wait_ge(cXp.sem, cXp.n)
                                if j == 0:
                                    s.wait_ge(cTM.sem, cTM.n)
                                    s.wait_ge(cWM.sem, cWM.n)
                                dst = bigA[:].rearrange("p (t c) -> p t c", c=1024)[:, :, j * 128:(j + 1) * 128]
                                src = xbf[:, (j % 2) * D:(j % 2 + 1) * D].rearrange("p (t c) -> p t c", c=128)
                                s.dma_start_transpose(dst, src).then_inc(cTRp.sem, 16)
                            cTRp.n += 16

                        @block.tensor
                        def _(t, hf=hf):
                            if E_STEPS < 2:
                                return
                            t.wait_ge(cTRa.sem, cTRa.n)
                            t.wait_ge(cTRb.sem, cTRb.n)
                            if hf == 0:
                                t.wait_ge(cSET.sem, SETUP_DONE)
                                t.wait_ge(cSC.sem, cSC.n)
                                t.wait_ge(cWV.sem, cWV.n)
                            for n in range(2):
                                for ft in range(FT):
                                    t.matmul(PS[:, hf * 1024 + n * 512: hf * 1024 + (n + 1) * 512],
                                             wkva_b[:, ft * 128:(ft + 1) * 128],
                                             bigA[:, ft * 1024 + n * 512: ft * 1024 + (n + 1) * 512],
                                             start=(ft == 0), stop=(ft == FT - 1))
                            for n in range(2):
                                for ft in range(FT):
                                    ins = t.matmul(PS[0:64, 2048 + hf * 1024 + n * 512: 2048 + hf * 1024 + (n + 1) * 512],
                                                   wqa_b[:, ft * 64:(ft + 1) * 64],
                                                   bigA[:, ft * 1024 + n * 512: ft * 1024 + (n + 1) * 512],
                                                   start=(ft == 0), stop=(ft == FT - 1))
                            ins.then_inc(cTM.sem, 1)
                        if E_STEPS >= 2:
                            cTM.n += 1
                    if E_STEPS < 3:
                        continue

                    @block.scalar
                    def _(s):
                        s.wait_ge(cTM.sem, cTM.n)
                        s.copy(kv_bf[:], PS[0:64, 0:2048])
                        s.copy(kpe_bf[:], PS[64:128, 0:2048])
                        s.copy(ql_bf[:], PS[0:64, 2048:4096])
                        s.mul(krot_bf[0:32, :], kpe_bf[32:64, :], -1.0)
                        s.copy(krot_bf[32:64, :], kpe_bf[0:32, :]).then_inc(cSC.sem, 1)
                    cSC.n += 1
                    if E_STEPS < 4:
                        continue

                    @block.vector
                    def _(v):
                        v.wait_ge(cSC.sem, cSC.n)
                        if cSET.n == SETUP_DONE:
                            v.wait_ge(cSET.sem, SETUP_DONE)
                        for n in range(4):
                            sl = slice(n * 512, (n + 1) * 512)
                            v.tensor_tensor(t1[0:64, :], kpe_bf[:, sl], cos4[0:64, sl], ALU.mult)
                            v.tensor_tensor(t2[0:64, :], krot_bf[:, sl], sin4[0:64, sl], ALU.mult)
                            ins = v.tensor_tensor(kpe_r[:, sl], t1[0:64, :], t2[0:64, :], ALU.add)
                        ins.then_inc(cVE.sem, 1)
                    cVE.n += 1

                    if E_SUB < 2:
                        continue

                    @block.vector
                    def _(v):
                        v.tensor_tensor(sq_bf[:], ql_bf[:], ql_bf[:], ALU.mult).then_inc(cVE.sem, 1)
                    cVE.n += 1
                    if E_SUB < 3:
                        continue

                    @block.tensor
                    def _(t):
                        t.wait_ge(cVE.sem, cVE.n)
                        for n in range(4):
                            ins = t.matmul(PS[:, n * 512:(n + 1) * 512], ones_b[:],
                                           sq_bf[:, n * 512:(n + 1) * 512], start=True, stop=True)
                        ins.then_inc(cTM.sem, 1)
                    cTM.n += 1

                    if E_SUB >= 4:
                        @block.scalar
                        def _(s):
                            s.wait_ge(cTM.sem, cTM.n)
                            s.activation(tln[:], PS[0:64, 0:2048], AF.Ln, bias=cpp[0:64, C_E6:C_E6 + 1], scale=1.0 / 64.0)
                            s.activation(tln[:], tln[:], AF.Exp, scale=-0.5).then_inc(cSC.sem, 1)
                        cSC.n += 1
                    if E_SUB < 5:
                        continue

                    @block.vector
                    def _(v):
                        v.wait_ge(cSC.sem, cSC.n)
                        v.tensor_tensor(tln[:], ql_bf[:], tln[:], ALU.mult)
                        v.tensor_scalar(qlat_n[:], tln[:], qnw_s[:], None, ALU.mult)
                        v.tensor_tensor(sq_bf[:], kv_bf[:], kv_bf[:], ALU.mult).then_inc(cVE.sem, 1)
                    cVE.n += 1
                    if E_SUB < 6:
                        continue

                    @block.tensor
                    def _(t):
                        t.wait_ge(cVE.sem, cVE.n)
                        for n in range(4):
                            ins = t.matmul(PS[:, 2048 + n * 512: 2048 + (n + 1) * 512], ones_b[:],
                                           sq_bf[:, n * 512:(n + 1) * 512], start=True, stop=True)
                        ins.then_inc(cTM.sem, 1)
                    cTM.n += 1

                    @block.scalar
                    def _(s):
                        s.wait_ge(cTM.sem, cTM.n)
                        s.activation(tln[:], PS[0:64, 2048:4096], AF.Ln, bias=cpp[0:64, C_E6:C_E6 + 1], scale=1.0 / 64.0)
                        s.activation(tln[:], tln[:], AF.Exp, scale=-0.5).then_inc(cSC.sem, 1)
                    cSC.n += 1

                    @block.vector
                    def _(v):
                        v.wait_ge(cSC.sem, cSC.n)
                        v.tensor_tensor(tln[:], kv_bf[:], tln[:], ALU.mult)
                        v.tensor_scalar(kv_n[:], tln[:], knw_s[:], None, ALU.mult).then_inc(cVE.sem, 1)
                    cVE.n += 1
                    if E_STEPS < 5:
                        continue

                    for tt in range(TT):
                        @block.tensor
                        def _(t, tt=tt):
                            t.wait_ge(cVE.sem, cVE.n)
                            t.wait_ge(cSC.sem, cSC.n)
                            for n in range(2):
                                ins = t.matmul(PS[:, (tt % 2) * 1024 + n * 512:(tt % 2) * 1024 + (n + 1) * 512],
                                               kv_n[:, tt * 128:(tt + 1) * 128],
                                               wkbV_b[:, n * 512:(n + 1) * 512], start=True, stop=True)
                            ins.then_inc(cTM.sem, 1)
                        cTM.n += 1

                        @block.scalar
                        def _(s, tt=tt, b=b):
                            s.wait_ge(cTM.sem, cTM.n)
                            if tt == 0 and b > 0:
                                s.wait_ge(cAV.sem, cAV.n)
                            src = PS[:, (tt % 2) * 1024:(tt % 2) * 1024 + 1024].rearrange(
                                "p (pr j c) -> p pr j c", pr=8, j=2, c=64)
                            dst = vbuf[:, tt * 1536:(tt + 1) * 1536].rearrange(
                                "p (pr g c) -> p pr g c", pr=8, g=3, c=64)[:, :, 0:3:2, :]
                            s.copy(dst, src).then_inc(cSC.sem, 1)
                        cSC.n += 1

                    if E_STEPS < 6:
                        continue

                    @block.vector
                    def _(v, b=b):
                        if b > 0:
                            v.wait_ge(cAV.sem, cAV.n)
                        ones_ap = vbuf[:].rearrange("p (t pr g c) -> p t pr g c", pr=8, g=3, c=64)[:, :, :, 1:2, :]
                        v.memset(ones_ap, 1.0).then_inc(cVE.sem, 1)
                    cVE.n += 1

                    for p in range(NP if E_STEPS >= 7 else 0):
                        @block.tensor
                        def _(t, p=p):
                            t.wait_ge(cSC.sem, cSC.n)
                            t.wait_ge(cVE.sem, cVE.n)
                            for n in range(4):
                                t.matmul(PS[:, n * 512:(n + 1) * 512], wqbX_b[:, p * 128:(p + 1) * 128],
                                         qlat_n[:, n * 512:(n + 1) * 512], start=True, stop=True)
                            for n in range(4):
                                ins = t.matmul(PS[:, 2048 + n * 512: 2048 + (n + 1) * 512], wqbR_b[:, p * 128:(p + 1) * 128],
                                               qlat_n[:, n * 512:(n + 1) * 512], start=True, stop=True)
                            ins.then_inc(cTM.sem, 1)
                        cTM.n += 1

                        if E_GEN < 2:
                            continue

                        @block.vector
                        def _(v, p=p):
                            v.wait_ge(cTM.sem, cTM.n)
                            if p > 0:
                                v.wait_ge(cQKD.sem, cQKD.n)
                            for n in range(4):
                                sl = slice(n * 512, (n + 1) * 512)
                                v.tensor_tensor(t1[:], PS[:, sl], cos4[:, sl], ALU.mult)
                                ins = v.tensor_tensor(t2[:], PS[:, 2048 + n * 512: 2048 + (n + 1) * 512],
                                                      sin4[:, sl], ALU.mult)
                                if n == 3:
                                    ins.then_inc(cVE.sem, 1)
                                v.tensor_tensor(qp[0][0:64, sl], t1[0:64, :], t2[0:64, :], ALU.add)
                                v.tensor_tensor(qp[1][64:128, sl], t1[64:128, :], t2[64:128, :], ALU.add)
                            v.tensor_copy(kp[0][0:64, :], kpe_r[:])
                            v.tensor_copy(kp[1][64:128, :], kpe_r[:]).then_inc(cVE.sem, 1)
                        cVE.n += 2

                        if E_GEN < 3:
                            continue

                        @block.tensor
                        def _(t, p=p):
                            t.wait_ge(cVE.sem, cVE.n)
                            for n in range(4):
                                t.matmul(PS[:, n * 512:(n + 1) * 512], wqbN_b[:, p * 128:(p + 1) * 128],
                                         qlat_n[:, n * 512:(n + 1) * 512], start=True, stop=True)
                            for n in range(4):
                                ins = t.matmul(PS[:, 2048 + n * 512: 2048 + (n + 1) * 512], wkbN_b[:, p * 128:(p + 1) * 128],
                                               kv_n[:, n * 512:(n + 1) * 512], start=True, stop=True)
                            ins.then_inc(cTM.sem, 1)
                        cTM.n += 1

                        if E_GEN < 4:
                            continue

                        @block.scalar
                        def _(s, p=p):
                            s.wait_ge(cTM.sem, cTM.n)
                            if p > 0:
                                s.wait_ge(cQKD.sem, cQKD.n)
                            s.copy(qp[0][64:128, :], PS[0:64, 0:2048])
                            s.copy(qp[1][0:64, :], PS[64:128, 0:2048])
                            s.copy(kp[0][64:128, :], PS[0:64, 2048:4096])
                            s.copy(kp[1][0:64, :], PS[64:128, 2048:4096]).then_inc(cSC.sem, 1)
                        cSC.n += 1

                        if E_GEN < 5:
                            continue

                        @block.sync
                        def _(sync, p=p):
                            sync.wait_ge(cSC.sem, cSC.n)
                            sync.wait_ge(cVE.sem, cVE.n)
                            sync.dma_start(out=d_qscr[2 * p], in_=qp[0][:]).then_inc(cQKD.sem, 16)
                            sync.dma_start(out=d_qscr[2 * p + 1], in_=qp[1][:]).then_inc(cQKD.sem, 16)
                            sync.dma_start(out=d_kscr[2 * p], in_=kp[0][:]).then_inc(cQKD.sem, 16)
                            sync.dma_start(out=d_kscr[2 * p + 1], in_=kp[1][:]).then_inc(cQKD.sem, 16)
                        cQKD.n += 64

                EXP_BASE, AV_BASE, EP_BASE, L_BASE = cE2.n, cAV.n, cEP.n, cL.n

                def emit_av(h, kt, g, hq):
                    @block.tensor
                    def _(t, h=h, kt=kt, g=g, hq=hq):
                        t.wait_ge(cE2.sem, EXP_BASE + g + 1)
                        if kt == 0 and hq >= 2:
                            t.wait_ge(cEP.sem, EP_BASE + hq - 1)
                        voff = kt * 1536 + (h // 2) * 192 + (h % 2) * 64
                        for n in range(2):
                            ins = t.matmul(PS[:, 2048 + (hq % 2) * 1024 + n * 512:
                                               2048 + (hq % 2) * 1024 + (n + 1) * 512],
                                           vbuf[:, voff:voff + 128],
                                           a2[g % 3][:, n * 512:(n + 1) * 512],
                                           start=(kt == 0), stop=(kt == 15))
                        ins.then_inc(cAV.sem, 1)
                    cAV.n += 1
                    if kt == 15:
                        @block.vector
                        def _(v, h=h, qc=hq % 2, hq=hq):
                            v.wait_ge(cAV.sem, AV_BASE + (hq + 1) * 16)
                            ob_ = 2048 + (hq % 2) * 1024
                            dst_c = (h // 2) * 2048 + qc * 1024
                            if h % 2 == 0:
                                v.reciprocal(rcp, PS[64:128, ob_:ob_ + 1024])
                                v.tensor_tensor(bigA[0:64, dst_c:dst_c + 1024],
                                                PS[0:64, ob_:ob_ + 1024], rcp, ALU.mult).then_inc(cEP.sem, 1)
                            else:
                                v.reciprocal(rcp, PS[0:64, ob_:ob_ + 1024])
                                v.tensor_tensor(bigA[64:128, dst_c:dst_c + 1024],
                                                PS[64:128, ob_:ob_ + 1024], rcp, ALU.mult).then_inc(cEP.sem, 1)
                        cEP.n += 1

                pend_av = None
                for h in range(NH if PHASES >= 3 else 0):
                    cQKLp = cQKLa if h % 2 == 0 else cQKLb

                    @block.sync
                    def _(sync, h=h, cQKLp=cQKLp):
                        if h < 2:
                            sync.wait_ge(cQKD.sem, cQKD.n)
                        else:
                            sync.wait_ge(cL.sem, L_BASE + (h - 1) * 32)
                        sync.dma_start(out=qp[h % 2][:], in_=d_qscr[h]).then_inc(cQKLp.sem, 16)
                        sync.dma_start(out=kp[h % 2][:], in_=d_kscr[h]).then_inc(cQKLp.sem, 16)
                    cQKLp.n += 32
                    QKL_TGT = cQKLp.n

                    for qc in range(2):
                        hq = h * 2 + qc
                        for kt in range(16):
                            g = hq * 16 + kt

                            @block.tensor
                            def _(t, h=h, qc=qc, kt=kt, g=g, cQKLp=cQKLp, QKL_TGT=QKL_TGT):
                                if kt == 0 and qc == 0:
                                    t.wait_ge(cQKLp.sem, QKL_TGT)
                                if g >= 2:
                                    t.wait_ge(cE2.sem, EXP_BASE + g - 1)
                                for n in range(2):
                                    ins = t.matmul(PS[:, (g % 2) * 1024 + n * 512:(g % 2) * 1024 + (n + 1) * 512],
                                                   kp[h % 2][:, kt * 128:(kt + 1) * 128],
                                                   qp[h % 2][:, qc * 1024 + n * 512: qc * 1024 + (n + 1) * 512],
                                                   start=True, stop=True)
                                ins.then_inc(cL.sem, 1)
                            cL.n += 1

                            @block.scalar
                            def _(s, g=g):
                                s.wait_ge(cL.sem, L_BASE + g + 1)
                                if g >= 3:
                                    s.wait_ge(cAV.sem, AV_BASE + g - 2)
                                s.activation(a2[g % 3][:], PS[:, (g % 2) * 1024:(g % 2) * 1024 + 1024],
                                             AF.Exp).then_inc(cE2.sem, 1)
                            cE2.n += 1

                            if pend_av is not None:
                                emit_av(*pend_av)
                            pend_av = (h, kt, g, hq)

                if pend_av is not None:
                    emit_av(*pend_av)
                    pend_av = None

                ATT_DONE_EP = cEP.n
                if PHASES < 4:
                    continue

            with ExitStack() as wf:
                def wft(name, shape, dt):
                    return wf.enter_context(nc.sbuf_tensor(name + str(b), shape, dt))
                wo_bf = wft("wo_bf", [128, 8 * D], BF16)
                x2 = wft("x2", [128, 2 * D], F32)
                vv = wft("vv", [128, D], F32)
                zz = wft("zz", [128, D], F32)
                ob = wft("ob", [128, D], F32)
                xqb = wft("xqb", [128, D], BF16)
                xqs = wft("xqs", [128, D], BF16)
                xqT8 = wft("xqT8", [128, FT * 1024], FP8)
                qd = wft("qd", [128, 7 * 512], F32)
                QU, QW, QE, QA32, QA52, QQ, QQI = [slice(i * 512, (i + 1) * 512) for i in range(7)]

                for p8 in range(8):
                    @block.sync
                    def _(sync, p8=p8):
                        if p8 == 0:
                            sync.wait_ge(cVE.sem, cVE.n)
                            sync.wait_ge(cSC.sem, cSC.n)
                            sync.wait_ge(cQKD.sem, cQKD.n)
                            sync.wait_ge(cQKLa.sem, cQKLa.n)
                            sync.wait_ge(cQKLb.sem, cQKLb.n)
                            sync.wait_ge(cO.sem, cO.n)
                            sync.wait_ge(cEP.sem, cEP.n)
                        sync.wait_ge(cWV.sem, cWV.n)
                        sync.dma_start(out=x2[:, 0:D], in_=d_wo[p8 * 128:(p8 + 1) * 128, :]).then_inc(cWL.sem, 16)
                    cWL.n += 16

                    @block.vector
                    def _(v, p8=p8):
                        v.wait_ge(cWL.sem, cWL.n)
                        v.tensor_copy(wo_bf[:, p8 * D:(p8 + 1) * D], x2[:, 0:D]).then_inc(cWV.sem, 1)
                    cWV.n += 1
                WO_READY = cWV.n

                for hf in range(2):
                    for j in range(8):
                        tt = hf * 8 + j

                        cX2p = cX2a if tt % 2 == 0 else cX2b

                        @block.sync
                        def _(sync, b=b, tt=tt, cX2p=cX2p):
                            sync.wait_ge(cWV.sem, cWV.n)
                            sync.dma_start(out=x2[:, (tt % 2) * D:(tt % 2 + 1) * D],
                                           in_=d_x[b, tt * 128:(tt + 1) * 128, :]).then_inc(cX2p.sem, 16)
                        cX2p.n += 16

                        @block.tensor
                        def _(t, tt=tt, j=j, hf=hf, PSW_T=cPSW.n, PSF_T=cPSF.n):
                            if j == 0 and hf == 0:
                                t.wait_ge(cEP.sem, ATT_DONE_EP)
                                t.wait_ge(cAV.sem, cAV.n)
                                t.wait_ge(cWV.sem, WO_READY)
                            if hf > 0 and j < 2:
                                t.wait_ge(cPSF.sem, PSF_T)
                            if j >= 2:
                                t.wait_ge(cPSW.sem, PSW_T - 1)
                            for p8 in range(8):
                                for n in range(4):
                                    ins = t.matmul(PS[:, (tt % 2) * 2048 + n * 512:(tt % 2) * 2048 + (n + 1) * 512],
                                                   bigA[:, p8 * 2048 + tt * 128: p8 * 2048 + (tt + 1) * 128],
                                                   wo_bf[:, p8 * D + n * 512: p8 * D + (n + 1) * 512],
                                                   start=(p8 == 0), stop=(p8 == 7))
                            ins.then_inc(cWM.sem, 1)
                        cWM.n += 1

                        @block.vector
                        def _(v, tt=tt, cX2p=cX2p):
                            v.wait_ge(cWM.sem, cWM.n)
                            v.wait_ge(cX2p.sem, cX2p.n)
                            v.wait_ge(cO.sem, cO.n)
                            pb = (tt % 2) * 2048
                            v.scalar_tensor_tensor(vv[:], PS[:, pb:pb + 2048], 1.0,
                                                   x2[:, (tt % 2) * D:(tt % 2 + 1) * D],
                                                   ALU.mult, ALU.add, accum_out=stats[:, 0:1]).then_inc(cPSW.sem, 1)
                            v.scalar_tensor_tensor(ob[:], vv[:], 1.0, vv[:], ALU.mult, ALU.mult,
                                                   accum_out=stats[:, 1:2])
                            v.tensor_scalar(stats[:, 2:3], stats[:, 0:1], 1.0 / 2048.0, None, ALU.mult)
                            v.tensor_scalar(stats[:, 3:4], stats[:, 1:2], 1.0 / 2048.0, None, ALU.mult).then_inc(cWV.sem, 1)
                            v.wait_ge(cWV.sem, cWV.n + 1)
                            v.tensor_tensor(stats[:, 4:5], stats[:, 2:3], stats[:, 2:3], ALU.mult).then_inc(cWV.sem, 1)
                            v.wait_ge(cWV.sem, cWV.n + 2)
                            v.tensor_tensor(stats[:, 3:4], stats[:, 3:4], stats[:, 4:5], ALU.subtract).then_inc(cWV.sem, 1)
                        cWV.n += 3
                        cPSW.n += 1

                        @block.scalar
                        def _(s):
                            s.wait_ge(cWV.sem, cWV.n)
                            s.activation(stats[:, 5:6], stats[:, 3:4], AF.Ln, bias=cpp[:, C_E5:C_E5 + 1]).then_inc(cSC.sem, 1)
                            s.wait_ge(cSC.sem, cSC.n + 1)
                            s.activation(stats[:, 5:6], stats[:, 5:6], AF.Exp, scale=-0.5).then_inc(cSC.sem, 1)
                        cSC.n += 2

                        @block.vector
                        def _(v):
                            v.wait_ge(cSC.sem, cSC.n)
                            v.wait_ge(cZD.sem, cZD.n)
                            v.tensor_scalar(zz[:], vv[:], stats[:, 2:3], stats[:, 5:6],
                                            ALU.subtract, ALU.mult).then_inc(cWV.sem, 1)
                        cWV.n += 1

                        @block.sync
                        def _(sync, b=b, tt=tt):
                            sync.wait_ge(cWV.sem, cWV.n)
                            sync.dma_start(out=d_zscr[b, tt * 128:(tt + 1) * 128, :], in_=zz[:]).then_inc(cZD.sem, 16)
                        cZD.n += 16

                        for qq in range(4):
                            c0, c1 = qq * 512, (qq + 1) * 512

                            @block.vector
                            def _(v, c0=c0, c1=c1):
                                v.tensor_scalar(qd[:, QU], zz[:, c0:c1], cpp[:, C_RS:C_RS + 1], 448.0,
                                                ALU.mult, ALU.min)
                                v.tensor_single_scalar(qd[:, QU], qd[:, QU], -448.0, ALU.max)
                                v.tensor_tensor(qd[:, QW], qd[:, QU], qd[:, QU], ALU.mult).then_inc(cQA.sem, 1)
                            cQA.n += 1

                            @block.scalar
                            def _(s):
                                s.wait_ge(cQA.sem, cQA.n)
                                s.activation(qd[:, QW], qd[:, QW], AF.Ln, bias=cpp[:, C_C40:C_C40 + 1]).then_inc(cQB.sem, 1)
                            cQB.n += 1

                            @block.vector
                            def _(v):
                                v.wait_ge(cQB.sem, cQB.n)
                                v.tensor_scalar(qd[:, QE], qd[:, QW], LN2INV2, -3.5, ALU.mult, ALU.add)
                                v.tensor_scalar(qd[:, QA32], qd[:, QE], C32, -C32, ALU.add, ALU.add)
                                v.tensor_scalar(qd[:, QA52], qd[:, QE], C52, -C52, ALU.add, ALU.add)
                                v.tensor_tensor(qd[:, QA52], qd[:, QA52], qd[:, QA32], ALU.subtract)
                                v.scalar_tensor_tensor(qd[:, QE], qd[:, QA52], cpp[:, C_TAU:C_TAU + 1],
                                                       qd[:, QA32], ALU.mult, ALU.add)
                                v.tensor_single_scalar(qd[:, QE], qd[:, QE], -9.0, ALU.max).then_inc(cQA.sem, 1)
                            cQA.n += 1

                            @block.scalar
                            def _(s):
                                s.wait_ge(cQA.sem, cQA.n)
                                s.activation(qd[:, QQ], qd[:, QE], AF.Exp,
                                             bias=cpp[:, C_N2LN2:C_N2LN2 + 1], scale=cpp[:, C_LN2:C_LN2 + 1])
                                s.activation(qd[:, QQI], qd[:, QE], AF.Exp,
                                             scale=cpp[:, C_NLN2:C_NLN2 + 1]).then_inc(cQB.sem, 1)
                            cQB.n += 1

                            @block.vector
                            def _(v, c0=c0, c1=c1):
                                v.wait_ge(cQB.sem, cQB.n)
                                if c0 == 0:
                                    v.wait_ge(cQT.sem, cQT.n)
                                v.tensor_tensor(qd[:, QQI], qd[:, QU], qd[:, QQI], ALU.mult)
                                v.tensor_scalar(qd[:, QA32], qd[:, QQI], C32, -C32, ALU.add, ALU.add)
                                v.tensor_scalar(qd[:, QA52], qd[:, QQI], C52, -C52, ALU.add, ALU.add)
                                v.tensor_tensor(qd[:, QA52], qd[:, QA52], qd[:, QA32], ALU.subtract)
                                v.scalar_tensor_tensor(qd[:, QE], qd[:, QA52], cpp[:, C_TAU:C_TAU + 1],
                                                       qd[:, QA32], ALU.mult, ALU.add)
                                v.tensor_tensor(xqb[:, c0:c1], qd[:, QE], qd[:, QQ], ALU.mult).then_inc(cWV.sem, 1)
                            cWV.n += 1

                            @block.sync
                            def _(s, j=j, hf=hf, b=b, qq=qq, cWVt=cWV.n, cQ8t=cQ8.n):
                                s.wait_ge(cWV.sem, cWVt)
                                s.wait_ge(cQ8.sem, cQ8t)
                                if qq == 0 and j == 0 and (hf == 1 or b > 0):
                                    s.wait_ge(cFM.sem, cFM.n)
                                dst = xqs[:, qq * 512:(qq + 1) * 512].rearrange("p (t c) -> p t c", c=128)
                                src = xqb[:, qq * 512:(qq + 1) * 512].rearrange("p (t c) -> p t c", c=128)
                                s.dma_start_transpose(dst, src).then_inc(cQT.sem, 16)
                            cQT.n += 16

                        @block.scalar
                        def _(s, j=j):
                            s.wait_ge(cQT.sem, cQT.n)
                            src = xqs[:].rearrange("p (f c) -> p f c", c=128)
                            dst = xqT8[:].rearrange("p (f c) -> p f c", c=1024)[:, :, j * 128:(j + 1) * 128]
                            s.copy(dst, src).then_inc(cQ8.sem, 1)
                        cQ8.n += 1

                        if DBG and b == 0 and hf == 0:
                            @block.sync
                            def _(sync, j=j):
                                sync.wait_ge(cWV.sem, cWV.n)
                                sync.dma_start(out=d_xqdbg[j], in_=xqb[:]).then_inc(cO.sem, 16)
                            cO.n += 16

                    if DBG and b == 0 and hf == 0:
                        @block.sync
                        def _(sync):
                            sync.wait_ge(cQ8.sem, cQ8.n)
                            sync.dma_start(out=d_xq8dbg, in_=xqT8[:]).then_inc(cO.sem, 16)
                            sync.dma_start(out=d_cppdbg, in_=cpp[:]).then_inc(cO.sem, 16)
                        cO.n += 32

                    Q8_HALF = cQ8.n
                    for j in range(8 if PHASES >= 5 else 0):
                        tt = hf * 8 + j

                        @block.sync
                        def _(sync, b=b, tt=tt, j=j, PSW_T=cPSW.n, PSF_T=cPSF.n):
                            if j < 2:
                                sync.wait_ge(cPSW.sem, PSW_T)
                            else:
                                sync.wait_ge(cPSF.sem, PSF_T - 1)
                            sync.dma_start(out=x2[:, (j % 2) * D:(j % 2 + 1) * D],
                                           in_=d_zscr[b, tt * 128:(tt + 1) * 128, :]).then_inc(cZL.sem, 16)
                        cZL.n += 16

                        @block.tensor
                        def _(t, tt=tt, j=j, PSW_T=cPSW.n, PSF_T=cPSF.n):
                            t.wait_ge(cQ8.sem, Q8_HALF - 7 + j)
                            if j == 0:
                                t.wait_ge(cPSW.sem, PSW_T)
                            if j >= 2:
                                t.wait_ge(cPSF.sem, PSF_T - 1)
                            for ft in range(FT):
                                for n in range(4):
                                    ins = t.matmul(PS[:, (tt % 2) * 2048 + n * 512:(tt % 2) * 2048 + (n + 1) * 512],
                                                   xqT8[:, ft * 1024 + j * 128: ft * 1024 + j * 128 + 128],
                                                   w8[:, ft * D + n * 512: ft * D + (n + 1) * 512],
                                                   start=(ft == 0), stop=(ft == FT - 1))
                            ins.then_inc(cFM.sem, 1)
                        cFM.n += 1

                        @block.vector
                        def _(v, tt=tt, j=j):
                            v.wait_ge(cFM.sem, cFM.n)
                            v.wait_ge(cZL.sem, cZL.n)
                            v.wait_ge(cO.sem, cO.n)
                            v.wait_ge(cZD.sem, cZD.n)
                            pb = (tt % 2) * 2048
                            v.scalar_tensor_tensor(vv[:], PS[:, pb:pb + 2048], cpp[:, C_DEQ:C_DEQ + 1],
                                                   x2[:, (j % 2) * D:(j % 2 + 1) * D],
                                                   ALU.mult, ALU.add, accum_out=stats[:, 0:1]).then_inc(cPSF.sem, 1)
                            v.scalar_tensor_tensor(zz[:], vv[:], 1.0, vv[:], ALU.mult, ALU.mult,
                                                   accum_out=stats[:, 1:2])
                            v.tensor_scalar(stats[:, 2:3], stats[:, 0:1], 1.0 / 2048.0, None, ALU.mult)
                            v.tensor_scalar(stats[:, 3:4], stats[:, 1:2], 1.0 / 2048.0, None, ALU.mult).then_inc(cWV.sem, 1)
                            v.wait_ge(cWV.sem, cWV.n + 1)
                            v.tensor_tensor(stats[:, 4:5], stats[:, 2:3], stats[:, 2:3], ALU.mult).then_inc(cWV.sem, 1)
                            v.wait_ge(cWV.sem, cWV.n + 2)
                            v.tensor_tensor(stats[:, 3:4], stats[:, 3:4], stats[:, 4:5], ALU.subtract).then_inc(cWV.sem, 1)
                        cWV.n += 3
                        cPSF.n += 1

                        @block.scalar
                        def _(s):
                            s.wait_ge(cWV.sem, cWV.n)
                            s.activation(stats[:, 5:6], stats[:, 3:4], AF.Ln, bias=cpp[:, C_E5:C_E5 + 1]).then_inc(cSC.sem, 1)
                            s.wait_ge(cSC.sem, cSC.n + 1)
                            s.activation(stats[:, 5:6], stats[:, 5:6], AF.Exp, scale=-0.5).then_inc(cSC.sem, 1)
                        cSC.n += 2

                        @block.vector
                        def _(v):
                            v.wait_ge(cSC.sem, cSC.n)
                            v.tensor_scalar(ob[:], vv[:], stats[:, 2:3], stats[:, 5:6],
                                            ALU.subtract, ALU.mult).then_inc(cWV.sem, 1)
                        cWV.n += 1

                        @block.sync
                        def _(sync, b=b, tt=tt):
                            sync.wait_ge(cWV.sem, cWV.n)
                            sync.dma_start(out=d_out[b, tt * 128:(tt + 1) * 128, :], in_=ob[:]).then_inc(cO.sem, 16)
                        cO.n += 16

        @block.sync
        def _(sync):
            for ch in C.values():
                if ch.n > 0:
                    sync.wait_ge(ch.sem, ch.n)

    return nc


def _host_layouts(inputs):
    wq_b = np.ascontiguousarray(np.asarray(inputs["L0_attn_wq_b"], dtype=np.float32))
    wkv_a = np.ascontiguousarray(np.asarray(inputs["L0_attn_wkv_a"], dtype=np.float32))
    wkv_b = np.ascontiguousarray(np.asarray(inputs["L0_attn_wkv_b"], dtype=np.float32))
    perm = np.concatenate([np.arange(0, 64, 2), np.arange(1, 64, 2)])
    perm_r = np.concatenate([np.arange(1, 64, 2), np.arange(0, 64, 2)])
    wqbN = np.empty((64, NH * 64), np.float32)
    wqbX = np.empty((64, NH * 64), np.float32)
    wqbR = np.empty((64, NH * 64), np.float32)
    wkbN = np.empty((64, NH * 64), np.float32)
    wkbV = np.empty((64, NH * 64), np.float32)
    for h in range(NH):
        base = h * 128
        wqbN[:, h * 64:(h + 1) * 64] = wq_b[:, base:base + 64]
        wqbX[:, h * 64:(h + 1) * 64] = wq_b[:, base + 64 + perm]
        wqbR[:, h * 64:(h + 1) * 64] = wq_b[:, base + 64 + perm_r]
        wkbN[:, h * 64:(h + 1) * 64] = wkv_b[:, base:base + 64]
        wkbV[:, h * 64:(h + 1) * 64] = wkv_b[:, base + 64:base + 128]
    wkva = np.empty((D, 128), np.float32)
    wkva[:, 0:64] = wkv_a[:, 0:64]
    wkva[:, 64:128] = wkv_a[:, 64 + perm]
    w8 = np.asarray(inputs["L0_ffn_weight"])
    w8T = np.ascontiguousarray(w8.view(np.uint8).T).view(ml_dtypes.float8_e4m3)
    return wqbN, wqbX, wqbR, wkbN, wkbV, wkva, w8T


_NC_CACHE = {}


def kernel(**inputs):
    x = np.ascontiguousarray(np.asarray(inputs["L0_attn_hidden_states"], dtype=np.float32))
    wqbN, wqbX, wqbR, wkbN, wkbV, wkva, w8T = _host_layouts(inputs)
    cos4, sin4 = _yarn_tables()
    wqa = np.ascontiguousarray(np.asarray(inputs["L0_attn_wq_a"], dtype=np.float32))
    qnw = np.ascontiguousarray(np.asarray(inputs["L0_attn_q_a_norm_w"], dtype=np.float32).reshape(64, 1))
    knw = np.ascontiguousarray(np.asarray(inputs["L0_attn_kv_a_norm_w"], dtype=np.float32).reshape(64, 1))
    wo = np.ascontiguousarray(np.asarray(inputs["L0_attn_wo"], dtype=np.float32))
    sw = np.ascontiguousarray(np.repeat(np.asarray(inputs["L0_ffn_weight_scale"], dtype=np.float32).reshape(1, 1), 128, axis=0))
    s_in = np.asarray(inputs["L0_ffn_input_scale"]).reshape(1, 1)
    if s_in.dtype == np.uint16:
        s_in = s_in.view(ml_dtypes.bfloat16)
    assert s_in.dtype == ml_dtypes.bfloat16, s_in.dtype
    s_in = np.ascontiguousarray(np.repeat(s_in, 128, axis=0))

    if "nc" not in _NC_CACHE:
        _NC_CACHE["nc"] = build_nc()
    nc = _NC_CACHE["nc"]

    common = {
        "wqa": wqa, "wqbN": wqbN, "wqbX": wqbX, "wqbR": wqbR,
        "wkva": wkva, "wkbN": wkbN, "wkbV": wkbV,
        "qnw": qnw, "knw": knw, "wo": wo, "w8T": w8T,
        "sw": sw, "sin_": s_in, "cos4": cos4, "sin4": sin4,
    }
    in_maps = []
    for c in range(NCORES):
        m = dict(common)
        m["x"] = np.ascontiguousarray(x[c * NB:(c + 1) * NB])
        in_maps.append(m)

    res = run_bass_kernel_spmd(nc, in_maps, core_ids=list(range(NCORES)))
    out = np.concatenate([np.asarray(r["out"]) for r in res.results], axis=0)
    return out.astype(np.float32, copy=False)


# version 15 (final)
# speedup vs baseline: 1.2518x; 1.0013x over previous
import math
import numpy as np
import ml_dtypes
from contextlib import ExitStack

import concourse.bass as bass
import concourse.mybir as mybir
from concourse.bass_utils import run_bass_kernel_spmd

F32 = mybir.dt.float32
BF16 = mybir.dt.bfloat16
FP8 = mybir.dt.float8e4
AF = mybir.ActivationFunctionType
ALU = mybir.AluOpType

NCORES = 8
NB = 2
S = 2048
D = 2048
NH = 16
NP = 8
TT = 16
FT = 16

C32 = float(np.float32(1.5 * 2.0 ** 23))
C52 = float(np.float32(1.5 * 2.0 ** 52))
LN2INV2 = float(np.float32(1.0 / (2.0 * math.log(2.0))))
SM_SCALE = float(np.float32(128.0 ** -0.5))


def _yarn_tables():
    ROPE_BASE, FACTOR = 10000.0, 40.0
    BETA_FAST, BETA_SLOW, ORIG_MAX, MSCALE = 32.0, 1.0, 4096, 0.707
    dim = 64
    exponents = np.arange(0, dim, 2, dtype=np.float64) / dim
    freq_extra = 1.0 / ROPE_BASE ** exponents
    freq_inter = 1.0 / (FACTOR * ROPE_BASE ** exponents)

    def corr_dim(num_rot):
        return dim * math.log(ORIG_MAX / (num_rot * 2 * math.pi)) / (2 * math.log(ROPE_BASE))

    low = max(math.floor(corr_dim(BETA_FAST)), 0)
    high = min(math.ceil(corr_dim(BETA_SLOW)), dim - 1)
    if low == high:
        high = high + 0.001
    ramp = np.clip((np.arange(dim // 2, dtype=np.float64) - low) / (high - low), 0.0, 1.0)
    inv_freq = freq_inter * ramp + freq_extra * (1.0 - ramp)
    t = np.arange(S, dtype=np.float64)
    freqs = np.outer(t, inv_freq)

    def yarn_get_mscale(scale, mscale=1.0):
        if scale <= 1:
            return 1.0
        return 0.1 * mscale * math.log(scale) + 1.0

    msc = yarn_get_mscale(FACTOR, MSCALE) / yarn_get_mscale(FACTOR, 0.0)
    cosF = (np.cos(freqs) * msc).astype(np.float32).T
    sinF = (np.sin(freqs) * msc).astype(np.float32).T
    return (np.ascontiguousarray(np.tile(cosF, (4, 1))),
            np.ascontiguousarray(np.tile(sinF, (4, 1))))


class Ch:
    def __init__(self, stack, nc, name):
        self.sem = stack.enter_context(nc.semaphore(name))
        self.n = 0


PHASES = 99
E_STEPS = 99
E_SUB = 99
E_GEN = 99
DBG = False


def build_nc():
    nc = bass.Bass("TRN2", target_bir_lowering=False, detect_race_conditions=False)

    d_x = nc.dram_tensor("x", [NB, S, D], F32, kind="ExternalInput").ap()
    d_wqa = nc.dram_tensor("wqa", [D, 64], F32, kind="ExternalInput").ap()
    d_wqbN = nc.dram_tensor("wqbN", [64, NH * 64], F32, kind="ExternalInput").ap()
    d_wqbX = nc.dram_tensor("wqbX", [64, NH * 64], F32, kind="ExternalInput").ap()
    d_wqbR = nc.dram_tensor("wqbR", [64, NH * 64], F32, kind="ExternalInput").ap()
    d_wkva = nc.dram_tensor("wkva", [D, 128], F32, kind="ExternalInput").ap()
    d_wkbN = nc.dram_tensor("wkbN", [64, NH * 64], F32, kind="ExternalInput").ap()
    d_wkbV = nc.dram_tensor("wkbV", [64, NH * 64], F32, kind="ExternalInput").ap()
    d_qnw = nc.dram_tensor("qnw", [64, 1], F32, kind="ExternalInput").ap()
    d_knw = nc.dram_tensor("knw", [64, 1], F32, kind="ExternalInput").ap()
    d_wo = nc.dram_tensor("wo", [NH * 64, D], F32, kind="ExternalInput").ap()
    d_w8T = nc.dram_tensor("w8T", [D, D], FP8, kind="ExternalInput").ap()
    d_sw = nc.dram_tensor("sw", [128, 1], F32, kind="ExternalInput").ap()
    d_sin = nc.dram_tensor("sin_", [128, 1], BF16, kind="ExternalInput").ap()
    d_cos4 = nc.dram_tensor("cos4", [128, S], F32, kind="ExternalInput").ap()
    d_sin4 = nc.dram_tensor("sin4", [128, S], F32, kind="ExternalInput").ap()
    d_out = nc.dram_tensor("out", [NB, S, D], F32, kind="ExternalOutput").ap()
    scrk = "ExternalOutput" if DBG else "Internal"
    d_xqdbg = nc.dram_tensor("xqdbg", [8, 128, D], BF16, kind="ExternalOutput").ap() if DBG else None
    d_cppdbg = nc.dram_tensor("cppdbg", [128, 16], F32, kind="ExternalOutput").ap() if DBG else None
    d_xq8dbg = nc.dram_tensor("xq8dbg", [128, FT * 1024], FP8, kind="ExternalOutput").ap() if DBG else None
    d_zscr = nc.dram_tensor("zscr", [NB, S, D], F32, kind=scrk).ap()
    d_qscr = nc.dram_tensor("qscr", [NH, 128, S], BF16, kind=scrk).ap()
    d_kscr = nc.dram_tensor("kscr", [NH, 128, S], BF16, kind=scrk).ap()

    with ExitStack() as ctx:
        def sbt(name, shape, dt=F32):
            return ctx.enter_context(nc.sbuf_tensor(name, shape, dt))

        w8 = sbt("w8", [128, FT * D], FP8)
        cos4 = sbt("cos4_b", [128, S], BF16)
        sin4 = sbt("sin4_b", [128, S], BF16)
        wqa_b = sbt("wqa_b", [128, FT * 64], BF16)
        wkva_b = sbt("wkva_b", [128, FT * 128], BF16)
        wqbN_b = sbt("wqbN_b", [64, NH * 64], BF16)
        wqbX_b = sbt("wqbX_b", [64, NH * 64], BF16)
        wqbR_b = sbt("wqbR_b", [64, NH * 64], BF16)
        wkbN_b = sbt("wkbN_b", [64, NH * 64], BF16)
        wkbV_b = sbt("wkbV_b", [64, NH * 64], BF16)
        ones_b = sbt("ones_b", [64, 128], BF16)
        cpp = sbt("cpp", [128, 16], F32)
        qnw_s = sbt("qnw_s", [64, 1], F32)
        knw_s = sbt("knw_s", [64, 1], F32)
        C_TAU, C_LN2, C_NLN2, C_N2LN2, C_RS, C_DEQ, C_E6, C_E5, C_C40, C_TWO = range(10)

        bigA = sbt("bigA", [128, 16384], BF16)
        stats = sbt("stats", [128, 8], F32)

        PS = ctx.enter_context(nc.psum_tensor("PS", [128, 4096], F32))

        chnames = ["cLD", "cSET", "cXa", "cXb", "cTRa", "cTRb", "cTM", "cSC", "cVE", "cQKD",
                   "cQKLa", "cQKLb", "cQKF", "cL", "cE2", "cAV", "cEP", "cWM", "cX2a", "cX2b",
                   "cZD", "cZL", "cQT", "cQ8", "cFM", "cO", "cWV", "cQA", "cQB", "cWL", "cPSW", "cPSF"]
        C = {n: Ch(ctx, nc, n) for n in chnames}
        (cLD, cSET, cXa, cXb, cTRa, cTRb, cTM, cSC, cVE, cQKD, cQKLa, cQKLb, cQKF,
         cL, cE2, cAV, cEP, cWM, cX2a, cX2b, cZD, cZL, cQT, cQ8, cFM, cO, cWV,
         cQA, cQB, cWL, cPSW, cPSF) = (C[n] for n in chnames)

        block = ctx.enter_context(nc.Block())

        @block.scalar
        def _(s):
            from concourse.hw_specs import get_activation_tables
            for idx, funcs in enumerate(get_activation_tables(nc.m.arch).values()):
                if AF.Ln in funcs and AF.Exp in funcs:
                    s.add_instruction(mybir.InstLoadActFuncSet(
                        name=nc.get_next_instruction_name(),
                        act_func_set_id=idx, ins=[], outs=[]))
                    break

        @block.sync
        def _(sync):
            sync.dma_start(out=w8[:].rearrange("p (t c) -> p t c", c=D),
                           in_=d_w8T.rearrange("(t p) c -> p t c", p=128)).then_inc(cLD.sem, 16)
        cLD.n += 16

        with nc.sbuf_tensor("stage32", [128, S], F32) as stage32, \
             nc.sbuf_tensor("sin_b", [128, 1], BF16) as sin_b:

            stage_jobs = [
                (d_cos4, cos4, 128, S),
                (d_sin4, sin4, 128, S),
                (d_wqa.rearrange("(t p) c -> p t c", p=128), wqa_b, 128, FT * 64),
                (d_wkva.rearrange("(t p) c -> p t c", p=128), wkva_b, 128, FT * 128),
                (d_wqbN, wqbN_b, 64, NH * 64),
                (d_wqbX, wqbX_b, 64, NH * 64),
                (d_wkbN, wkbN_b, 64, NH * 64),
                (d_wkbV, wkbV_b, 64, NH * 64),
            ]
            for src, dst, p, width in stage_jobs:
                @block.sync
                def _(sync, src=src, p=p, width=width):
                    sync.wait_ge(cSET.sem, cSET.n)
                    o = stage32[0:p, 0:width]
                    if len(src.shape) == 3:
                        o = o.rearrange("p (t c) -> p t c", t=src.shape[1])
                    sync.dma_start(out=o, in_=src).then_inc(cLD.sem, 16)
                cLD.n += 16

                @block.vector
                def _(v, dst=dst, p=p, width=width):
                    v.wait_ge(cLD.sem, cLD.n)
                    v.tensor_copy(dst[0:p, 0:width], stage32[0:p, 0:width]).then_inc(cSET.sem, 1)
                cSET.n += 1

            @block.sync
            def _(sync):
                sync.wait_ge(cSET.sem, cSET.n)
                sync.dma_start(out=stage32[0:64, 0:NH * 64], in_=d_wqbR).then_inc(cLD.sem, 16)
            cLD.n += 16

            @block.vector
            def _(v):
                v.wait_ge(cLD.sem, cLD.n)
                si = stage32[0:64, 0:NH * 64].rearrange("p (h c) -> p h c", c=64)
                so = wqbR_b[0:64, :].rearrange("p (h c) -> p h c", c=64)
                v.tensor_scalar(so[:, :, 0:32], si[:, :, 0:32], -1.0, None, ALU.mult)
                v.tensor_copy(so[:, :, 32:64], si[:, :, 32:64]).then_inc(cSET.sem, 1)
            cSET.n += 1

            @block.sync
            def _(sync):
                sync.wait_ge(cSET.sem, cSET.n)
                sync.dma_start(out=cpp[:, C_RS:C_RS + 1], in_=d_sw).then_inc(cLD.sem, 16)
                sync.dma_start(out=sin_b[:], in_=d_sin).then_inc(cLD.sem, 16)
                sync.dma_start(out=qnw_s[:], in_=d_qnw).then_inc(cLD.sem, 16)
                sync.dma_start(out=knw_s[:], in_=d_knw).then_inc(cLD.sem, 16)
            cLD.n += 64

            @block.vector
            def _(v):
                v.memset(cpp[:, 10:16], 0.0)
                v.wait_ge(cLD.sem, cLD.n)
                v.memset(cpp[:, C_TAU:C_TAU + 1], 2.0 ** 52).then_inc(cSET.sem, 1)
                v.memset(cpp[:, C_TWO:C_TWO + 1], 2.0)
                v.memset(cpp[:, C_E6:C_E6 + 1], 1e-6)
                v.memset(cpp[:, C_E5:C_E5 + 1], 1e-5)
                v.memset(cpp[:, C_C40:C_C40 + 1], 2.0 ** -40)
                v.tensor_copy(cpp[:, C_DEQ:C_DEQ + 1], sin_b[:]).then_inc(cSET.sem, 1)
                v.wait_ge(cSET.sem, cSET.n + 2)
                v.tensor_scalar(cpp[:, C_TAU:C_TAU + 1], cpp[:, C_TAU:C_TAU + 1], 1.0, -(2.0 ** 52), ALU.add, ALU.add)
                v.tensor_tensor(cpp[:, C_DEQ:C_DEQ + 1], cpp[:, C_DEQ:C_DEQ + 1], cpp[:, C_RS:C_RS + 1], ALU.mult).then_inc(cSET.sem, 1)
                v.wait_ge(cSET.sem, cSET.n + 3)
                v.tensor_scalar(cpp[:, C_DEQ:C_DEQ + 1], cpp[:, C_DEQ:C_DEQ + 1], 4.0, None, ALU.mult)
                v.tensor_copy(cpp[:, C_RS:C_RS + 1], sin_b[:]).then_inc(cSET.sem, 1)
                v.wait_ge(cSET.sem, cSET.n + 4)
                v.reciprocal(cpp[:, C_RS:C_RS + 1], cpp[:, C_RS:C_RS + 1])
                v.tensor_scalar(qnw_s[:], qnw_s[:], SM_SCALE, None, ALU.mult)
                v.memset(ones_b[:], 1.0).then_inc(cSET.sem, 1)
            cSET.n += 5

            @block.scalar
            def _(s):
                s.wait_ge(cSET.sem, cSET.n)
                s.activation(cpp[:, C_LN2:C_LN2 + 1], cpp[:, C_TWO:C_TWO + 1], AF.Ln)
                s.mul(cpp[:, C_NLN2:C_NLN2 + 1], cpp[:, C_LN2:C_LN2 + 1], -1.0)
                s.mul(cpp[:, C_N2LN2:C_N2LN2 + 1], cpp[:, C_LN2:C_LN2 + 1], -2.0).then_inc(cSET.sem, 1)
            cSET.n += 1

        SETUP_DONE = cSET.n

        for b in range(NB if PHASES >= 6 else (1 if PHASES >= 2 else 0)):
            with ExitStack() as ea:
                def eat(name, shape, dt):
                    return ea.enter_context(nc.sbuf_tensor(name + str(b), shape, dt))
                vbuf = eat("vbuf", [128, TT * 1536], BF16)
                qp = [eat("qp0", [128, S], BF16), eat("qp1", [128, S], BF16)]
                kp = [eat("kp0", [128, S], BF16), eat("kp1", [128, S], BF16)]
                a2 = [eat(f"a2_{i}", [128, 1024], BF16) for i in range(3)]
                rcp_full = eat("rcp", [64, S], F32)
                rcp = rcp_full[0:64, 0:1024]
                qlat_n = eat("qlat_n", [64, S], BF16)
                kv_n = eat("kv_n", [64, S], BF16)
                kpe_r = eat("kpe_r", [64, S], BF16)

                with ExitStack() as ee:
                    def eet(name, shape, dt):
                        return ee.enter_context(nc.sbuf_tensor(name + str(b), shape, dt))
                    xbf = eet("xbf", [128, 2 * D], BF16)
                    kv_bf = eet("kv_bf", [64, S], BF16)
                    ql_bf = eet("ql_bf", [64, S], BF16)
                    kpe_bf = eet("kpe_bf", [64, S], BF16)
                    krot_bf = eet("krot_bf", [64, S], BF16)
                    tln = rcp_full
                    t1 = eet("t1", [128, 512], F32)
                    t2 = eet("t2", [128, 512], F32)
                    sq_bf = krot_bf

                    for hf in range(2):
                        for j in range(8):
                            tt = hf * 8 + j

                            cXp = cXa if tt % 2 == 0 else cXb
                            cTRp = cTRa if tt % 2 == 0 else cTRb

                            @block.gpsimd
                            def _(g, b=b, tt=tt, j=j, hf=hf, cXp=cXp, cTRp=cTRp):
                                if b > 0 and hf == 0 and j == 0:
                                    g.wait_ge(cO.sem, cO.n)
                                    g.wait_ge(cWV.sem, cWV.n)
                                    g.wait_ge(cSC.sem, cSC.n)
                                    g.wait_ge(cVE.sem, cVE.n)
                                if tt >= 2:
                                    g.wait_ge(cTRp.sem, cTRp.n)
                                g.dma_start(out=xbf[:, (j % 2) * D:(j % 2 + 1) * D],
                                            in_=d_x[b, tt * 128:(tt + 1) * 128, :]).then_inc(cXp.sem, 16)
                            cXp.n += 16

                            @block.sync
                            def _(s, j=j, b=b, hf=hf, cXp=cXp, cTRp=cTRp):
                                s.wait_ge(cXp.sem, cXp.n)
                                if j == 0:
                                    s.wait_ge(cTM.sem, cTM.n)
                                    s.wait_ge(cWM.sem, cWM.n)
                                dst = bigA[:].rearrange("p (t c) -> p t c", c=1024)[:, :, j * 128:(j + 1) * 128]
                                src = xbf[:, (j % 2) * D:(j % 2 + 1) * D].rearrange("p (t c) -> p t c", c=128)
                                s.dma_start_transpose(dst, src).then_inc(cTRp.sem, 16)
                            cTRp.n += 16

                        @block.tensor
                        def _(t, hf=hf):
                            if E_STEPS < 2:
                                return
                            t.wait_ge(cTRa.sem, cTRa.n)
                            t.wait_ge(cTRb.sem, cTRb.n)
                            if hf == 0:
                                t.wait_ge(cSET.sem, SETUP_DONE)
                                t.wait_ge(cSC.sem, cSC.n)
                                t.wait_ge(cWV.sem, cWV.n)
                            for n in range(2):
                                for ft in range(FT):
                                    t.matmul(PS[:, hf * 1024 + n * 512: hf * 1024 + (n + 1) * 512],
                                             wkva_b[:, ft * 128:(ft + 1) * 128],
                                             bigA[:, ft * 1024 + n * 512: ft * 1024 + (n + 1) * 512],
                                             start=(ft == 0), stop=(ft == FT - 1))
                            for n in range(2):
                                for ft in range(FT):
                                    ins = t.matmul(PS[0:64, 2048 + hf * 1024 + n * 512: 2048 + hf * 1024 + (n + 1) * 512],
                                                   wqa_b[:, ft * 64:(ft + 1) * 64],
                                                   bigA[:, ft * 1024 + n * 512: ft * 1024 + (n + 1) * 512],
                                                   start=(ft == 0), stop=(ft == FT - 1))
                            ins.then_inc(cTM.sem, 1)
                        if E_STEPS >= 2:
                            cTM.n += 1
                    if E_STEPS < 3:
                        continue

                    @block.scalar
                    def _(s):
                        s.wait_ge(cTM.sem, cTM.n)
                        s.copy(kv_bf[:], PS[0:64, 0:2048])
                        s.copy(kpe_bf[:], PS[64:128, 0:2048])
                        s.copy(ql_bf[:], PS[0:64, 2048:4096])
                        s.mul(krot_bf[0:32, :], kpe_bf[32:64, :], -1.0)
                        s.copy(krot_bf[32:64, :], kpe_bf[0:32, :]).then_inc(cSC.sem, 1)
                    cSC.n += 1
                    if E_STEPS < 4:
                        continue

                    @block.vector
                    def _(v):
                        v.wait_ge(cSC.sem, cSC.n)
                        if cSET.n == SETUP_DONE:
                            v.wait_ge(cSET.sem, SETUP_DONE)
                        for n in range(4):
                            sl = slice(n * 512, (n + 1) * 512)
                            v.tensor_tensor(t1[0:64, :], kpe_bf[:, sl], cos4[0:64, sl], ALU.mult)
                            v.tensor_tensor(t2[0:64, :], krot_bf[:, sl], sin4[0:64, sl], ALU.mult)
                            ins = v.tensor_tensor(kpe_r[:, sl], t1[0:64, :], t2[0:64, :], ALU.add)
                        ins.then_inc(cVE.sem, 1)
                    cVE.n += 1

                    if E_SUB < 2:
                        continue

                    @block.vector
                    def _(v):
                        v.tensor_tensor(sq_bf[:], ql_bf[:], ql_bf[:], ALU.mult).then_inc(cVE.sem, 1)
                    cVE.n += 1
                    if E_SUB < 3:
                        continue

                    @block.tensor
                    def _(t):
                        t.wait_ge(cVE.sem, cVE.n)
                        for n in range(4):
                            ins = t.matmul(PS[:, n * 512:(n + 1) * 512], ones_b[:],
                                           sq_bf[:, n * 512:(n + 1) * 512], start=True, stop=True)
                        ins.then_inc(cTM.sem, 1)
                    cTM.n += 1

                    if E_SUB >= 4:
                        @block.scalar
                        def _(s):
                            s.wait_ge(cTM.sem, cTM.n)
                            s.activation(tln[:], PS[0:64, 0:2048], AF.Ln, bias=cpp[0:64, C_E6:C_E6 + 1], scale=1.0 / 64.0)
                            s.activation(tln[:], tln[:], AF.Exp, scale=-0.5).then_inc(cSC.sem, 1)
                        cSC.n += 1
                    if E_SUB < 5:
                        continue

                    @block.vector
                    def _(v):
                        v.wait_ge(cSC.sem, cSC.n)
                        v.tensor_tensor(tln[:], ql_bf[:], tln[:], ALU.mult)
                        v.tensor_scalar(qlat_n[:], tln[:], qnw_s[:], None, ALU.mult)
                        v.tensor_tensor(sq_bf[:], kv_bf[:], kv_bf[:], ALU.mult).then_inc(cVE.sem, 1)
                    cVE.n += 1
                    if E_SUB < 6:
                        continue

                    @block.tensor
                    def _(t):
                        t.wait_ge(cVE.sem, cVE.n)
                        for n in range(4):
                            ins = t.matmul(PS[:, 2048 + n * 512: 2048 + (n + 1) * 512], ones_b[:],
                                           sq_bf[:, n * 512:(n + 1) * 512], start=True, stop=True)
                        ins.then_inc(cTM.sem, 1)
                    cTM.n += 1

                    @block.scalar
                    def _(s):
                        s.wait_ge(cTM.sem, cTM.n)
                        s.activation(tln[:], PS[0:64, 2048:4096], AF.Ln, bias=cpp[0:64, C_E6:C_E6 + 1], scale=1.0 / 64.0)
                        s.activation(tln[:], tln[:], AF.Exp, scale=-0.5).then_inc(cSC.sem, 1)
                    cSC.n += 1

                    @block.vector
                    def _(v):
                        v.wait_ge(cSC.sem, cSC.n)
                        v.tensor_tensor(tln[:], kv_bf[:], tln[:], ALU.mult)
                        v.tensor_scalar(kv_n[:], tln[:], knw_s[:], None, ALU.mult).then_inc(cVE.sem, 1)
                    cVE.n += 1
                    if E_STEPS < 5:
                        continue

                    for tt in range(TT):
                        @block.tensor
                        def _(t, tt=tt):
                            t.wait_ge(cVE.sem, cVE.n)
                            t.wait_ge(cSC.sem, cSC.n)
                            for n in range(2):
                                ins = t.matmul(PS[:, (tt % 2) * 1024 + n * 512:(tt % 2) * 1024 + (n + 1) * 512],
                                               kv_n[:, tt * 128:(tt + 1) * 128],
                                               wkbV_b[:, n * 512:(n + 1) * 512], start=True, stop=True)
                            ins.then_inc(cTM.sem, 1)
                        cTM.n += 1

                        @block.scalar
                        def _(s, tt=tt, b=b):
                            s.wait_ge(cTM.sem, cTM.n)
                            if tt == 0 and b > 0:
                                s.wait_ge(cAV.sem, cAV.n)
                            src = PS[:, (tt % 2) * 1024:(tt % 2) * 1024 + 1024].rearrange(
                                "p (pr j c) -> p pr j c", pr=8, j=2, c=64)
                            dst = vbuf[:, tt * 1536:(tt + 1) * 1536].rearrange(
                                "p (pr g c) -> p pr g c", pr=8, g=3, c=64)[:, :, 0:3:2, :]
                            s.copy(dst, src).then_inc(cSC.sem, 1)
                        cSC.n += 1

                    if E_STEPS < 6:
                        continue

                    @block.vector
                    def _(v, b=b):
                        if b > 0:
                            v.wait_ge(cAV.sem, cAV.n)
                        ones_ap = vbuf[:].rearrange("p (t pr g c) -> p t pr g c", pr=8, g=3, c=64)[:, :, :, 1:2, :]
                        v.memset(ones_ap, 1.0).then_inc(cVE.sem, 1)
                    cVE.n += 1

                    for p in range(NP if E_STEPS >= 7 else 0):
                        @block.tensor
                        def _(t, p=p):
                            t.wait_ge(cSC.sem, cSC.n)
                            t.wait_ge(cVE.sem, cVE.n)
                            for n in range(4):
                                t.matmul(PS[:, n * 512:(n + 1) * 512], wqbX_b[:, p * 128:(p + 1) * 128],
                                         qlat_n[:, n * 512:(n + 1) * 512], start=True, stop=True)
                            for n in range(4):
                                ins = t.matmul(PS[:, 2048 + n * 512: 2048 + (n + 1) * 512], wqbR_b[:, p * 128:(p + 1) * 128],
                                               qlat_n[:, n * 512:(n + 1) * 512], start=True, stop=True)
                            ins.then_inc(cTM.sem, 1)
                        cTM.n += 1

                        if E_GEN < 2:
                            continue

                        @block.vector
                        def _(v, p=p):
                            v.wait_ge(cTM.sem, cTM.n)
                            if p > 0:
                                v.wait_ge(cQKD.sem, cQKD.n)
                            for n in range(4):
                                sl = slice(n * 512, (n + 1) * 512)
                                v.tensor_tensor(t1[:], PS[:, sl], cos4[:, sl], ALU.mult)
                                ins = v.tensor_tensor(t2[:], PS[:, 2048 + n * 512: 2048 + (n + 1) * 512],
                                                      sin4[:, sl], ALU.mult)
                                if n == 3:
                                    ins.then_inc(cVE.sem, 1)
                                v.tensor_tensor(qp[0][0:64, sl], t1[0:64, :], t2[0:64, :], ALU.add)
                                v.tensor_tensor(qp[1][64:128, sl], t1[64:128, :], t2[64:128, :], ALU.add)
                            v.tensor_copy(kp[0][0:64, :], kpe_r[:])
                            v.tensor_copy(kp[1][64:128, :], kpe_r[:]).then_inc(cVE.sem, 1)
                        cVE.n += 2

                        if E_GEN < 3:
                            continue

                        @block.tensor
                        def _(t, p=p):
                            t.wait_ge(cVE.sem, cVE.n)
                            for n in range(4):
                                t.matmul(PS[:, n * 512:(n + 1) * 512], wqbN_b[:, p * 128:(p + 1) * 128],
                                         qlat_n[:, n * 512:(n + 1) * 512], start=True, stop=True)
                            for n in range(4):
                                ins = t.matmul(PS[:, 2048 + n * 512: 2048 + (n + 1) * 512], wkbN_b[:, p * 128:(p + 1) * 128],
                                               kv_n[:, n * 512:(n + 1) * 512], start=True, stop=True)
                            ins.then_inc(cTM.sem, 1)
                        cTM.n += 1

                        if E_GEN < 4:
                            continue

                        @block.scalar
                        def _(s, p=p):
                            s.wait_ge(cTM.sem, cTM.n)
                            if p > 0:
                                s.wait_ge(cQKD.sem, cQKD.n)
                            s.copy(qp[0][64:128, :], PS[0:64, 0:2048])
                            s.copy(qp[1][0:64, :], PS[64:128, 0:2048])
                            s.copy(kp[0][64:128, :], PS[0:64, 2048:4096])
                            s.copy(kp[1][0:64, :], PS[64:128, 2048:4096]).then_inc(cSC.sem, 1)
                        cSC.n += 1

                        if E_GEN < 5:
                            continue

                        @block.sync
                        def _(sync, p=p):
                            sync.wait_ge(cSC.sem, cSC.n)
                            sync.wait_ge(cVE.sem, cVE.n)
                            sync.dma_start(out=d_qscr[2 * p], in_=qp[0][:]).then_inc(cQKD.sem, 16)
                            sync.dma_start(out=d_qscr[2 * p + 1], in_=qp[1][:]).then_inc(cQKD.sem, 16)
                            sync.dma_start(out=d_kscr[2 * p], in_=kp[0][:]).then_inc(cQKD.sem, 16)
                            sync.dma_start(out=d_kscr[2 * p + 1], in_=kp[1][:]).then_inc(cQKD.sem, 16)
                        cQKD.n += 64

                EXP_BASE, AV_BASE, EP_BASE, L_BASE = cE2.n, cAV.n, cEP.n, cL.n

                def emit_av(h, kt, g, hq):
                    @block.tensor
                    def _(t, h=h, kt=kt, g=g, hq=hq):
                        t.wait_ge(cE2.sem, EXP_BASE + g + 1)
                        if kt == 0 and hq >= 2:
                            t.wait_ge(cEP.sem, EP_BASE + hq - 1)
                        voff = kt * 1536 + (h // 2) * 192 + (h % 2) * 64
                        for n in range(2):
                            ins = t.matmul(PS[:, 2048 + (hq % 2) * 1024 + n * 512:
                                               2048 + (hq % 2) * 1024 + (n + 1) * 512],
                                           vbuf[:, voff:voff + 128],
                                           a2[g % 3][:, n * 512:(n + 1) * 512],
                                           start=(kt == 0), stop=(kt == 15))
                        ins.then_inc(cAV.sem, 1)
                    cAV.n += 1
                    if kt == 15:
                        @block.vector
                        def _(v, h=h, qc=hq % 2, hq=hq):
                            v.wait_ge(cAV.sem, AV_BASE + (hq + 1) * 16)
                            ob_ = 2048 + (hq % 2) * 1024
                            dst_c = (h // 2) * 2048 + qc * 1024
                            if h % 2 == 0:
                                v.reciprocal(rcp, PS[64:128, ob_:ob_ + 1024])
                                v.tensor_tensor(bigA[0:64, dst_c:dst_c + 1024],
                                                PS[0:64, ob_:ob_ + 1024], rcp, ALU.mult).then_inc(cEP.sem, 1)
                            else:
                                v.reciprocal(rcp, PS[0:64, ob_:ob_ + 1024])
                                v.tensor_tensor(bigA[64:128, dst_c:dst_c + 1024],
                                                PS[64:128, ob_:ob_ + 1024], rcp, ALU.mult).then_inc(cEP.sem, 1)
                        cEP.n += 1

                pend_av = None
                for h in range(NH if PHASES >= 3 else 0):
                    cQKLp = cQKLa if h % 2 == 0 else cQKLb

                    @block.sync
                    def _(sync, h=h, cQKLp=cQKLp):
                        if h < 2:
                            sync.wait_ge(cQKD.sem, cQKD.n)
                        else:
                            sync.wait_ge(cL.sem, L_BASE + (h - 1) * 32)
                        sync.dma_start(out=qp[h % 2][:], in_=d_qscr[h]).then_inc(cQKLp.sem, 16)
                        sync.dma_start(out=kp[h % 2][:], in_=d_kscr[h]).then_inc(cQKLp.sem, 16)
                    cQKLp.n += 32
                    QKL_TGT = cQKLp.n

                    for qc in range(2):
                        hq = h * 2 + qc
                        for kt in range(16):
                            g = hq * 16 + kt

                            @block.tensor
                            def _(t, h=h, qc=qc, kt=kt, g=g, cQKLp=cQKLp, QKL_TGT=QKL_TGT):
                                if kt == 0 and qc == 0:
                                    t.wait_ge(cQKLp.sem, QKL_TGT)
                                if g >= 2:
                                    t.wait_ge(cE2.sem, EXP_BASE + g - 1)
                                for n in range(2):
                                    ins = t.matmul(PS[:, (g % 2) * 1024 + n * 512:(g % 2) * 1024 + (n + 1) * 512],
                                                   kp[h % 2][:, kt * 128:(kt + 1) * 128],
                                                   qp[h % 2][:, qc * 1024 + n * 512: qc * 1024 + (n + 1) * 512],
                                                   start=True, stop=True)
                                ins.then_inc(cL.sem, 1)
                            cL.n += 1

                            @block.scalar
                            def _(s, g=g):
                                s.wait_ge(cL.sem, L_BASE + g + 1)
                                if g >= 3:
                                    s.wait_ge(cAV.sem, AV_BASE + g - 2)
                                s.activation(a2[g % 3][:], PS[:, (g % 2) * 1024:(g % 2) * 1024 + 1024],
                                             AF.Exp).then_inc(cE2.sem, 1)
                            cE2.n += 1

                            if pend_av is not None:
                                emit_av(*pend_av)
                            pend_av = (h, kt, g, hq)

                if pend_av is not None:
                    emit_av(*pend_av)
                    pend_av = None

                ATT_DONE_EP = cEP.n
                if PHASES < 4:
                    continue

            with ExitStack() as wf:
                def wft(name, shape, dt):
                    return wf.enter_context(nc.sbuf_tensor(name + str(b), shape, dt))
                wo_bf = wft("wo_bf", [128, 8 * D], BF16)
                x2 = wft("x2", [128, 2 * D], F32)
                vv = wft("vv", [128, D], F32)
                zz = wft("zz", [128, D], F32)
                ob = wft("ob", [128, D], F32)
                xqb = wft("xqb", [128, D], BF16)
                xqs = wft("xqs", [128, D], BF16)
                xqT8 = wft("xqT8", [128, FT * 1024], FP8)
                qd = wft("qd", [128, 7 * 512], F32)
                QU, QW, QE, QA32, QA52, QQ, QQI = [slice(i * 512, (i + 1) * 512) for i in range(7)]

                for p8 in range(8):
                    @block.sync
                    def _(sync, p8=p8):
                        if p8 == 0:
                            sync.wait_ge(cVE.sem, cVE.n)
                            sync.wait_ge(cSC.sem, cSC.n)
                            sync.wait_ge(cQKD.sem, cQKD.n)
                            sync.wait_ge(cQKLa.sem, cQKLa.n)
                            sync.wait_ge(cQKLb.sem, cQKLb.n)
                            sync.wait_ge(cO.sem, cO.n)
                            sync.wait_ge(cEP.sem, cEP.n)
                        sync.wait_ge(cWV.sem, cWV.n)
                        sync.dma_start(out=x2[:, 0:D], in_=d_wo[p8 * 128:(p8 + 1) * 128, :]).then_inc(cWL.sem, 16)
                    cWL.n += 16

                    @block.vector
                    def _(v, p8=p8):
                        v.wait_ge(cWL.sem, cWL.n)
                        v.tensor_copy(wo_bf[:, p8 * D:(p8 + 1) * D], x2[:, 0:D]).then_inc(cWV.sem, 1)
                    cWV.n += 1
                WO_READY = cWV.n

                for hf in range(2):
                    for j in range(8):
                        tt = hf * 8 + j

                        cX2p = cX2a if tt % 2 == 0 else cX2b

                        @block.sync
                        def _(sync, b=b, tt=tt, cX2p=cX2p):
                            sync.wait_ge(cWV.sem, cWV.n)
                            sync.dma_start(out=x2[:, (tt % 2) * D:(tt % 2 + 1) * D],
                                           in_=d_x[b, tt * 128:(tt + 1) * 128, :]).then_inc(cX2p.sem, 16)
                        cX2p.n += 16

                        @block.tensor
                        def _(t, tt=tt, j=j, hf=hf, PSW_T=cPSW.n, PSF_T=cPSF.n):
                            if j == 0 and hf == 0:
                                t.wait_ge(cEP.sem, ATT_DONE_EP)
                                t.wait_ge(cAV.sem, cAV.n)
                                t.wait_ge(cWV.sem, WO_READY)
                            if hf > 0 and j < 2:
                                t.wait_ge(cPSF.sem, PSF_T)
                            if j >= 2:
                                t.wait_ge(cPSW.sem, PSW_T - 1)
                            for p8 in range(8):
                                for n in range(4):
                                    ins = t.matmul(PS[:, (tt % 2) * 2048 + n * 512:(tt % 2) * 2048 + (n + 1) * 512],
                                                   bigA[:, p8 * 2048 + tt * 128: p8 * 2048 + (tt + 1) * 128],
                                                   wo_bf[:, p8 * D + n * 512: p8 * D + (n + 1) * 512],
                                                   start=(p8 == 0), stop=(p8 == 7))
                            ins.then_inc(cWM.sem, 1)
                        cWM.n += 1

                        @block.vector
                        def _(v, tt=tt, cX2p=cX2p):
                            v.wait_ge(cWM.sem, cWM.n)
                            v.wait_ge(cX2p.sem, cX2p.n)
                            v.wait_ge(cO.sem, cO.n)
                            pb = (tt % 2) * 2048
                            v.scalar_tensor_tensor(vv[:], PS[:, pb:pb + 2048], 1.0,
                                                   x2[:, (tt % 2) * D:(tt % 2 + 1) * D],
                                                   ALU.mult, ALU.add, accum_out=stats[:, 0:1]).then_inc(cPSW.sem, 1)
                            v.scalar_tensor_tensor(ob[:], vv[:], 1.0, vv[:], ALU.mult, ALU.mult,
                                                   accum_out=stats[:, 1:2])
                            v.tensor_scalar(stats[:, 2:3], stats[:, 0:1], 1.0 / 2048.0, None, ALU.mult)
                            v.tensor_scalar(stats[:, 3:4], stats[:, 1:2], 1.0 / 2048.0, None, ALU.mult).then_inc(cWV.sem, 1)
                            v.wait_ge(cWV.sem, cWV.n + 1)
                            v.tensor_tensor(stats[:, 4:5], stats[:, 2:3], stats[:, 2:3], ALU.mult).then_inc(cWV.sem, 1)
                            v.wait_ge(cWV.sem, cWV.n + 2)
                            v.tensor_tensor(stats[:, 3:4], stats[:, 3:4], stats[:, 4:5], ALU.subtract).then_inc(cWV.sem, 1)
                        cWV.n += 3
                        cPSW.n += 1

                        @block.scalar
                        def _(s):
                            s.wait_ge(cWV.sem, cWV.n)
                            s.activation(stats[:, 5:6], stats[:, 3:4], AF.Ln, bias=cpp[:, C_E5:C_E5 + 1]).then_inc(cSC.sem, 1)
                            s.wait_ge(cSC.sem, cSC.n + 1)
                            s.activation(stats[:, 5:6], stats[:, 5:6], AF.Exp, scale=-0.5).then_inc(cSC.sem, 1)
                        cSC.n += 2

                        @block.vector
                        def _(v):
                            v.wait_ge(cSC.sem, cSC.n)
                            v.wait_ge(cZD.sem, cZD.n)
                            v.tensor_scalar(zz[:], vv[:], stats[:, 2:3], stats[:, 5:6],
                                            ALU.subtract, ALU.mult).then_inc(cWV.sem, 1)
                        cWV.n += 1

                        @block.sync
                        def _(sync, b=b, tt=tt):
                            sync.wait_ge(cWV.sem, cWV.n)
                            sync.dma_start(out=d_zscr[b, tt * 128:(tt + 1) * 128, :], in_=zz[:]).then_inc(cZD.sem, 16)
                        cZD.n += 16

                        for qq in range(4):
                            c0, c1 = qq * 512, (qq + 1) * 512

                            @block.vector
                            def _(v, c0=c0, c1=c1):
                                v.tensor_scalar(qd[:, QU], zz[:, c0:c1], cpp[:, C_RS:C_RS + 1], 448.0,
                                                ALU.mult, ALU.min)
                                v.tensor_single_scalar(qd[:, QU], qd[:, QU], -448.0, ALU.max)
                                v.tensor_tensor(qd[:, QW], qd[:, QU], qd[:, QU], ALU.mult).then_inc(cQA.sem, 1)
                            cQA.n += 1

                            @block.scalar
                            def _(s):
                                s.wait_ge(cQA.sem, cQA.n)
                                s.activation(qd[:, QW], qd[:, QW], AF.Ln, bias=cpp[:, C_C40:C_C40 + 1]).then_inc(cQB.sem, 1)
                            cQB.n += 1

                            @block.vector
                            def _(v):
                                v.wait_ge(cQB.sem, cQB.n)
                                v.tensor_scalar(qd[:, QE], qd[:, QW], LN2INV2, -3.5, ALU.mult, ALU.add)
                                v.tensor_scalar(qd[:, QA32], qd[:, QE], C32, -C32, ALU.add, ALU.add)
                                v.tensor_scalar(qd[:, QA52], qd[:, QE], C52, -C52, ALU.add, ALU.add)
                                v.tensor_tensor(qd[:, QA52], qd[:, QA52], qd[:, QA32], ALU.subtract)
                                v.scalar_tensor_tensor(qd[:, QE], qd[:, QA52], cpp[:, C_TAU:C_TAU + 1],
                                                       qd[:, QA32], ALU.mult, ALU.add)
                                v.tensor_single_scalar(qd[:, QE], qd[:, QE], -9.0, ALU.max).then_inc(cQA.sem, 1)
                            cQA.n += 1

                            @block.scalar
                            def _(s):
                                s.wait_ge(cQA.sem, cQA.n)
                                s.activation(qd[:, QQ], qd[:, QE], AF.Exp,
                                             bias=cpp[:, C_N2LN2:C_N2LN2 + 1], scale=cpp[:, C_LN2:C_LN2 + 1])
                                s.activation(qd[:, QQI], qd[:, QE], AF.Exp,
                                             scale=cpp[:, C_NLN2:C_NLN2 + 1]).then_inc(cQB.sem, 1)
                            cQB.n += 1

                            @block.vector
                            def _(v, c0=c0, c1=c1):
                                v.wait_ge(cQB.sem, cQB.n)
                                if c0 == 0:
                                    v.wait_ge(cQT.sem, cQT.n)
                                v.tensor_tensor(qd[:, QQI], qd[:, QU], qd[:, QQI], ALU.mult)
                                v.tensor_scalar(qd[:, QA32], qd[:, QQI], C32, -C32, ALU.add, ALU.add)
                                v.tensor_scalar(qd[:, QA52], qd[:, QQI], C52, -C52, ALU.add, ALU.add)
                                v.tensor_tensor(qd[:, QA52], qd[:, QA52], qd[:, QA32], ALU.subtract)
                                v.scalar_tensor_tensor(qd[:, QE], qd[:, QA52], cpp[:, C_TAU:C_TAU + 1],
                                                       qd[:, QA32], ALU.mult, ALU.add)
                                v.tensor_tensor(xqb[:, c0:c1], qd[:, QE], qd[:, QQ], ALU.mult).then_inc(cWV.sem, 1)
                            cWV.n += 1

                            @block.sync
                            def _(s, j=j, hf=hf, b=b, qq=qq, cWVt=cWV.n, cQ8t=cQ8.n):
                                s.wait_ge(cWV.sem, cWVt)
                                s.wait_ge(cQ8.sem, cQ8t)
                                if qq == 0 and j == 0 and (hf == 1 or b > 0):
                                    s.wait_ge(cFM.sem, cFM.n)
                                dst = xqs[:, qq * 512:(qq + 1) * 512].rearrange("p (t c) -> p t c", c=128)
                                src = xqb[:, qq * 512:(qq + 1) * 512].rearrange("p (t c) -> p t c", c=128)
                                s.dma_start_transpose(dst, src).then_inc(cQT.sem, 16)
                            cQT.n += 16

                        @block.vector
                        def _(v, j=j):
                            v.wait_ge(cQT.sem, cQT.n)
                            src = xqs[:].rearrange("p (f c) -> p f c", c=128)
                            dst = xqT8[:].rearrange("p (f c) -> p f c", c=1024)[:, :, j * 128:(j + 1) * 128]
                            v.tensor_copy(dst, src).then_inc(cQ8.sem, 1)
                        cQ8.n += 1

                        if DBG and b == 0 and hf == 0:
                            @block.sync
                            def _(sync, j=j):
                                sync.wait_ge(cWV.sem, cWV.n)
                                sync.dma_start(out=d_xqdbg[j], in_=xqb[:]).then_inc(cO.sem, 16)
                            cO.n += 16

                    if DBG and b == 0 and hf == 0:
                        @block.sync
                        def _(sync):
                            sync.wait_ge(cQ8.sem, cQ8.n)
                            sync.dma_start(out=d_xq8dbg, in_=xqT8[:]).then_inc(cO.sem, 16)
                            sync.dma_start(out=d_cppdbg, in_=cpp[:]).then_inc(cO.sem, 16)
                        cO.n += 32

                    Q8_HALF = cQ8.n
                    for j in range(8 if PHASES >= 5 else 0):
                        tt = hf * 8 + j

                        @block.sync
                        def _(sync, b=b, tt=tt, j=j, PSW_T=cPSW.n, PSF_T=cPSF.n):
                            if j < 2:
                                sync.wait_ge(cPSW.sem, PSW_T)
                            else:
                                sync.wait_ge(cPSF.sem, PSF_T - 1)
                            sync.dma_start(out=x2[:, (j % 2) * D:(j % 2 + 1) * D],
                                           in_=d_zscr[b, tt * 128:(tt + 1) * 128, :]).then_inc(cZL.sem, 16)
                        cZL.n += 16

                        @block.tensor
                        def _(t, tt=tt, j=j, PSW_T=cPSW.n, PSF_T=cPSF.n):
                            t.wait_ge(cQ8.sem, Q8_HALF - 7 + j)
                            if j == 0:
                                t.wait_ge(cPSW.sem, PSW_T)
                            if j >= 2:
                                t.wait_ge(cPSF.sem, PSF_T - 1)
                            for ft in range(FT):
                                for n in range(4):
                                    ins = t.matmul(PS[:, (tt % 2) * 2048 + n * 512:(tt % 2) * 2048 + (n + 1) * 512],
                                                   xqT8[:, ft * 1024 + j * 128: ft * 1024 + j * 128 + 128],
                                                   w8[:, ft * D + n * 512: ft * D + (n + 1) * 512],
                                                   start=(ft == 0), stop=(ft == FT - 1))
                            ins.then_inc(cFM.sem, 1)
                        cFM.n += 1

                        @block.vector
                        def _(v, tt=tt, j=j):
                            v.wait_ge(cFM.sem, cFM.n)
                            v.wait_ge(cZL.sem, cZL.n)
                            v.wait_ge(cO.sem, cO.n)
                            v.wait_ge(cZD.sem, cZD.n)
                            pb = (tt % 2) * 2048
                            v.scalar_tensor_tensor(vv[:], PS[:, pb:pb + 2048], cpp[:, C_DEQ:C_DEQ + 1],
                                                   x2[:, (j % 2) * D:(j % 2 + 1) * D],
                                                   ALU.mult, ALU.add, accum_out=stats[:, 0:1]).then_inc(cPSF.sem, 1)
                            v.scalar_tensor_tensor(zz[:], vv[:], 1.0, vv[:], ALU.mult, ALU.mult,
                                                   accum_out=stats[:, 1:2])
                            v.tensor_scalar(stats[:, 2:3], stats[:, 0:1], 1.0 / 2048.0, None, ALU.mult)
                            v.tensor_scalar(stats[:, 3:4], stats[:, 1:2], 1.0 / 2048.0, None, ALU.mult).then_inc(cWV.sem, 1)
                            v.wait_ge(cWV.sem, cWV.n + 1)
                            v.tensor_tensor(stats[:, 4:5], stats[:, 2:3], stats[:, 2:3], ALU.mult).then_inc(cWV.sem, 1)
                            v.wait_ge(cWV.sem, cWV.n + 2)
                            v.tensor_tensor(stats[:, 3:4], stats[:, 3:4], stats[:, 4:5], ALU.subtract).then_inc(cWV.sem, 1)
                        cWV.n += 3
                        cPSF.n += 1

                        @block.scalar
                        def _(s):
                            s.wait_ge(cWV.sem, cWV.n)
                            s.activation(stats[:, 5:6], stats[:, 3:4], AF.Ln, bias=cpp[:, C_E5:C_E5 + 1]).then_inc(cSC.sem, 1)
                            s.wait_ge(cSC.sem, cSC.n + 1)
                            s.activation(stats[:, 5:6], stats[:, 5:6], AF.Exp, scale=-0.5).then_inc(cSC.sem, 1)
                        cSC.n += 2

                        @block.vector
                        def _(v):
                            v.wait_ge(cSC.sem, cSC.n)
                            v.tensor_scalar(ob[:], vv[:], stats[:, 2:3], stats[:, 5:6],
                                            ALU.subtract, ALU.mult).then_inc(cWV.sem, 1)
                        cWV.n += 1

                        @block.sync
                        def _(sync, b=b, tt=tt):
                            sync.wait_ge(cWV.sem, cWV.n)
                            sync.dma_start(out=d_out[b, tt * 128:(tt + 1) * 128, :], in_=ob[:]).then_inc(cO.sem, 16)
                        cO.n += 16

        @block.sync
        def _(sync):
            for ch in C.values():
                if ch.n > 0:
                    sync.wait_ge(ch.sem, ch.n)

    return nc


def _host_layouts(inputs):
    wq_b = np.ascontiguousarray(np.asarray(inputs["L0_attn_wq_b"], dtype=np.float32))
    wkv_a = np.ascontiguousarray(np.asarray(inputs["L0_attn_wkv_a"], dtype=np.float32))
    wkv_b = np.ascontiguousarray(np.asarray(inputs["L0_attn_wkv_b"], dtype=np.float32))
    perm = np.concatenate([np.arange(0, 64, 2), np.arange(1, 64, 2)])
    perm_r = np.concatenate([np.arange(1, 64, 2), np.arange(0, 64, 2)])
    wqbN = np.empty((64, NH * 64), np.float32)
    wqbX = np.empty((64, NH * 64), np.float32)
    wqbR = np.empty((64, NH * 64), np.float32)
    wkbN = np.empty((64, NH * 64), np.float32)
    wkbV = np.empty((64, NH * 64), np.float32)
    for h in range(NH):
        base = h * 128
        wqbN[:, h * 64:(h + 1) * 64] = wq_b[:, base:base + 64]
        wqbX[:, h * 64:(h + 1) * 64] = wq_b[:, base + 64 + perm]
        wqbR[:, h * 64:(h + 1) * 64] = wq_b[:, base + 64 + perm_r]
        wkbN[:, h * 64:(h + 1) * 64] = wkv_b[:, base:base + 64]
        wkbV[:, h * 64:(h + 1) * 64] = wkv_b[:, base + 64:base + 128]
    wkva = np.empty((D, 128), np.float32)
    wkva[:, 0:64] = wkv_a[:, 0:64]
    wkva[:, 64:128] = wkv_a[:, 64 + perm]
    w8 = np.asarray(inputs["L0_ffn_weight"])
    w8T = np.ascontiguousarray(w8.view(np.uint8).T).view(ml_dtypes.float8_e4m3)
    return wqbN, wqbX, wqbR, wkbN, wkbV, wkva, w8T


_NC_CACHE = {}


def kernel(**inputs):
    x = np.ascontiguousarray(np.asarray(inputs["L0_attn_hidden_states"], dtype=np.float32))
    wqbN, wqbX, wqbR, wkbN, wkbV, wkva, w8T = _host_layouts(inputs)
    cos4, sin4 = _yarn_tables()
    wqa = np.ascontiguousarray(np.asarray(inputs["L0_attn_wq_a"], dtype=np.float32))
    qnw = np.ascontiguousarray(np.asarray(inputs["L0_attn_q_a_norm_w"], dtype=np.float32).reshape(64, 1))
    knw = np.ascontiguousarray(np.asarray(inputs["L0_attn_kv_a_norm_w"], dtype=np.float32).reshape(64, 1))
    wo = np.ascontiguousarray(np.asarray(inputs["L0_attn_wo"], dtype=np.float32))
    sw = np.ascontiguousarray(np.repeat(np.asarray(inputs["L0_ffn_weight_scale"], dtype=np.float32).reshape(1, 1), 128, axis=0))
    s_in = np.asarray(inputs["L0_ffn_input_scale"]).reshape(1, 1)
    if s_in.dtype == np.uint16:
        s_in = s_in.view(ml_dtypes.bfloat16)
    assert s_in.dtype == ml_dtypes.bfloat16, s_in.dtype
    s_in = np.ascontiguousarray(np.repeat(s_in, 128, axis=0))

    if "nc" not in _NC_CACHE:
        _NC_CACHE["nc"] = build_nc()
    nc = _NC_CACHE["nc"]

    common = {
        "wqa": wqa, "wqbN": wqbN, "wqbX": wqbX, "wqbR": wqbR,
        "wkva": wkva, "wkbN": wkbN, "wkbV": wkbV,
        "qnw": qnw, "knw": knw, "wo": wo, "w8T": w8T,
        "sw": sw, "sin_": s_in, "cos4": cos4, "sin4": sin4,
    }
    in_maps = []
    for c in range(NCORES):
        m = dict(common)
        m["x"] = np.ascontiguousarray(x[c * NB:(c + 1) * NB])
        in_maps.append(m)

    res = run_bass_kernel_spmd(nc, in_maps, core_ids=list(range(NCORES)))
    out = np.concatenate([np.asarray(r["out"]) for r in res.results], axis=0)
    return out.astype(np.float32, copy=False)
